# version 08
# baseline (speedup 1.0000x reference)
import numpy as np
import ml_dtypes
from contextlib import ExitStack

import concourse.bass as bass
import concourse.mybir as mybir
from concourse.bass_utils import run_bass_kernel_spmd

F32 = mybir.dt.float32
BF16 = mybir.dt.bfloat16
I8 = mybir.dt.int8
AF = mybir.ActivationFunctionType
ALU = mybir.AluOpType

HID = 4096
T = 1024
TH = T + 3
NH = 128
HD = 64
DIN = 8192
G = 8
NST = 128
CS = 128
NCH = T // CS
N_OT = 145
XBC0 = 64
N_XBC = 80
NCORE = 8
EPS = 1e-5

DEBUG = False
CONSTS = (1e-5, -0.5, -1.0, 1.0, 1.0 / 1024.0)


def bv(ap2d, p, k, reps):
    return ap2d.unsqueeze(2).broadcast_to((p, k, reps))


def build_nc():
    nc = bass.Bass("TRN2", target_bir_lowering=False)

    xt = nc.dram_tensor("xt", [HID, TH], F32, kind="ExternalInput").ap()
    wp = nc.dram_tensor("wp", [N_OT, 128, 4096], F32, kind="ExternalInput").ap()
    wop = nc.dram_tensor("wop", [32, 128, 8192], F32, kind="ExternalInput").ap()
    tern = nc.dram_tensor("tern", [32, 128, 4096], I8, kind="ExternalInput").ap()
    scf = nc.dram_tensor("scf", [32, 128, 4096], F32, kind="ExternalInput").ap()
    cwi = nc.dram_tensor("cwi", [128, N_XBC * 4], F32, kind="ExternalInput").ap()
    cbi = nc.dram_tensor("cbi", [128, N_XBC], F32, kind="ExternalInput").ap()
    dtbi = nc.dram_tensor("dtbi", [128, 1], F32, kind="ExternalInput").ap()
    alogi = nc.dram_tensor("alogi", [128, 1], F32, kind="ExternalInput").ap()
    dexpi = nc.dram_tensor("dexpi", [128, 64], F32, kind="ExternalInput").ap()
    nwi = nc.dram_tensor("nwi", [128, 64], F32, kind="ExternalInput").ap()
    m2i = nc.dram_tensor("m2i", [8, 1], F32, kind="ExternalInput").ap()
    m2ci = nc.dram_tensor("m2ci", [8, 1], F32, kind="ExternalInput").ap()
    m2hi = nc.dram_tensor("m2hi", [128, 8], F32, kind="ExternalInput").ap()

    outT = nc.dram_tensor("outT", [HID, T], F32, kind="ExternalOutput").ap()

    x_cm = nc.dram_tensor("x_cm", [DIN, T], BF16).ap()
    b_cm = nc.dram_tensor("b_cm", [G * NST, T], BF16).ap()
    xn_d = nc.dram_tensor("xn_d", [128, 32 * TH], BF16).ap()
    xs_d = nc.dram_tensor("xs_d", [NCH, 128, DIN], BF16).ap()
    st_d = nc.dram_tensor("st_d", [NCH, 128, DIN], BF16).ap()
    yf_d = nc.dram_tensor("yf_d", [T, DIN], BF16).ap()
    yo_d = nc.dram_tensor("yo_d", [DIN, T], BF16).ap()
    cfl_d = nc.dram_tensor("cfl_d", [NCH, CS * NH], F32).ap()
    contrib = nc.dram_tensor("contrib", [128, DIN], BF16).ap()
    gath = nc.dram_tensor("gath", [NCORE * 128, DIN], BF16).ap()
    alc = nc.dram_tensor("alc", [1, 128], F32).ap()
    wob_d = nc.dram_tensor("wob_d", [32, 128, 8192], BF16).ap()
    alg = nc.dram_tensor("alg", [NCORE, 128], F32).ap()
    dbg = {}
    if DEBUG:
        for nm, shape, dty in [("dbg_xcm", [DIN, T], BF16), ("dbg_bcm", [G * NST, T], BF16),
                               ("dbg_yf", [T, DIN], BF16), ("dbg_yo", [DIN, T], BF16),
                               ("dbg_st", [NCH * 128, DIN], BF16), ("dbg_cum", [128, T], F32),
                               ("dbg_gath", [NCORE * 128, DIN], BF16), ("dbg_xn", [128, 32 * TH], BF16),
                               ("dbg_dt", [128, T], F32), ("dbg_yg", [DIN, T], BF16),
                               ("dbg_rs", [8 * 128, T], BF16)]:
            dbg[nm] = nc.dram_tensor(nm, shape, dty, kind="ExternalOutput").ap()

    st = ExitStack()

    def sb(stk, name, shape, dtype):
        return stk.enter_context(nc.sbuf_tensor(name, shape, dtype))

    ones1 = sb(st, "ones1", [1, 128], F32)
    onescol = sb(st, "onescol", [128, 1], F32)
    onescol_bf = sb(st, "onescol_bf", [128, 1], BF16)
    eye = sb(st, "eye", [128, 128], F32)
    umask = sb(st, "umask", [128, 128], F32)
    scr_a = sb(st, "scr_a", [128, 128], F32)
    scr_b = sb(st, "scr_b", [128, 128], F32)
    s_dtb = sb(st, "s_dtb", [128, 1], F32)
    s_acol = sb(st, "s_acol", [128, 1], F32)
    alsum = sb(st, "alsum", [1, 128], F32)
    lrow = sb(st, "lrow", [1, 128], F32)
    s_m2 = sb(st, "s_m2", [8, 1], F32)
    s_m2c = sb(st, "s_m2c", [8, 1], F32)
    s_m2h = sb(st, "s_m2h", [128, 8], F32)
    s_ag = sb(st, "s_ag", [8, 128], F32)
    a_hm = sb(st, "a_hm", [128, 8], F32)
    whm = sb(st, "whm", [128, 8], F32)
    wexp = [sb(st, f"wexp{i}", [128, 128], BF16) for i in range(2)]
    cst = sb(st, "cst", [128, 8], F32)

    st23 = ExitStack()
    cumT = sb(st23, "cumT", [128, T], F32)
    dtT_bf = sb(st23, "dtT_bf", [128, T], BF16)
    sd_f = sb(st23, "sd_f", [128, T], F32)
    dc_bf = sb(st23, "dc_bf", [128, T], BF16)
    al_bf = sb(st23, "al_bf", [128, T], BF16)
    bcm = sb(st23, "bcm", [128, G * T], BF16)
    ccm = sb(st23, "ccm", [128, G * T], BF16)

    pA = st.enter_context(nc.psum_tensor("pA", [128, 512], F32))
    pB = st.enter_context(nc.psum_tensor("pB", [128, 512], F32))
    pC = st.enter_context(nc.psum_tensor("pC", [128, 512], F32))
    pD = st.enter_context(nc.psum_tensor("pD", [128, 512], F32))
    pE = st.enter_context(nc.psum_tensor("pE", [128, 512], F32))
    pF = st.enter_context(nc.psum_tensor("pF", [128, 512], F32))
    pG = st.enter_context(nc.psum_tensor("pG", [128, 512], F32))
    pH = st.enter_context(nc.psum_tensor("pH", [128, 512], F32))

    class S:
        def __init__(self, name):
            self.h = st.enter_context(nc.semaphore(name))
            self.n = 0

        def inc(self, instr, k=1):
            instr.then_inc(self.h, k)
            self.n += k
            return self.n

    LW = S("LW")
    LX = S("LX")
    SP = S("SP")
    CC = S("CC")
    PEs = S("PEs")
    VEs = S("VEs")
    AEs = S("AEs")
    GPs = S("GPs")
    BAR = S("BAR")
    STO = S("STO")
    DBG = S("DBG")
    WCL = S("WCL")
    WCS = S("WCS")
    LWB = S("LWB")

    block = st.enter_context(nc.Block())
    barn = [0]

    def barrier():
        barn[0] += 5
        tgt = barn[0]
        for reg in (block.tensor, block.vector, block.scalar, block.gpsimd, block.sync):
            @reg
            def _(e):
                e.sem_inc(BAR.h, 1)
                e.wait_ge(BAR.h, tgt)
        BAR.n = tgt

    @block.gpsimd
    def _(g):
        g.iota(scr_a[:], [[0, 128]], channel_multiplier=1, allow_small_or_imprecise_dtypes=True)
        g.iota(scr_b[:], [[1, 128]], channel_multiplier=0, allow_small_or_imprecise_dtypes=True)
        g.memset(ones1[:], 1.0)
        g.memset(onescol[:], 1.0)
        g.memset(onescol_bf[:], 1.0)
        g.memset(alsum[:], 0.0)
        ins = g.memset(whm[:], 0.0)
        GPs.inc(ins)
        g.dma_start(out=s_dtb[:], in_=dtbi).then_inc(LW.h, 16)
        g.dma_start(out=s_acol[:], in_=alogi).then_inc(LW.h, 16)
        g.dma_start(out=s_m2[:], in_=m2i).then_inc(LW.h, 16)
        g.dma_start(out=s_m2c[:], in_=m2ci).then_inc(LW.h, 16)
        g.dma_start(out=s_m2h[:], in_=m2hi).then_inc(LW.h, 16)
        for ci, cv in enumerate(CONSTS):
            ins = g.memset(cst[:, ci:ci + 1], cv)
        GPs.inc(ins)
    LW.n += 80
    for ci, cv in enumerate(CONSTS):
        nc.const_aps.register(F32, cv, cst[:, ci:ci + 1])

    @block.vector
    def _(v):
        v.wait_ge(GPs.h, GPs.n)
        v.tensor_tensor(eye[:], scr_a[:], scr_b[:], ALU.is_equal)
        ins = v.tensor_tensor(umask[:], scr_a[:], scr_b[:], ALU.is_le)
        VEs.inc(ins)

    @block.scalar
    def _(s):
        s.wait_ge(LW.h, 80)
        s.wait_ge(GPs.h, GPs.n)
        s.activation(s_acol[:], s_acol[:], AF.Exp)
        ins = s.mul(s_acol[:], s_acol[:], -1.0)
        AEs.inc(ins)

    barrier()

    st12 = ExitStack()
    xn_cm = sb(st12, "xn_cm", [128, 32 * TH], BF16)
    st1 = ExitStack()
    xbuf = [sb(st1, f"xbuf{i}", [128, TH], F32) for i in range(2)]
    sqb = [sb(st1, f"sqb{i}", [128, TH], F32) for i in range(2)]
    mrow = sb(st1, "mrow", [1, TH], F32)
    vrow = sb(st1, "vrow", [1, TH], F32)
    mbc = sb(st1, "mbc", [128, TH], F32)
    rbc = sb(st1, "rbc", [128, TH], F32)
    tmp1 = sb(st1, "tmp1", [128, TH], F32)

    lx0, ae0, pe0, ve0, lw0 = LX.n, AEs.n, PEs.n, VEs.n, LW.n
    ld_tile1 = {}

    @block.sync
    def _(sy):
        for i in range(0, 32, 2):
            if i >= 2:
                sy.wait_ge(PEs.h, pe0 + i - 1)
                sy.wait_ge(AEs.h, ae0 + i - 1)
            ins = sy.dma_start(out=xbuf[i % 2][:], in_=xt[128 * i:128 * (i + 1), :])
            LX.inc(ins, 16)
            ld_tile1[i] = (LX, LX.n)

    @block.gpsimd
    def _(g):
        for i in range(1, 32, 2):
            if i >= 2:
                g.wait_ge(PEs.h, pe0 + i - 1)
                g.wait_ge(AEs.h, ae0 + i - 1)
            ins = g.dma_start(out=xbuf[i % 2][:], in_=xt[128 * i:128 * (i + 1), :])
            LW.inc(ins, 16)
            ld_tile1[i] = (LW, LW.n)

    @block.scalar
    def _(s):
        for i in range(32):
            sm, val = ld_tile1[i]
            s.wait_ge(sm.h, val)
            if i >= 2:
                s.wait_ge(PEs.h, pe0 + i - 1)
            ins = s.activation(sqb[i % 2][:], xbuf[i % 2][:], AF.Square)
            AEs.inc(ins)

    @block.tensor
    def _(t):
        for i in range(32):
            s0, s1 = (i == 0), (i == 31)
            sm, val = ld_tile1[i]
            t.wait_ge(sm.h, val)
            t.wait_ge(AEs.h, ae0 + i + 1)
            t.matmul(pA[0:1, :], onescol[:], xbuf[i % 2][:, 0:512], start=s0, stop=s1)
            t.matmul(pB[0:1, :], onescol[:], xbuf[i % 2][:, 512:1024], start=s0, stop=s1)
            t.matmul(pH[0:1, 0:3], onescol[:], xbuf[i % 2][:, 1024:1027], start=s0, stop=s1)
            t.matmul(pC[0:1, :], onescol[:], sqb[i % 2][:, 0:512], start=s0, stop=s1)
            t.matmul(pD[0:1, :], onescol[:], sqb[i % 2][:, 512:1024], start=s0, stop=s1)
            ins = t.matmul(pE[0:1, 0:3], onescol[:], sqb[i % 2][:, 1024:1027], start=s0, stop=s1)
            PEs.inc(ins)

    @block.scalar
    def _(s):
        s.wait_ge(PEs.h, pe0 + 32)
        s.copy(mrow[0:1, 0:512], pA[0:1, :])
        s.copy(mrow[0:1, 512:1024], pB[0:1, :])
        s.copy(mrow[0:1, 1024:1027], pH[0:1, 0:3])
        s.copy(vrow[0:1, 0:512], pC[0:1, :])
        s.copy(vrow[0:1, 512:1024], pD[0:1, :])
        ins = s.copy(vrow[0:1, 1024:1027], pE[0:1, 0:3])
        AEs.inc(ins)

    @block.vector
    def _(v):
        v.wait_ge(AEs.h, ae0 + 33)
        v.tensor_scalar(mrow[:], mrow[:], 1.0 / HID, None, ALU.mult)
        v.tensor_scalar(vrow[:], vrow[:], 1.0 / HID, None, ALU.mult)
        v.tensor_tensor(tmp1[0:1, :], mrow[:], mrow[:], ALU.mult)
        ins = v.tensor_tensor(vrow[:], vrow[:], tmp1[0:1, :], ALU.subtract)
        VEs.inc(ins)

    @block.scalar
    def _(s):
        s.wait_ge(VEs.h, ve0 + 1)
        s.activation(vrow[:], vrow[:], AF.Ln, bias=EPS)
        ins = s.activation(vrow[:], vrow[:], AF.Exp, scale=-0.5)
        AEs.inc(ins)

    @block.tensor
    def _(t):
        t.wait_ge(AEs.h, ae0 + 34)
        t.matmul(pA[:], ones1[:], mrow[0:1, 0:512], start=True, stop=True)
        t.matmul(pB[:], ones1[:], mrow[0:1, 512:1024], start=True, stop=True)
        t.matmul(pH[:, 0:3], ones1[:], mrow[0:1, 1024:1027], start=True, stop=True)
        t.matmul(pC[:], ones1[:], vrow[0:1, 0:512], start=True, stop=True)
        t.matmul(pD[:], ones1[:], vrow[0:1, 512:1024], start=True, stop=True)
        ins = t.matmul(pE[:, 0:3], ones1[:], vrow[0:1, 1024:1027], start=True, stop=True)
        PEs.inc(ins)

    @block.scalar
    def _(s):
        s.wait_ge(PEs.h, pe0 + 33)
        s.copy(mbc[:, 0:512], pA[:])
        s.copy(mbc[:, 512:1024], pB[:])
        s.copy(mbc[:, 1024:1027], pH[:, 0:3])
        s.copy(rbc[:, 0:512], pC[:])
        s.copy(rbc[:, 512:1024], pD[:])
        ins = s.copy(rbc[:, 1024:1027], pE[:, 0:3])
        AEs.inc(ins)

    lx1, ve1 = LX.n, VEs.n
    ld_tile2 = {}

    @block.sync
    def _(sy):
        for i in range(0, 32, 2):
            if i >= 2:
                sy.wait_ge(VEs.h, ve1 + i - 1)
            ins = sy.dma_start(out=xbuf[i % 2][:], in_=xt[128 * i:128 * (i + 1), :])
            LX.inc(ins, 16)
            ld_tile2[i] = (LX, LX.n)

    @block.gpsimd
    def _(g):
        for i in range(1, 32, 2):
            if i >= 2:
                g.wait_ge(VEs.h, ve1 + i - 1)
            ins = g.dma_start(out=xbuf[i % 2][:], in_=xt[128 * i:128 * (i + 1), :])
            LW.inc(ins, 16)
            ld_tile2[i] = (LW, LW.n)

    @block.vector
    def _(v):
        v.wait_ge(AEs.h, ae0 + 35)
        for i in range(32):
            sm, val = ld_tile2[i]
            v.wait_ge(sm.h, val)
            v.tensor_tensor(tmp1[:], xbuf[i % 2][:], mbc[:], ALU.subtract)
            ins = v.tensor_tensor(xn_cm[:, TH * i:TH * (i + 1)], tmp1[:], rbc[:], ALU.mult)
            VEs.inc(ins)

    @block.gpsimd
    def _(g):
        g.wait_ge(VEs.h, ve1 + 32)
        ins = g.dma_start(out=xn_d, in_=xn_cm[:])
        SP.inc(ins, 16)

    st1.close()
    barrier()

    st2 = ExitStack()
    wbuf = [sb(st2, f"wbuf{i}", [128, 4096], BF16) for i in range(3)]
    zx = [sb(st2, f"zx{i}", [128, TH], F32) for i in range(2)]
    cacc = [sb(st2, f"cacc{i}", [128, T], F32) for i in range(2)]
    xout = [sb(st2, f"xout{i}", [128, T], BF16) for i in range(2)]
    s_cw = sb(st2, "s_cw", [128, N_XBC * 4], F32)
    s_cb = sb(st2, "s_cb", [128, N_XBC], F32)
    dt_f = sb(st2, "dt_f", [128, T], F32)
    adt = sb(st2, "adt", [128, T], F32)
    cum_hm = sb(st2, "cum_hm", [128, T], F32)
    tdt = sb(st2, "tdt", [128, T], F32)
    z128 = sb(st2, "z128", [128, 128], F32)

    lw2, pe2, ae2, ve2, sp2 = LW.n, PEs.n, AEs.n, VEs.n, SP.n
    NT2 = N_XBC + 1

    act_seq = []
    for i in range(NT2):
        act_seq.append(("evac", i))
        act_seq.append(("c0", i))
        if i >= 1 and i - 1 < N_XBC:
            act_seq.append(("silu", i - 1))
    act_at = {tg: ae2 + k + 1 for k, tg in enumerate(act_seq)}

    @block.gpsimd
    def _(g):
        g.memset(z128[:], 0.0)
        g.dma_start(out=s_cw[:], in_=cwi).then_inc(LW.h, 16)
        g.dma_start(out=s_cb[:], in_=cbi).then_inc(LW.h, 16)
        for i in range(NT2):
            if i >= 3:
                g.wait_ge(PEs.h, pe2 + i - 2)
            ins = g.dma_start(out=wbuf[i % 3][:], in_=wp[XBC0 + i])
            LW.inc(ins, 16)
    LW.n += 32
    lw2b = lw2 + 32

    @block.sync
    def _(sy):
        for j in range(64):
            sy.wait_ge(AEs.h, act_at[("silu", j)])
            ins = sy.dma_start(out=x_cm[128 * j:128 * (j + 1), :], in_=xout[j % 2][:])
            SP.inc(ins, 16)
        sy.wait_ge(AEs.h, act_at[("silu", 71)])
        for gi in range(8):
            ins = sy.dma_start(out=b_cm[NST * gi:NST * (gi + 1), :], in_=bcm[:, T * gi:T * (gi + 1)])
            SP.inc(ins, 16)

    @block.tensor
    def _(t):
        psets = [(pA, pB), (pC, pD)]
        halos = [pE, pH]
        for i in range(NT2):
            t.wait_ge(LW.h, lw2b + 16 * (i + 1))
            if i >= 2:
                t.wait_ge(AEs.h, act_at[("evac", i - 2)])
            a, b = psets[i % 2]
            hp = halos[i % 2]
            w = wbuf[i % 3]
            if i < N_XBC:
                for k in range(32):
                    s0, s1 = (k == 0), (k == 31)
                    t.matmul(a[:], w[:, 128 * k:128 * (k + 1)], xn_cm[:, TH * k:TH * k + 512], start=s0, stop=s1)
                    t.matmul(b[:], w[:, 128 * k:128 * (k + 1)], xn_cm[:, TH * k + 512:TH * k + 1024], start=s0, stop=s1)
                    ins = t.matmul(hp[:, 0:3], w[:, 128 * k:128 * (k + 1)],
                                   xn_cm[:, TH * k + 1024:TH * k + 1027], start=s0, stop=s1)
            else:
                for k in range(32):
                    s0, s1 = (k == 0), (k == 31)
                    t.matmul(a[:], w[:, 128 * k:128 * (k + 1)], xn_cm[:, TH * k + 3:TH * k + 515], start=s0, stop=s1)
                    ins = t.matmul(b[:], w[:, 128 * k:128 * (k + 1)], xn_cm[:, TH * k + 515:TH * k + 1027], start=s0, stop=s1)
            PEs.inc(ins)

    @block.scalar
    def _(s):
        psets = [(pA, pB), (pC, pD)]
        halos = [pE, pH]
        for i in range(NT2):
            s.wait_ge(PEs.h, pe2 + i + 1)
            if i >= 2:
                s.wait_ge(VEs.h, ve2 + i - 1)
            a, b = psets[i % 2]
            hp = halos[i % 2]
            z = zx[i % 2]
            s.copy(z[:, 0:512], a[:])
            s.copy(z[:, 512:1024], b[:])
            if i < N_XBC:
                ins = s.copy(z[:, 1024:1027], hp[:, 0:3])
                AEs.inc(ins)
                ins = s.activation(cacc[i % 2][:], z[:, 0:T], AF.Copy, scale=s_cw[:, 4 * i:4 * i + 1])
                AEs.inc(ins)
            else:
                ins = s.copy(z[:, 1024:1027], z[:, 1024:1027])
                AEs.inc(ins)
                ins = s.activation(cacc[i % 2][:], z[:, 0:T], AF.Copy)
                AEs.inc(ins)
            if i >= 1 and i - 1 < N_XBC:
                j = i - 1
                s.wait_ge(VEs.h, ve2 + j + 1)
                if j < 64:
                    s.wait_ge(SP.h, sp2 + 16 * max(0, j - 1))
                    dst = xout[j % 2][:]
                elif j < 72:
                    dst = bcm[:, T * (j - 64):T * (j - 63)]
                else:
                    dst = ccm[:, T * (j - 72):T * (j - 71)]
                ins = s.activation(dst, cacc[j % 2][:], AF.Silu, bias=s_cb[:, j:j + 1])
                AEs.inc(ins)
    assert AEs.n == ae2 + len(act_seq), (AEs.n, ae2, len(act_seq))

    @block.vector
    def _(v):
        for i in range(N_XBC):
            v.wait_ge(AEs.h, act_at[("c0", i)])
            z = zx[i % 2]
            c = cacc[i % 2]
            v.scalar_tensor_tensor(c[:], z[:, 1:1 + T], s_cw[:, 4 * i + 1:4 * i + 2], c[:], ALU.mult, ALU.add)
            v.scalar_tensor_tensor(c[:], z[:, 2:2 + T], s_cw[:, 4 * i + 2:4 * i + 3], c[:], ALU.mult, ALU.add)
            ins = v.scalar_tensor_tensor(c[:], z[:, 3:3 + T], s_cw[:, 4 * i + 3:4 * i + 4], c[:], ALU.mult, ALU.add)
            VEs.inc(ins)
        v.wait_ge(AEs.h, act_at[("c0", N_XBC)])
        u = cacc[N_XBC % 2]
        v.tensor_scalar(tdt[:], u[:], s_dtb[:], None, ALU.add)
        ins = v.tensor_scalar(dt_f[:], tdt[:], 0.0, None, ALU.max)
        VEs.inc(ins)

    ve2b = ve2 + N_XBC + 1

    @block.scalar
    def _(s):
        s.wait_ge(VEs.h, ve2b)
        s.activation(tdt[:], tdt[:], AF.Abs)
        s.activation(tdt[:], tdt[:], AF.Exp, scale=-1.0)
        ins = s.activation(tdt[:], tdt[:], AF.Ln, bias=1.0)
        AEs.inc(ins)
    ae2b = AEs.n

    @block.vector
    def _(v):
        v.wait_ge(AEs.h, ae2b)
        v.tensor_tensor(dt_f[:], dt_f[:], tdt[:], ALU.add)
        ins = v.tensor_scalar(adt[:], dt_f[:], s_acol[:], None, ALU.mult)
        VEs.inc(ins)
        for c in range(NCH):
            ins = v.tensor_tensor_scan(cum_hm[:, CS * c:CS * (c + 1)], adt[:, CS * c:CS * (c + 1)],
                                       z128[:], 0.0, ALU.add, ALU.add)
            VEs.inc(ins)
    ve2c = VEs.n

    peX, aeX, veX = PEs.n, AEs.n, VEs.n
    for c in range(NCH):
        @block.tensor
        def _(t, c=c):
            t.wait_ge(VEs.h, ve2c)
            if c >= 1:
                t.wait_ge(AEs.h, aeX + 3 * c)
            t.transpose(pF[:, 0:128], cum_hm[:, CS * c:CS * (c + 1)], eye[:])
            ins = t.transpose(pG[:, 0:128], dt_f[:, CS * c:CS * (c + 1)], eye[:])
            PEs.inc(ins)
            ins = t.transpose(pE[0:1, 0:128], cum_hm[:, CS * c + 127:CS * c + 128], eye[:])
            PEs.inc(ins)

        @block.scalar
        def _(s, c=c):
            s.wait_ge(PEs.h, peX + 3 * c + 2)
            s.copy(cumT[:, CS * c:CS * (c + 1)], pF[:, 0:128])
            ins = s.copy(dtT_bf[:, CS * c:CS * (c + 1)], pG[:, 0:128])
            AEs.inc(ins)
            ins = s.copy(lrow[:], pE[0:1, 0:128])
            AEs.inc(ins)

        @block.tensor
        def _(t, c=c):
            t.wait_ge(AEs.h, aeX + 3 * c + 2)
            ins = t.matmul(pF[:, 0:128], ones1[:], lrow[:], start=True, stop=True)
            PEs.inc(ins)

        @block.vector
        def _(v, c=c):
            v.wait_ge(PEs.h, peX + 3 * (c + 1))
            v.tensor_tensor(scr_a[:], pF[:, 0:128], cumT[:, CS * c:CS * (c + 1)], ALU.subtract)
            ins = v.tensor_tensor(alsum[:], alsum[:], lrow[:], ALU.add)
            VEs.inc(ins)

        @block.scalar
        def _(s, c=c):
            s.wait_ge(VEs.h, veX + c + 1)
            s.activation(dc_bf[:, CS * c:CS * (c + 1)], scr_a[:], AF.Exp)
            s.activation(al_bf[:, CS * c:CS * (c + 1)], pF[:, 0:128], AF.Exp)
            ins = s.activation(sd_f[:, CS * c:CS * (c + 1)], cumT[:, CS * c:CS * (c + 1)], AF.Exp)
            AEs.inc(ins)

    @block.gpsimd
    def _(g):
        g.wait_ge(VEs.h, ve2c)
        for c in range(NCH):
            ins = g.dma_start(out=cfl_d[c].rearrange("(h i) -> h i", h=128), in_=cum_hm[:, CS * c:CS * (c + 1)])
            SP.inc(ins, 16)

    @block.scalar
    def _(s):
        s.wait_ge(VEs.h, veX + NCH)
        ins = s.activation(alsum[:], alsum[:], AF.Exp)
        AEs.inc(ins)
    aeX2 = AEs.n

    @block.gpsimd
    def _(g):
        g.wait_ge(AEs.h, aeX2)
        ins = g.dma_start(out=alc, in_=alsum[:])
        SP.inc(ins, 16)

    if DEBUG:
        @block.gpsimd
        def _(g):
            g.wait_ge(AEs.h, aeX2)
            g.dma_start(out=dbg["dbg_cum"], in_=cum_hm[:]).then_inc(DBG.h, 16)
            g.dma_start(out=dbg["dbg_dt"], in_=dt_f[:]).then_inc(DBG.h, 16)
        DBG.n += 32

    st2.close()
    st12.close()
    barrier()

    st3 = ExitStack()
    xsb = sb(st3, "xsb", [128, DIN], BF16)
    incb = sb(st3, "incb", [128, DIN], BF16)
    sreb = sb(st3, "sreb", [128, DIN], BF16)
    big1 = sb(st3, "big1", [128, DIN], BF16)
    st3a = ExitStack()
    xTb = sb(st3a, "xTb", [128, DIN], BF16)
    xs2b = sb(st3a, "xs2b", [128, DIN], BF16)
    btb = sb(st3a, "btb", [128, G * NST], BF16)

    lx3, pe3, ae3, ve3, sp3 = LX.n, PEs.n, AEs.n, VEs.n, SP.n

    @block.sync
    def _(sy):
        sy.wait_ge(SP.h, sp3)
        for c in range(NCH):
            if c >= 1:
                sy.wait_ge(VEs.h, ve3 + 1 + 4 * c)
                sy.wait_ge(PEs.h, pe3 + 16 * c)
            ins = sy.dma_start(out=xTb[:], in_=x_cm[:, CS * c:CS * (c + 1)], transpose=True)
            LX.inc(ins, 16)
            ins = sy.dma_start(out=btb[:], in_=b_cm[:, CS * c:CS * (c + 1)], transpose=True)
            LX.inc(ins, 16)

    @block.vector
    def _(v):
        ins = v.memset(incb[:], 0.0)
        VEs.inc(ins)
        for c in range(NCH):
            v.wait_ge(LX.h, lx3 + 32 * (c + 1))
            if c >= 1:
                v.wait_ge(SP.h, sp3 + 32 * c)
            xv = xsb[:].rearrange("p (h d) -> p h d", h=NH)
            ins = v.tensor_tensor(xv, xTb[:].rearrange("p (h d) -> p h d", h=NH),
                                  bv(dtT_bf[:, CS * c:CS * (c + 1)], 128, NH, HD), ALU.mult)
            VEs.inc(ins)
            ins = v.tensor_tensor(xs2b[:].rearrange("p (h d) -> p h d", h=NH), xv,
                                  bv(dc_bf[:, CS * c:CS * (c + 1)], 128, NH, HD), ALU.mult)
            VEs.inc(ins)
            v.wait_ge(AEs.h, ae3 + 16 * (c + 1))
            ins = v.tensor_tensor(xTb[:].rearrange("p (h d) -> p h d", h=NH),
                                  incb[:].rearrange("p (h d) -> p h d", h=NH),
                                  bv(al_bf[:, CS * c:CS * (c + 1)], 128, NH, HD), ALU.mult)
            VEs.inc(ins)
            ins = v.tensor_tensor(incb[:], xTb[:], big1[:], ALU.add)
            VEs.inc(ins)

    @block.tensor
    def _(t):
        for c in range(NCH):
            t.wait_ge(VEs.h, ve3 + 1 + 4 * c + 2)
            for hb in range(16):
                p = pA if hb % 2 == 0 else pB
                m = 16 * c + hb
                if m >= 2:
                    t.wait_ge(AEs.h, ae3 + m - 1)
                for hh in range(8):
                    h = 8 * hb + hh
                    ins = t.matmul(p[:, 64 * hh:64 * (hh + 1)], btb[:, NST * (h // 16):NST * (h // 16 + 1)],
                                   xs2b[:, HD * h:HD * (h + 1)], start=True, stop=True)
                PEs.inc(ins)

    @block.scalar
    def _(s):
        for c in range(NCH):
            for hb in range(16):
                p = pA if hb % 2 == 0 else pB
                s.wait_ge(PEs.h, pe3 + 16 * c + hb + 1)
                if hb == 0 and c >= 1:
                    s.wait_ge(VEs.h, ve3 + 1 + 4 * c)
                    s.wait_ge(SP.h, sp3 + 32 * c)
                ins = s.copy(big1[:, 512 * hb:512 * (hb + 1)], p[:])
                AEs.inc(ins)

    @block.gpsimd
    def _(g):
        for c in range(NCH):
            g.wait_ge(VEs.h, ve3 + 1 + 4 * c + 1)
            ins = g.dma_start(out=xs_d[c], in_=xsb[:])
            SP.inc(ins, 16)
            g.wait_ge(AEs.h, ae3 + 16 * (c + 1))
            ins = g.dma_start(out=st_d[c], in_=big1[:])
            SP.inc(ins, 16)
        g.wait_ge(VEs.h, ve3 + 1 + 4 * NCH)
        ins = g.dma_start(out=contrib, in_=incb[:])
        SP.inc(ins, 16)
        g.wait_ge(SP.h, SP.n)
        ins = g.collective_compute("AllGather", ALU.bypass, replica_groups=[list(range(NCORE))],
                                   ins=[contrib.opt()], outs=[gath.opt()])
        CC.inc(ins)
        ins = g.collective_compute("AllGather", ALU.bypass, replica_groups=[list(range(NCORE))],
                                   ins=[alc.opt()], outs=[alg.opt()])
        CC.inc(ins)

    st3a.close()
    barrier()

    st3b = ExitStack()
    Eb = sb(st3b, "Eb", [128, NH * CS], BF16)
    g2m = sb(st3b, "g2m", [128, G * CS], BF16)
    t1b = [sb(st3b, f"t1b{i}", [128, 512], BF16) for i in range(2)]
    t1e = [sb(st3b, f"t1e{i}", [128, 512], BF16) for i in range(2)]
    t32b = sb(st3b, "t32b", [128, 512], F32)
    t1f = sb(st3b, "t1f", [128, 512], F32)
    cflb = sb(st3b, "cflb", [1, 4096], F32)

    peM, aeM, veM, lxM = PEs.n, AEs.n, VEs.n, LX.n

    @block.sync
    def _(sy):
        sy.wait_ge(CC.h, 2)
        ins = sy.dma_start(out=s_ag[:], in_=alg)
        LX.inc(ins, 16)

    @block.vector
    def _(v):
        v.wait_ge(LX.h, lxM + 16)
        ins = v.tensor_scalar(s_ag[:], s_ag[:], s_m2[:], s_m2c[:], ALU.mult, ALU.add)
        VEs.inc(ins)

    @block.tensor
    def _(t):
        t.wait_ge(VEs.h, veM + 1)
        ins = t.transpose(pF[:, 0:8], s_ag[:], eye[0:8, 0:8])
        PEs.inc(ins)

    @block.scalar
    def _(s):
        s.wait_ge(PEs.h, peM + 1)
        ins = s.copy(a_hm[:], pF[:, 0:8])
        AEs.inc(ins)

    @block.vector
    def _(v):
        v.wait_ge(AEs.h, aeM + 1)
        v.memset(whm[:, 7:8], 1.0)
        for gg in range(6, -1, -1):
            v.tensor_tensor(whm[:, gg:gg + 1], whm[:, gg + 1:gg + 2], a_hm[:, gg + 1:gg + 2], ALU.mult)
        ins = v.tensor_tensor(whm[:], whm[:], s_m2h[:], ALU.mult)
        VEs.inc(ins)
        ins = v.memset(incb[:], 0.0)
        VEs.inc(ins)

    peM2, aeM2, veM2, lxM2 = PEs.n, AEs.n, VEs.n, LX.n
    for gg in range(NCORE):
        @block.tensor
        def _(t, gg=gg):
            t.wait_ge(VEs.h, veM + 2)
            t.wait_ge(AEs.h, aeM2 + 2 * gg)
            ins = t.transpose(pG[0:1, 0:128], whm[:, gg:gg + 1], eye[:])
            PEs.inc(ins)

        @block.scalar
        def _(s, gg=gg):
            s.wait_ge(PEs.h, peM2 + 2 * gg + 1)
            ins = s.copy(lrow[:], pG[0:1, 0:128])
            AEs.inc(ins)

        @block.tensor
        def _(t, gg=gg):
            t.wait_ge(AEs.h, aeM2 + 2 * gg + 1)
            ins = t.matmul(pF[:, 0:128], ones1[:], lrow[:], start=True, stop=True)
            PEs.inc(ins)

        @block.scalar
        def _(s, gg=gg):
            s.wait_ge(PEs.h, peM2 + 2 * gg + 2)
            if gg >= 2:
                s.wait_ge(VEs.h, veM2 + 2 * (gg - 1))
            ins = s.copy(wexp[gg % 2][:], pF[:, 0:128])
            AEs.inc(ins)

        @block.sync
        def _(sy, gg=gg):
            if gg >= 2:
                sy.wait_ge(VEs.h, veM2 + 2 * (gg - 1))
            ins = sy.dma_start(out=(sreb if gg % 2 == 0 else big1)[:], in_=gath[128 * gg:128 * (gg + 1), :])
            LX.inc(ins, 16)

        @block.vector
        def _(v, gg=gg):
            v.wait_ge(AEs.h, aeM2 + 2 * gg + 2)
            v.wait_ge(LX.h, lxM2 + 16 * (gg + 1))
            src = (sreb if gg % 2 == 0 else big1)
            ins = v.tensor_tensor(Eb[:, 0:DIN].rearrange("p (h d) -> p h d", h=NH),
                                  src[:].rearrange("p (h d) -> p h d", h=NH),
                                  bv(wexp[gg % 2][:], 128, NH, HD), ALU.mult)
            VEs.inc(ins)
            ins = v.tensor_tensor(incb[:], incb[:], Eb[:, 0:DIN], ALU.add)
            VEs.inc(ins)

    barrier()

    lx4, pe4, ae4, ve4, sp4 = LX.n, PEs.n, AEs.n, VEs.n, SP.n

    dve_seq = []
    for c in range(NCH):
        if c >= 1:
            dve_seq.append(("incu", c))
        for g in range(8):
            dve_seq.append(("g2m", c, g))
        for s_ in range(32):
            dve_seq.append(("min", c, s_))
            if s_ >= 1:
                dve_seq.append(("emul", c, s_ - 1))
        dve_seq.append(("emul", c, 31))
        for hb in range(16):
            dve_seq.append(("mrg", c, hb))
    dve_at = {tg: ve4 + k + 1 for k, tg in enumerate(dve_seq)}

    pe_seq = []
    for c in range(NCH):
        for g in range(8):
            pe_seq.append(("g2", c, g))
        for s_ in range(32):
            pe_seq.append(("bc", c, s_))
        for hb in range(16):
            pe_seq.append(("yb", c, hb))
    pe_at = {tg: pe4 + k + 1 for k, tg in enumerate(pe_seq)}

    ae_at = {}
    k = 0
    for c in range(NCH):
        for s_ in range(32):
            k += 1
            ae_at[("exp", c, s_)] = ae4 + k

    lx_at = {}
    cnt = lx4
    for c in range(NCH):
        cnt += 16
        lx_at[("xs", c)] = cnt
        cnt += 16
        lx_at[("cf", c, 0)] = cnt
        if c >= 1:
            cnt += 16
            lx_at[("sre", c)] = cnt
        for q in range(1, 4):
            cnt += 16
            lx_at[("cf", c, q)] = cnt

    @block.sync
    def _(sy):
        sy.wait_ge(SP.h, sp4)
        for c in range(NCH):
            if c >= 1:
                sy.wait_ge(PEs.h, pe_at[("yb", c - 1, 15)])
            ins = sy.dma_start(out=xsb[:], in_=xs_d[c])
            LX.inc(ins, 16)
            if c >= 1:
                sy.wait_ge(PEs.h, pe_at[("bc", c - 1, 31)])
            ins = sy.dma_start(out=cflb[0:1, 0:4096], in_=cfl_d[c, 0:4096].rearrange("(a b) -> a b", a=1))
            LX.inc(ins, 16)
            if c >= 1:
                ins = sy.dma_start(out=sreb[:], in_=st_d[c - 1])
                LX.inc(ins, 16)
            for q in range(1, 4):
                sy.wait_ge(PEs.h, pe_at[("bc", c, 8 * q - 1)])
                ins = sy.dma_start(out=cflb[0:1, 0:4096],
                                   in_=cfl_d[c, 4096 * q:4096 * (q + 1)].rearrange("(a b) -> a b", a=1))
                LX.inc(ins, 16)

    @block.vector
    def _(v):
        for c in range(NCH):
            if c >= 1:
                v.wait_ge(LX.h, lx_at[("sre", c)])
                v.wait_ge(PEs.h, pe_at[("yb", c - 1, 15)])
                v.tensor_tensor(Eb[:, 0:DIN].rearrange("p (h d) -> p h d", h=NH),
                                incb[:].rearrange("p (h d) -> p h d", h=NH),
                                bv(al_bf[:, CS * (c - 1):CS * c], 128, NH, HD), ALU.mult)
                ins = v.tensor_tensor(incb[:], Eb[:, 0:DIN], sreb[:], ALU.add)
                VEs.inc(ins)
            for g in range(8):
                v.wait_ge(PEs.h, pe_at[("g2", c, g)])
                ins = v.tensor_tensor(g2m[:, CS * g:CS * (g + 1)], (pF if g % 2 == 0 else pG)[:, 0:128],
                                      umask[:], ALU.mult)
                VEs.inc(ins)
            for s_ in range(32):
                v.wait_ge(PEs.h, pe_at[("bc", c, s_)])
                p = pF if s_ % 2 == 0 else pG
                v.tensor_tensor(t32b[:].rearrange("p (hh i) -> p hh i", hh=4),
                                p[:].rearrange("p (hh i) -> p hh i", hh=4),
                                bv(cumT[:, CS * c + 4 * s_:CS * c + 4 * (s_ + 1)], 128, 4, 128), ALU.subtract)
                if s_ >= 2:
                    v.wait_ge(AEs.h, ae_at[("exp", c, s_ - 2)])
                elif c >= 1:
                    v.wait_ge(AEs.h, ae_at[("exp", c - 1, 30 + s_)])
                ins = v.tensor_scalar(t1b[s_ % 2][:], t32b[:], 0.0, None, ALU.min)
                VEs.inc(ins)
                if s_ >= 1:
                    sj = s_ - 1
                    v.wait_ge(AEs.h, ae_at[("exp", c, sj)])
                    gg = sj // 4
                    ins = v.tensor_tensor(Eb[:, 512 * sj:512 * (sj + 1)].rearrange("p (hh i) -> p hh i", hh=4),
                                          t1e[sj % 2][:].rearrange("p (hh i) -> p hh i", hh=4),
                                          g2m[:, CS * gg:CS * (gg + 1)].unsqueeze(1).broadcast_to((128, 4, 128)),
                                          ALU.mult)
                    VEs.inc(ins)
            sj = 31
            v.wait_ge(AEs.h, ae_at[("exp", c, sj)])
            gg = sj // 4
            ins = v.tensor_tensor(Eb[:, 512 * sj:512 * (sj + 1)].rearrange("p (hh i) -> p hh i", hh=4),
                                  t1e[sj % 2][:].rearrange("p (hh i) -> p hh i", hh=4),
                                  g2m[:, CS * gg:CS * (gg + 1)].unsqueeze(1).broadcast_to((128, 4, 128)),
                                  ALU.mult)
            VEs.inc(ins)
            for hb in range(16):
                v.wait_ge(PEs.h, pe_at[("yb", c, hb)])
                pyd = pA if hb % 2 == 0 else pC
                pyo = pB if hb % 2 == 0 else pD
                v.tensor_tensor(t1f[:].rearrange("p (hh d) -> p hh d", hh=8),
                                pyo[:].rearrange("p (hh d) -> p hh d", hh=8),
                                bv(sd_f[:, CS * c + 8 * hb:CS * c + 8 * (hb + 1)], 128, 8, HD), ALU.mult)
                if c >= 1 and hb == 0:
                    v.wait_ge(SP.h, sp4 + 16 * c)
                ins = v.tensor_tensor(big1[:, 512 * hb:512 * (hb + 1)], t1f[:], pyd[:], ALU.add)
                VEs.inc(ins)
    assert VEs.n == ve4 + len(dve_seq), (VEs.n, ve4, len(dve_seq))

    @block.tensor
    def _(t):
        for c in range(NCH):
            for g in range(8):
                p = pF if g % 2 == 0 else pG
                if (c, g) not in ((0, 0), (0, 1)):
                    prev = ("g2m", c, g - 2) if g >= 2 else ("min", c - 1, 30 + g)
                    t.wait_ge(VEs.h, dve_at[prev])
                ins = t.matmul(p[:, 0:128], bcm[:, T * g + CS * c:T * g + CS * (c + 1)],
                               ccm[:, T * g + CS * c:T * g + CS * (c + 1)], start=True, stop=True)
                PEs.inc(ins)
            for s_ in range(32):
                p = pF if s_ % 2 == 0 else pG
                prev = ("min", c, s_ - 2) if s_ >= 2 else ("g2m", c, 6 + s_)
                t.wait_ge(VEs.h, dve_at[prev])
                t.wait_ge(LX.h, lx_at[("cf", c, s_ // 8)])
                off = 512 * (s_ % 8)
                ins = t.matmul(p[:, 0:512], ones1[:], cflb[0:1, off:off + 512], start=True, stop=True)
                PEs.inc(ins)
            for hb in range(16):
                pyd = pA if hb % 2 == 0 else pC
                pyo = pB if hb % 2 == 0 else pD
                t.wait_ge(VEs.h, dve_at[("emul", c, 2 * hb + 1)])
                if c >= 1 or hb >= 2:
                    prev = ("mrg", c, hb - 2) if hb >= 2 else ("mrg", c - 1, 14 + hb)
                    t.wait_ge(VEs.h, dve_at[prev])
                if hb == 0 and c >= 1:
                    t.wait_ge(VEs.h, dve_at[("incu", c)])
                t.wait_ge(LX.h, lx_at[("xs", c)])
                for hh in range(8):
                    h = 8 * hb + hh
                    t.matmul(pyd[:, 64 * hh:64 * (hh + 1)], Eb[:, CS * h:CS * (h + 1)],
                             xsb[:, HD * h:HD * (h + 1)], start=True, stop=True)
                    ins = t.matmul(pyo[:, 64 * hh:64 * (hh + 1)],
                                   ccm[:, T * (h // 16) + CS * c:T * (h // 16) + CS * (c + 1)],
                                   incb[:, HD * h:HD * (h + 1)], start=True, stop=True)
                PEs.inc(ins)
    assert PEs.n == pe4 + len(pe_seq)

    @block.scalar
    def _(s):
        for c in range(NCH):
            for sl in range(32):
                s.wait_ge(VEs.h, dve_at[("min", c, sl)])
                if sl >= 2:
                    s.wait_ge(VEs.h, dve_at[("emul", c, sl - 2)])
                elif c >= 1:
                    s.wait_ge(VEs.h, dve_at[("emul", c - 1, 30 + sl)])
                ins = s.activation(t1e[sl % 2][:], t1b[sl % 2][:], AF.Exp)
                AEs.inc(ins)
    assert AEs.n == ae4 + 32 * NCH

    @block.gpsimd
    def _(g):
        for c in range(NCH):
            g.wait_ge(VEs.h, dve_at[("mrg", c, 15)])
            ins = g.dma_start(out=yf_d[CS * c:CS * (c + 1), :], in_=big1[:])
            SP.inc(ins, 16)

    if DEBUG:
        @block.gpsimd
        def _(g):
            g.wait_ge(SP.h, SP.n)
            g.dma_start(out=dbg["dbg_xcm"], in_=x_cm).then_inc(DBG.h, 16)
            g.dma_start(out=dbg["dbg_bcm"], in_=b_cm).then_inc(DBG.h, 16)
            g.dma_start(out=dbg["dbg_yf"], in_=yf_d).then_inc(DBG.h, 16)
            g.dma_start(out=dbg["dbg_st"], in_=st_d.rearrange("c p d -> (c p) d")).then_inc(DBG.h, 16)
            g.dma_start(out=dbg["dbg_gath"], in_=gath).then_inc(DBG.h, 16)
            g.dma_start(out=dbg["dbg_xn"], in_=xn_d).then_inc(DBG.h, 16)
        DBG.n += 96

    st3b.close()
    st3.close()
    st23.close()
    barrier()

    st4 = ExitStack()
    pass_st4 = None
    st4a = ExitStack()
    xn2 = sb(st4a, "xn2", [128, 32 * TH], BF16)
    wb4 = [sb(st4a, f"wb4{i}", [128, 4096], BF16) for i in range(3)]
    ytl = [sb(st4a, f"ytl{i}", [128, T], BF16) for i in range(3)]
    xre = [sb(st4a, f"xre{i}", [128, T], BF16) for i in range(3)]
    sgb = [sb(st4a, f"sgb{i}", [128, T], BF16) for i in range(2)]
    y2b = [sb(st4a, f"y2b{i}", [128, T], BF16) for i in range(2)]
    ysq = [sb(st4a, f"ysq{i}", [128, T], BF16) for i in range(2)]
    ynm = [sb(st4a, f"ynm{i}", [128, T], BF16) for i in range(2)]
    yob = [sb(st4a, f"yob{i}", [128, T], BF16) for i in range(2)]
    yGg = sb(st4a, "yGg", [128, 8 * T], BF16)
    rsb = sb(st4a, "rsb", [128, T], BF16)
    vr2 = sb(st4a, "vr2", [1, T], F32)
    s_dex = sb(st4a, "s_dex", [128, 64], BF16)
    s_nw = sb(st4a, "s_nw", [128, 64], F32)

    lx5, pe5, ae5, ve5, sp5, lw5 = LX.n, PEs.n, AEs.n, VEs.n, SP.n, LW.n

    pe_seq4 = []
    for gt in range(64):
        pe_seq4.append(("gmm", gt))
        pe_seq4.append(("vmm", gt))
        if gt % 8 == 7:
            pe_seq4.append(("rbc", gt // 8))
    pe_at4 = {tg: pe5 + k + 1 for k, tg in enumerate(pe_seq4)}

    ae_seq4 = []
    for gt in range(64):
        ae_seq4.append(("silu", gt))
        if gt % 8 == 7:
            gr = gt // 8
            ae_seq4.append(("vev", gr))
            ae_seq4.append(("rbce", gr))
            for q in range(8):
                ae_seq4.append(("nw", 8 * gr + q))
    ae_at4 = {tg: ae5 + k + 1 for k, tg in enumerate(ae_seq4)}

    ve_seq4 = []
    for gt in range(64):
        ve_seq4.append(("y2", gt))
        ve_seq4.append(("yG", gt))
        ve_seq4.append(("sq", gt))
        if gt % 8 == 7:
            for q in range(8):
                ve_seq4.append(("ynm", 8 * (gt // 8) + q))
    ve_at4 = {tg: ve5 + k + 1 for k, tg in enumerate(ve_seq4)}

    @block.sync
    def _(sy):
        sy.wait_ge(SP.h, sp5)
        ins = sy.dma_start(out=xn2[:], in_=xn_d)
        LX.inc(ins, 16)
        ins = sy.dma_start(out=s_nw[:], in_=nwi)
        LX.inc(ins, 16)
        for gt in range(64):
            if gt >= 3:
                sy.wait_ge(VEs.h, ve_at4[("y2", gt - 3)])
            ins = sy.dma_start(out=ytl[gt % 3][:], in_=yf_d[:, 128 * gt:128 * (gt + 1)], transpose=True)
            LX.inc(ins, 16)
            ins = sy.dma_start(out=xre[gt % 3][:], in_=x_cm[128 * gt:128 * (gt + 1), :])
            LX.inc(ins, 16)

    @block.gpsimd
    def _(g):
        g.wait_ge(SP.h, sp5)
        ins = g.dma_start(out=s_dex[:], in_=dexpi)
        LW.inc(ins, 16)
        for gt in range(64):
            if gt >= 3:
                g.wait_ge(PEs.h, pe_at4[("gmm", gt - 3)])
            ins = g.dma_start(out=wb4[gt % 3][:], in_=wp[gt])
            LW.inc(ins, 16)

    @block.tensor
    def _(t):
        for gt in range(64):
            gr = gt // 8
            t.wait_ge(LW.h, lw5 + 16 * (gt + 2))
            t.wait_ge(LX.h, lx5 + 32)
            if gt >= 2:
                t.wait_ge(AEs.h, ae_at4[("silu", gt - 2)])
            a = pA if gt % 2 == 0 else pC
            b = pB if gt % 2 == 0 else pD
            w = wb4[gt % 3]
            for k in range(32):
                s0, s1 = (k == 0), (k == 31)
                t.matmul(a[:], w[:, 128 * k:128 * (k + 1)], xn2[:, TH * k + 3:TH * k + 515], start=s0, stop=s1)
                ins = t.matmul(b[:], w[:, 128 * k:128 * (k + 1)], xn2[:, TH * k + 515:TH * k + 1027], start=s0, stop=s1)
            PEs.inc(ins)
            t.wait_ge(VEs.h, ve_at4[("sq", gt)])
            if gt % 8 == 0 and gr >= 1:
                t.wait_ge(AEs.h, ae_at4[("rbce", gr - 1)])
            t.matmul(pF[0:1, :], onescol_bf[:], ysq[gt % 2][:, 0:512], start=(gt % 8 == 0), stop=(gt % 8 == 7))
            ins = t.matmul(pG[0:1, :], onescol_bf[:], ysq[gt % 2][:, 512:1024], start=(gt % 8 == 0), stop=(gt % 8 == 7))
            PEs.inc(ins)
            if gt % 8 == 7:
                t.wait_ge(AEs.h, ae_at4[("vev", gr)])
                t.matmul(pF[:], ones1[:], vr2[0:1, 0:512], start=True, stop=True)
                ins = t.matmul(pG[:], ones1[:], vr2[0:1, 512:1024], start=True, stop=True)
                PEs.inc(ins)

    @block.scalar
    def _(s):
        for gt in range(64):
            gr = gt // 8
            s.wait_ge(PEs.h, pe_at4[("gmm", gt)])
            if gt >= 2:
                s.wait_ge(VEs.h, ve_at4[("yG", gt - 2)])
            a = pA if gt % 2 == 0 else pC
            b = pB if gt % 2 == 0 else pD
            s.activation(sgb[gt % 2][:, 0:512], a[:], AF.Silu)
            ins = s.activation(sgb[gt % 2][:, 512:1024], b[:], AF.Silu)
            AEs.inc(ins)
            if gt % 8 == 7:
                s.wait_ge(PEs.h, pe_at4[("vmm", gt)])
                s.copy(vr2[0:1, 0:512], pF[0:1, :])
                s.copy(vr2[0:1, 512:1024], pG[0:1, :])
                s.activation(vr2[:], vr2[:], AF.Ln, scale=1.0 / 1024.0, bias=EPS)
                ins = s.activation(vr2[:], vr2[:], AF.Exp, scale=-0.5)
                AEs.inc(ins)
                s.wait_ge(PEs.h, pe_at4[("rbc", gr)])
                s.copy(rsb[:, 0:512], pF[:])
                ins = s.copy(rsb[:, 512:1024], pG[:])
                AEs.inc(ins)
                for q in range(8):
                    gq = 8 * gr + q
                    s.wait_ge(VEs.h, ve_at4[("ynm", gq)])
                    ins = s.copy(lrow[:], lrow[:])
                    AEs.inc(ins)
                    ins = s.dma_start(out=yo_d[128 * gq:128 * (gq + 1), :], in_=yob[gq % 2][:])
                    SP.inc(ins, 16)
                if DEBUG:
                    for q in range(8):
                        gq = 8 * gr + q
                        ins = s.dma_start(out=dbg["dbg_yg"][128 * gq:128 * (gq + 1), :], in_=yGg[:, T * q:T * (q + 1)])
                        DBG.inc(ins, 16)
                    ins = s.dma_start(out=dbg["dbg_rs"][128 * gr:128 * (gr + 1), :], in_=rsb[:])
                    DBG.inc(ins, 16)
    assert AEs.n == ae5 + len(ae_seq4)

    @block.vector
    def _(v):
        for gt in range(64):
            gr = gt // 8
            v.wait_ge(LX.h, lx5 + 32 + 32 * (gt + 1))
            v.wait_ge(LW.h, lw5 + 16)
            ins = v.scalar_tensor_tensor(y2b[gt % 2][:], xre[gt % 3][:], s_dex[:, gt:gt + 1],
                                         ytl[gt % 3][:], ALU.mult, ALU.add)
            VEs.inc(ins)
            v.wait_ge(AEs.h, ae_at4[("silu", gt)])
            if gt >= 16:
                v.wait_ge(AEs.h, ae_at4[("nw", gt - 16)])
            ins = v.tensor_tensor(yGg[:, T * (gt % 8):T * (gt % 8 + 1)], y2b[gt % 2][:], sgb[gt % 2][:], ALU.mult)
            VEs.inc(ins)
            if gt >= 2:
                v.wait_ge(PEs.h, pe_at4[("vmm", gt - 2)])
            ins = v.tensor_tensor(ysq[gt % 2][:], yGg[:, T * (gt % 8):T * (gt % 8 + 1)],
                                  yGg[:, T * (gt % 8):T * (gt % 8 + 1)], ALU.mult)
            VEs.inc(ins)
            if gt % 8 == 7:
                v.wait_ge(AEs.h, ae_at4[("rbce", gr)])
                for q in range(8):
                    gq = 8 * gr + q
                    if gq >= 2:
                        v.wait_ge(SP.h, sp5 + 16 * (gq - 1))
                    ins = v.scalar_tensor_tensor(yob[gq % 2][:], yGg[:, T * q:T * (q + 1)],
                                                 s_nw[:, gq:gq + 1], rsb[:], ALU.mult, ALU.mult)
                    VEs.inc(ins)
    assert VEs.n == ve5 + len(ve_seq4)

    st4a.close()
    barrier()

    st4b = ExitStack()
    yh = sb(st4b, "yh", [128, 64 * 512], BF16)
    xnh = sb(st4b, "xnh", [128, 32 * 512], BF16)
    wob = [sb(st4b, f"wob{i}", [128, 8192], BF16) for i in range(2)]
    tb = [sb(st4b, f"tb{i}", [128, 4096], BF16) for i in range(2)]
    scb = [sb(st4b, f"scb{i}", [128, 4096], BF16) for i in range(2)]
    fb = [sb(st4b, f"fb{i}", [128, 4096], BF16) for i in range(2)]
    xrb = [sb(st4b, f"xrb{i}", [128, 512], F32) for i in range(2)]
    oat = [sb(st4b, f"oat{i}", [128, 512], F32) for i in range(4)]

    lx6, pe6, ae6, ve6, lw6, sp6, lwb6 = LX.n, PEs.n, AEs.n, VEs.n, LW.n, SP.n, LWB.n

    lx_xr = {}
    cnt = lx6
    lxh = []
    for th in range(2):
        cnt += 96 * 16
        lxh.append(cnt)
        for ot in range(32):
            cnt += 16
            lx_xr[(th, ot)] = cnt
    lxh = [lxh[0], lxh[1]]

    @block.sync
    def _(sy):
        sy.wait_ge(SP.h, sp6)
        for th in range(2):
            if th == 1:
                sy.wait_ge(PEs.h, pe6 + 32)
            for k in range(64):
                ins = sy.dma_start(out=yh[:, 512 * k:512 * (k + 1)],
                                   in_=yo_d[128 * k:128 * (k + 1), 512 * th:512 * (th + 1)])
                LX.inc(ins, 16)
            for k in range(32):
                ins = sy.dma_start(out=xnh[:, 512 * k:512 * (k + 1)],
                                   in_=xn_d[0:128, TH * k + 3 + 512 * th:TH * k + 3 + 512 * (th + 1)])
                LX.inc(ins, 16)
            for ot in range(32):
                i = 32 * th + ot
                if i >= 2:
                    sy.wait_ge(VEs.h, ve6 + 2 * (i - 1))
                ins = sy.dma_start(out=xrb[i % 2][:], in_=xt[128 * ot:128 * (ot + 1), 3 + 512 * th:3 + 512 * (th + 1)])
                LX.inc(ins, 16)

    @block.gpsimd
    def _(g):
        g.wait_ge(SP.h, sp6)
        for th in range(2):
            for ot in range(32):
                i = 32 * th + ot
                if i >= 2:
                    g.wait_ge(PEs.h, pe6 + i - 1)
                ins = g.dma_start(out=wob[i % 2][:], in_=wop[ot])
                LWB.inc(ins, 16)
                if i >= 2:
                    g.wait_ge(VEs.h, ve6 + 2 * (i - 2) + 1)
                ins = g.dma_start(out=tb[i % 2][:], in_=tern[ot])
                LW.inc(ins, 16)
                ins = g.dma_start(out=scb[i % 2][:], in_=scf[ot])
                LW.inc(ins, 16)

    @block.vector
    def _(v):
        for th in range(2):
            for ot in range(32):
                i = 32 * th + ot
                v.wait_ge(LW.h, lw6 + 32 * (i + 1))
                if i >= 2:
                    v.wait_ge(PEs.h, pe6 + i - 1)
                ins = v.tensor_tensor(fb[i % 2][:], tb[i % 2][:], scb[i % 2][:], ALU.mult)
                VEs.inc(ins)
                v.wait_ge(PEs.h, pe6 + i + 1)
                v.wait_ge(LX.h, lx_xr[(th, ot)])
                if i >= 4:
                    v.wait_ge(STO.h, 16 * (i - 3))
                p = pA if i % 2 == 0 else pB
                ins = v.tensor_tensor(oat[i % 4][:], p[:], xrb[i % 2][:], ALU.add)
                VEs.inc(ins)

    @block.tensor
    def _(t):
        for th in range(2):
            t.wait_ge(LX.h, lxh[th])
            for ot in range(32):
                i = 32 * th + ot
                t.wait_ge(LWB.h, lwb6 + 16 * (i + 1))
                t.wait_ge(VEs.h, ve6 + 2 * i + 1)
                if i >= 2:
                    t.wait_ge(VEs.h, ve6 + 2 * (i - 1))
                p = pA if i % 2 == 0 else pB
                w = wob[i % 2]
                f = fb[i % 2]
                for k in range(64):
                    t.matmul(p[:], w[:, 128 * k:128 * (k + 1)], yh[:, 512 * k:512 * (k + 1)],
                             start=(k == 0), stop=False)
                for k2 in range(32):
                    ins = t.matmul(p[:], f[:, 128 * k2:128 * (k2 + 1)], xnh[:, 512 * k2:512 * (k2 + 1)],
                                   start=False, stop=(k2 == 31))
                PEs.inc(ins)

    @block.scalar
    def _(s):
        for th in range(2):
            for ot in range(32):
                i = 32 * th + ot
                s.wait_ge(VEs.h, ve6 + 2 * (i + 1))
                ins = s.dma_start(out=outT[128 * ot:128 * (ot + 1), 512 * th:512 * (th + 1)], in_=oat[i % 4][:])
                STO.inc(ins, 16)
        s.wait_ge(STO.h, STO.n)

    if DEBUG:
        @block.gpsimd
        def _(g):
            g.wait_ge(STO.h, 1024)
            g.dma_start(out=dbg["dbg_yo"], in_=yo_d).then_inc(DBG.h, 16)
            g.wait_ge(DBG.h, DBG.n + 16)
        DBG.n += 16

    st4b.close()
    st4.close()
    return nc, st


def _prep(inputs):
    x = np.asarray(inputs["L0_attn_hidden_states"])[0]
    W_in = np.asarray(inputs["L0_attn_in_proj_w"])
    conv_w = np.asarray(inputs["L0_attn_conv_w"])
    conv_b = np.asarray(inputs["L0_attn_conv_b"])
    dt_bias = np.asarray(inputs["L0_attn_dt_bias"])
    A_log = np.asarray(inputs["L0_attn_A_log"])
    D = np.asarray(inputs["L0_attn_D"])
    norm_w = np.asarray(inputs["L0_attn_norm_w"])
    W_out = np.asarray(inputs["L0_attn_out_proj_w"])
    ternary = np.asarray(inputs["L0_ffn_ternary"])
    scales = np.asarray(inputs["L0_ffn_scales"])

    xT = np.ascontiguousarray(x.T.astype(np.float32))
    wp = np.ascontiguousarray(W_in.astype(np.float32).reshape(N_OT, 128, 32, 128).transpose(0, 3, 2, 1)).reshape(N_OT, 128, 4096)
    wopk = np.ascontiguousarray(W_out.astype(np.float32).reshape(32, 128, 64, 128).transpose(0, 3, 2, 1)).reshape(32, 128, 8192)
    ternk = np.ascontiguousarray(ternary.reshape(32, 128, 32, 128).transpose(0, 3, 2, 1)).reshape(32, 128, 4096)
    s2 = scales.astype(np.float32).reshape(4096, 32)
    s3 = np.ascontiguousarray(s2.reshape(32, 128, 32).transpose(0, 2, 1))
    scfk = np.ascontiguousarray(np.broadcast_to(s3[:, None, :, :], (32, 128, 32, 128)).reshape(32, 128, 4096))
    cwp = np.ascontiguousarray(conv_w.astype(np.float32).reshape(N_XBC, 128, 4).transpose(1, 0, 2)).reshape(128, N_XBC * 4)
    cbp = np.ascontiguousarray(conv_b.astype(np.float32).reshape(N_XBC, 128).transpose(1, 0))
    dtb = np.ascontiguousarray(dt_bias.astype(np.float32).reshape(128, 1))
    alg_ = np.ascontiguousarray(A_log.astype(np.float32).reshape(128, 1))
    dexp = np.empty((128, 64), np.float32)
    nwp_ = np.empty((128, 64), np.float32)
    ar = np.arange(128)
    for tt in range(64):
        ch = 128 * tt + ar
        dexp[:, tt] = D[ch // 64]
        nwp_[:, tt] = norm_w[ch]
    in_maps = []
    for g in range(NCORE):
        lo = g * T
        if g == 0:
            xs = np.concatenate([np.zeros((HID, 3), np.float32), xT[:, 0:T]], axis=1)
        else:
            xs = np.ascontiguousarray(xT[:, lo - 3:lo + T])
        m2v = (np.arange(8) < g).astype(np.float32).reshape(8, 1)
        d = dict(xt=xs, wp=wp, wop=wopk, tern=ternk, scf=scfk,
                 cwi=cwp, cbi=cbp, dtbi=dtb, alogi=alg_, dexpi=dexp, nwi=nwp_,
                 m2i=np.ascontiguousarray(m2v), m2ci=np.ascontiguousarray(1.0 - m2v),
                 m2hi=np.ascontiguousarray(np.broadcast_to(m2v.reshape(1, 8), (128, 8))))
        in_maps.append(d)
    return in_maps


def kernel(**inputs):
    in_maps = _prep(inputs)
    nc, st = build_nc()
    try:
        res = run_bass_kernel_spmd(nc, in_maps, core_ids=list(range(NCORE)))
    finally:
        st.close()
    outs = [np.asarray(res.results[g]["outT"]) for g in range(NCORE)]
    outT = np.concatenate(outs, axis=1)
    out = np.ascontiguousarray(outT.T).reshape(1, 8192, HID).astype(np.float32)
    if DEBUG:
        kernel._last = res
    return out


# version 09
# speedup vs baseline: 1.0011x; 1.0011x over previous
import numpy as np
import ml_dtypes
from contextlib import ExitStack

import concourse.bass as bass
import concourse.mybir as mybir
from concourse.bass_utils import run_bass_kernel_spmd

F32 = mybir.dt.float32
BF16 = mybir.dt.bfloat16
I8 = mybir.dt.int8
AF = mybir.ActivationFunctionType
ALU = mybir.AluOpType

HID = 4096
T = 1024
TH = T + 3
NH = 128
HD = 64
DIN = 8192
G = 8
NST = 128
CS = 128
NCH = T // CS
N_OT = 145
XBC0 = 64
N_XBC = 80
NCORE = 8
EPS = 1e-5

DEBUG = False
CONSTS = (1e-5, -0.5, -1.0, 1.0, 1.0 / 1024.0)


def bv(ap2d, p, k, reps):
    return ap2d.unsqueeze(2).broadcast_to((p, k, reps))


def build_nc():
    nc = bass.Bass("TRN2", target_bir_lowering=False)

    xt = nc.dram_tensor("xt", [HID, TH], F32, kind="ExternalInput").ap()
    wp = nc.dram_tensor("wp", [N_OT, 128, 4096], F32, kind="ExternalInput").ap()
    wop = nc.dram_tensor("wop", [32, 128, 8192], F32, kind="ExternalInput").ap()
    tern = nc.dram_tensor("tern", [32, 128, 4096], I8, kind="ExternalInput").ap()
    scf = nc.dram_tensor("scf", [32, 128, 4096], F32, kind="ExternalInput").ap()
    cwi = nc.dram_tensor("cwi", [128, N_XBC * 4], F32, kind="ExternalInput").ap()
    cbi = nc.dram_tensor("cbi", [128, N_XBC], F32, kind="ExternalInput").ap()
    dtbi = nc.dram_tensor("dtbi", [128, 1], F32, kind="ExternalInput").ap()
    alogi = nc.dram_tensor("alogi", [128, 1], F32, kind="ExternalInput").ap()
    dexpi = nc.dram_tensor("dexpi", [128, 64], F32, kind="ExternalInput").ap()
    nwi = nc.dram_tensor("nwi", [128, 64], F32, kind="ExternalInput").ap()
    m2i = nc.dram_tensor("m2i", [8, 1], F32, kind="ExternalInput").ap()
    m2ci = nc.dram_tensor("m2ci", [8, 1], F32, kind="ExternalInput").ap()
    m2hi = nc.dram_tensor("m2hi", [128, 8], F32, kind="ExternalInput").ap()

    outT = nc.dram_tensor("outT", [HID, T], F32, kind="ExternalOutput").ap()

    x_cm = nc.dram_tensor("x_cm", [DIN, T], BF16).ap()
    b_cm = nc.dram_tensor("b_cm", [G * NST, T], BF16).ap()
    xn_d = nc.dram_tensor("xn_d", [128, 32 * TH], BF16).ap()
    xs_d = nc.dram_tensor("xs_d", [NCH, 128, DIN], BF16).ap()
    st_d = nc.dram_tensor("st_d", [NCH, 128, DIN], BF16).ap()
    yf_d = nc.dram_tensor("yf_d", [T, DIN], BF16).ap()
    yo_d = nc.dram_tensor("yo_d", [DIN, T], BF16).ap()
    cfl_d = nc.dram_tensor("cfl_d", [NCH, CS * NH], F32).ap()
    contrib = nc.dram_tensor("contrib", [128, DIN], BF16).ap()
    gath = nc.dram_tensor("gath", [NCORE * 128, DIN], BF16).ap()
    alc = nc.dram_tensor("alc", [1, 128], F32).ap()
    wob_d = nc.dram_tensor("wob_d", [32, 128, 8192], BF16).ap()
    alg = nc.dram_tensor("alg", [NCORE, 128], F32).ap()
    dbg = {}
    if DEBUG:
        for nm, shape, dty in [("dbg_xcm", [DIN, T], BF16), ("dbg_bcm", [G * NST, T], BF16),
                               ("dbg_yf", [T, DIN], BF16), ("dbg_yo", [DIN, T], BF16),
                               ("dbg_st", [NCH * 128, DIN], BF16), ("dbg_cum", [128, T], F32),
                               ("dbg_gath", [NCORE * 128, DIN], BF16), ("dbg_xn", [128, 32 * TH], BF16),
                               ("dbg_dt", [128, T], F32), ("dbg_yg", [DIN, T], BF16),
                               ("dbg_rs", [8 * 128, T], BF16)]:
            dbg[nm] = nc.dram_tensor(nm, shape, dty, kind="ExternalOutput").ap()

    st = ExitStack()

    def sb(stk, name, shape, dtype):
        return stk.enter_context(nc.sbuf_tensor(name, shape, dtype))

    ones1 = sb(st, "ones1", [1, 128], F32)
    onescol = sb(st, "onescol", [128, 1], F32)
    onescol_bf = sb(st, "onescol_bf", [128, 1], BF16)
    eye = sb(st, "eye", [128, 128], F32)
    umask = sb(st, "umask", [128, 128], F32)
    scr_a = sb(st, "scr_a", [128, 128], F32)
    scr_b = sb(st, "scr_b", [128, 128], F32)
    s_dtb = sb(st, "s_dtb", [128, 1], F32)
    s_acol = sb(st, "s_acol", [128, 1], F32)
    alsum = sb(st, "alsum", [1, 128], F32)
    lrow = sb(st, "lrow", [1, 128], F32)
    s_m2 = sb(st, "s_m2", [8, 1], F32)
    s_m2c = sb(st, "s_m2c", [8, 1], F32)
    s_m2h = sb(st, "s_m2h", [128, 8], F32)
    s_ag = sb(st, "s_ag", [8, 128], F32)
    a_hm = sb(st, "a_hm", [128, 8], F32)
    whm = sb(st, "whm", [128, 8], F32)
    wexp = [sb(st, f"wexp{i}", [128, 128], BF16) for i in range(2)]
    cst = sb(st, "cst", [128, 8], F32)
    dmy = sb(st, "dmy", [1, 8], F32)

    st23 = ExitStack()
    cumT = sb(st23, "cumT", [128, T], F32)
    dtT_bf = sb(st23, "dtT_bf", [128, T], BF16)
    sd_f = sb(st23, "sd_f", [128, T], F32)
    dc_bf = sb(st23, "dc_bf", [128, T], BF16)
    al_bf = sb(st23, "al_bf", [128, T], BF16)
    bcm = sb(st23, "bcm", [128, G * T], BF16)
    ccm = sb(st23, "ccm", [128, G * T], BF16)

    pA = st.enter_context(nc.psum_tensor("pA", [128, 512], F32))
    pB = st.enter_context(nc.psum_tensor("pB", [128, 512], F32))
    pC = st.enter_context(nc.psum_tensor("pC", [128, 512], F32))
    pD = st.enter_context(nc.psum_tensor("pD", [128, 512], F32))
    pE = st.enter_context(nc.psum_tensor("pE", [128, 512], F32))
    pF = st.enter_context(nc.psum_tensor("pF", [128, 512], F32))
    pG = st.enter_context(nc.psum_tensor("pG", [128, 512], F32))
    pH = st.enter_context(nc.psum_tensor("pH", [128, 512], F32))

    class S:
        def __init__(self, name):
            self.h = st.enter_context(nc.semaphore(name))
            self.n = 0

        def inc(self, instr, k=1):
            instr.then_inc(self.h, k)
            self.n += k
            return self.n

    LW = S("LW")
    LX = S("LX")
    SP = S("SP")
    CC = S("CC")
    PEs = S("PEs")
    VEs = S("VEs")
    AEs = S("AEs")
    GPs = S("GPs")
    BAR = S("BAR")
    STO = S("STO")
    DBG = S("DBG")
    WCL = S("WCL")
    WCS = S("WCS")
    LWB = S("LWB")

    block = st.enter_context(nc.Block())
    barn = [0]

    def barrier():
        barn[0] += 5
        tgt = barn[0]
        for reg in (block.tensor, block.vector, block.scalar, block.gpsimd, block.sync):
            @reg
            def _(e):
                e.sem_inc(BAR.h, 1)
                e.wait_ge(BAR.h, tgt)
        BAR.n = tgt

    @block.gpsimd
    def _(g):
        g.iota(scr_a[:], [[0, 128]], channel_multiplier=1, allow_small_or_imprecise_dtypes=True)
        g.iota(scr_b[:], [[1, 128]], channel_multiplier=0, allow_small_or_imprecise_dtypes=True)
        g.memset(ones1[:], 1.0)
        g.memset(onescol[:], 1.0)
        g.memset(onescol_bf[:], 1.0)
        g.memset(alsum[:], 0.0)
        ins = g.memset(whm[:], 0.0)
        GPs.inc(ins)
        g.dma_start(out=s_dtb[:], in_=dtbi).then_inc(LW.h, 16)
        g.dma_start(out=s_acol[:], in_=alogi).then_inc(LW.h, 16)
        g.dma_start(out=s_m2[:], in_=m2i).then_inc(LW.h, 16)
        g.dma_start(out=s_m2c[:], in_=m2ci).then_inc(LW.h, 16)
        g.dma_start(out=s_m2h[:], in_=m2hi).then_inc(LW.h, 16)
        for ci, cv in enumerate(CONSTS):
            ins = g.memset(cst[:, ci:ci + 1], cv)
        GPs.inc(ins)
    LW.n += 80
    for ci, cv in enumerate(CONSTS):
        nc.const_aps.register(F32, cv, cst[:, ci:ci + 1])

    @block.vector
    def _(v):
        v.wait_ge(GPs.h, GPs.n)
        v.tensor_tensor(eye[:], scr_a[:], scr_b[:], ALU.is_equal)
        ins = v.tensor_tensor(umask[:], scr_a[:], scr_b[:], ALU.is_le)
        VEs.inc(ins)

    @block.scalar
    def _(s):
        s.wait_ge(LW.h, 80)
        s.wait_ge(GPs.h, GPs.n)
        s.activation(s_acol[:], s_acol[:], AF.Exp)
        ins = s.mul(s_acol[:], s_acol[:], -1.0)
        AEs.inc(ins)

    barrier()

    st12 = ExitStack()
    xn_cm = sb(st12, "xn_cm", [128, 32 * TH], BF16)
    st1 = ExitStack()
    xbuf = [sb(st1, f"xbuf{i}", [128, TH], F32) for i in range(2)]
    sqb = [sb(st1, f"sqb{i}", [128, TH], F32) for i in range(2)]
    mrow = sb(st1, "mrow", [1, TH], F32)
    vrow = sb(st1, "vrow", [1, TH], F32)
    mbc = sb(st1, "mbc", [128, TH], F32)
    rbc = sb(st1, "rbc", [128, TH], F32)
    tmp1 = sb(st1, "tmp1", [128, TH], F32)

    lx0, ae0, pe0, ve0, lw0 = LX.n, AEs.n, PEs.n, VEs.n, LW.n
    ld_tile1 = {}

    @block.sync
    def _(sy):
        for i in range(0, 32, 2):
            if i >= 2:
                sy.wait_ge(PEs.h, pe0 + i - 1)
                sy.wait_ge(AEs.h, ae0 + i - 1)
            ins = sy.dma_start(out=xbuf[i % 2][:], in_=xt[128 * i:128 * (i + 1), :])
            LX.inc(ins, 16)
            ld_tile1[i] = (LX, LX.n)

    @block.gpsimd
    def _(g):
        for i in range(1, 32, 2):
            if i >= 2:
                g.wait_ge(PEs.h, pe0 + i - 1)
                g.wait_ge(AEs.h, ae0 + i - 1)
            ins = g.dma_start(out=xbuf[i % 2][:], in_=xt[128 * i:128 * (i + 1), :])
            LW.inc(ins, 16)
            ld_tile1[i] = (LW, LW.n)

    @block.scalar
    def _(s):
        for i in range(32):
            sm, val = ld_tile1[i]
            s.wait_ge(sm.h, val)
            if i >= 2:
                s.wait_ge(PEs.h, pe0 + i - 1)
            ins = s.activation(sqb[i % 2][:], xbuf[i % 2][:], AF.Square)
            AEs.inc(ins)

    @block.tensor
    def _(t):
        for i in range(32):
            s0, s1 = (i == 0), (i == 31)
            sm, val = ld_tile1[i]
            t.wait_ge(sm.h, val)
            t.wait_ge(AEs.h, ae0 + i + 1)
            t.matmul(pA[0:1, :], onescol[:], xbuf[i % 2][:, 0:512], start=s0, stop=s1)
            t.matmul(pB[0:1, :], onescol[:], xbuf[i % 2][:, 512:1024], start=s0, stop=s1)
            t.matmul(pH[0:1, 0:3], onescol[:], xbuf[i % 2][:, 1024:1027], start=s0, stop=s1)
            t.matmul(pC[0:1, :], onescol[:], sqb[i % 2][:, 0:512], start=s0, stop=s1)
            t.matmul(pD[0:1, :], onescol[:], sqb[i % 2][:, 512:1024], start=s0, stop=s1)
            ins = t.matmul(pE[0:1, 0:3], onescol[:], sqb[i % 2][:, 1024:1027], start=s0, stop=s1)
            PEs.inc(ins)

    @block.scalar
    def _(s):
        s.wait_ge(PEs.h, pe0 + 32)
        s.copy(mrow[0:1, 0:512], pA[0:1, :])
        s.copy(mrow[0:1, 512:1024], pB[0:1, :])
        s.copy(mrow[0:1, 1024:1027], pH[0:1, 0:3])
        s.copy(vrow[0:1, 0:512], pC[0:1, :])
        s.copy(vrow[0:1, 512:1024], pD[0:1, :])
        ins = s.copy(vrow[0:1, 1024:1027], pE[0:1, 0:3])
        AEs.inc(ins)

    @block.vector
    def _(v):
        v.wait_ge(AEs.h, ae0 + 33)
        v.tensor_scalar(mrow[:], mrow[:], 1.0 / HID, None, ALU.mult)
        v.tensor_scalar(vrow[:], vrow[:], 1.0 / HID, None, ALU.mult)
        v.tensor_tensor(tmp1[0:1, :], mrow[:], mrow[:], ALU.mult)
        ins = v.tensor_tensor(vrow[:], vrow[:], tmp1[0:1, :], ALU.subtract)
        VEs.inc(ins)

    @block.scalar
    def _(s):
        s.wait_ge(VEs.h, ve0 + 1)
        s.activation(vrow[:], vrow[:], AF.Ln, bias=EPS)
        ins = s.activation(vrow[:], vrow[:], AF.Exp, scale=-0.5)
        AEs.inc(ins)

    @block.tensor
    def _(t):
        t.wait_ge(AEs.h, ae0 + 34)
        t.matmul(pA[:], ones1[:], mrow[0:1, 0:512], start=True, stop=True)
        t.matmul(pB[:], ones1[:], mrow[0:1, 512:1024], start=True, stop=True)
        t.matmul(pH[:, 0:3], ones1[:], mrow[0:1, 1024:1027], start=True, stop=True)
        t.matmul(pC[:], ones1[:], vrow[0:1, 0:512], start=True, stop=True)
        t.matmul(pD[:], ones1[:], vrow[0:1, 512:1024], start=True, stop=True)
        ins = t.matmul(pE[:, 0:3], ones1[:], vrow[0:1, 1024:1027], start=True, stop=True)
        PEs.inc(ins)

    @block.scalar
    def _(s):
        s.wait_ge(PEs.h, pe0 + 33)
        s.copy(mbc[:, 0:512], pA[:])
        s.copy(mbc[:, 512:1024], pB[:])
        s.copy(mbc[:, 1024:1027], pH[:, 0:3])
        s.copy(rbc[:, 0:512], pC[:])
        s.copy(rbc[:, 512:1024], pD[:])
        ins = s.copy(rbc[:, 1024:1027], pE[:, 0:3])
        AEs.inc(ins)

    lx1, ve1 = LX.n, VEs.n
    ld_tile2 = {}

    @block.sync
    def _(sy):
        for i in range(0, 32, 2):
            if i >= 2:
                sy.wait_ge(VEs.h, ve1 + i - 1)
            else:
                sy.wait_ge(PEs.h, pe0 + 32)
                sy.wait_ge(AEs.h, ae0 + 32)
            ins = sy.dma_start(out=xbuf[i % 2][:], in_=xt[128 * i:128 * (i + 1), :])
            LX.inc(ins, 16)
            ld_tile2[i] = (LX, LX.n)

    @block.gpsimd
    def _(g):
        for i in range(1, 32, 2):
            if i >= 2:
                g.wait_ge(VEs.h, ve1 + i - 1)
            else:
                g.wait_ge(PEs.h, pe0 + 32)
                g.wait_ge(AEs.h, ae0 + 32)
            ins = g.dma_start(out=xbuf[i % 2][:], in_=xt[128 * i:128 * (i + 1), :])
            LW.inc(ins, 16)
            ld_tile2[i] = (LW, LW.n)

    @block.vector
    def _(v):
        v.wait_ge(AEs.h, ae0 + 35)
        for i in range(32):
            sm, val = ld_tile2[i]
            v.wait_ge(sm.h, val)
            v.tensor_tensor(tmp1[:], xbuf[i % 2][:], mbc[:], ALU.subtract)
            ins = v.tensor_tensor(xn_cm[:, TH * i:TH * (i + 1)], tmp1[:], rbc[:], ALU.mult)
            VEs.inc(ins)

    @block.gpsimd
    def _(g):
        g.wait_ge(VEs.h, ve1 + 32)
        ins = g.dma_start(out=xn_d, in_=xn_cm[:])
        SP.inc(ins, 16)

    st1.close()
    barrier()

    st2 = ExitStack()
    wbuf = [sb(st2, f"wbuf{i}", [128, 4096], BF16) for i in range(3)]
    zx = [sb(st2, f"zx{i}", [128, TH], F32) for i in range(2)]
    cacc = [sb(st2, f"cacc{i}", [128, T], F32) for i in range(2)]
    xout = [sb(st2, f"xout{i}", [128, T], BF16) for i in range(2)]
    s_cw = sb(st2, "s_cw", [128, N_XBC * 4], F32)
    s_cb = sb(st2, "s_cb", [128, N_XBC], F32)
    dt_f = sb(st2, "dt_f", [128, T], F32)
    adt = sb(st2, "adt", [128, T], F32)
    cum_hm = sb(st2, "cum_hm", [128, T], F32)
    tdt = sb(st2, "tdt", [128, T], F32)
    z128 = sb(st2, "z128", [128, 128], F32)

    lw2, pe2, ae2, ve2, sp2 = LW.n, PEs.n, AEs.n, VEs.n, SP.n
    NT2 = N_XBC + 1

    act_seq = []
    for i in range(NT2):
        act_seq.append(("evac", i))
        act_seq.append(("c0", i))
        if i >= 1 and i - 1 < N_XBC:
            act_seq.append(("silu", i - 1))
    act_at = {tg: ae2 + k + 1 for k, tg in enumerate(act_seq)}

    @block.gpsimd
    def _(g):
        g.memset(z128[:], 0.0)
        g.dma_start(out=s_cw[:], in_=cwi).then_inc(LW.h, 16)
        g.dma_start(out=s_cb[:], in_=cbi).then_inc(LW.h, 16)
        for i in range(NT2):
            if i >= 3:
                g.wait_ge(PEs.h, pe2 + i - 2)
            ins = g.dma_start(out=wbuf[i % 3][:], in_=wp[XBC0 + i])
            LW.inc(ins, 16)
    LW.n += 32
    lw2b = lw2 + 32

    @block.sync
    def _(sy):
        for j in range(64):
            sy.wait_ge(AEs.h, act_at[("silu", j)])
            ins = sy.dma_start(out=x_cm[128 * j:128 * (j + 1), :], in_=xout[j % 2][:])
            SP.inc(ins, 16)
        sy.wait_ge(AEs.h, act_at[("silu", 71)])
        for gi in range(8):
            ins = sy.dma_start(out=b_cm[NST * gi:NST * (gi + 1), :], in_=bcm[:, T * gi:T * (gi + 1)])
            SP.inc(ins, 16)

    @block.tensor
    def _(t):
        psets = [(pA, pB), (pC, pD)]
        halos = [pE, pH]
        for i in range(NT2):
            t.wait_ge(LW.h, lw2b + 16 * (i + 1))
            if i >= 2:
                t.wait_ge(AEs.h, act_at[("evac", i - 2)])
            a, b = psets[i % 2]
            hp = halos[i % 2]
            w = wbuf[i % 3]
            if i < N_XBC:
                for k in range(32):
                    s0, s1 = (k == 0), (k == 31)
                    t.matmul(a[:], w[:, 128 * k:128 * (k + 1)], xn_cm[:, TH * k:TH * k + 512], start=s0, stop=s1)
                    t.matmul(b[:], w[:, 128 * k:128 * (k + 1)], xn_cm[:, TH * k + 512:TH * k + 1024], start=s0, stop=s1)
                    ins = t.matmul(hp[:, 0:3], w[:, 128 * k:128 * (k + 1)],
                                   xn_cm[:, TH * k + 1024:TH * k + 1027], start=s0, stop=s1)
            else:
                for k in range(32):
                    s0, s1 = (k == 0), (k == 31)
                    t.matmul(a[:], w[:, 128 * k:128 * (k + 1)], xn_cm[:, TH * k + 3:TH * k + 515], start=s0, stop=s1)
                    ins = t.matmul(b[:], w[:, 128 * k:128 * (k + 1)], xn_cm[:, TH * k + 515:TH * k + 1027], start=s0, stop=s1)
            PEs.inc(ins)

    @block.scalar
    def _(s):
        psets = [(pA, pB), (pC, pD)]
        halos = [pE, pH]
        for i in range(NT2):
            s.wait_ge(PEs.h, pe2 + i + 1)
            if i >= 2:
                s.wait_ge(VEs.h, ve2 + i - 1)
            a, b = psets[i % 2]
            hp = halos[i % 2]
            z = zx[i % 2]
            s.copy(z[:, 0:512], a[:])
            s.copy(z[:, 512:1024], b[:])
            if i < N_XBC:
                ins = s.copy(z[:, 1024:1027], hp[:, 0:3])
                AEs.inc(ins)
                ins = s.activation(cacc[i % 2][:], z[:, 0:T], AF.Copy, scale=s_cw[:, 4 * i:4 * i + 1])
                AEs.inc(ins)
            else:
                ins = s.copy(z[:, 1024:1027], z[:, 1024:1027])
                AEs.inc(ins)
                ins = s.activation(cacc[i % 2][:], z[:, 0:T], AF.Copy)
                AEs.inc(ins)
            if i >= 1 and i - 1 < N_XBC:
                j = i - 1
                s.wait_ge(VEs.h, ve2 + j + 1)
                if j < 64:
                    s.wait_ge(SP.h, sp2 + 16 * max(0, j - 1))
                    dst = xout[j % 2][:]
                elif j < 72:
                    dst = bcm[:, T * (j - 64):T * (j - 63)]
                else:
                    dst = ccm[:, T * (j - 72):T * (j - 71)]
                ins = s.activation(dst, cacc[j % 2][:], AF.Silu, bias=s_cb[:, j:j + 1])
                AEs.inc(ins)
    assert AEs.n == ae2 + len(act_seq), (AEs.n, ae2, len(act_seq))

    @block.vector
    def _(v):
        for i in range(N_XBC):
            v.wait_ge(AEs.h, act_at[("c0", i)])
            z = zx[i % 2]
            c = cacc[i % 2]
            v.scalar_tensor_tensor(c[:], z[:, 1:1 + T], s_cw[:, 4 * i + 1:4 * i + 2], c[:], ALU.mult, ALU.add)
            v.scalar_tensor_tensor(c[:], z[:, 2:2 + T], s_cw[:, 4 * i + 2:4 * i + 3], c[:], ALU.mult, ALU.add)
            ins = v.scalar_tensor_tensor(c[:], z[:, 3:3 + T], s_cw[:, 4 * i + 3:4 * i + 4], c[:], ALU.mult, ALU.add)
            VEs.inc(ins)
        v.wait_ge(AEs.h, act_at[("c0", N_XBC)])
        u = cacc[N_XBC % 2]
        v.tensor_scalar(tdt[:], u[:], s_dtb[:], None, ALU.add)
        ins = v.tensor_scalar(dt_f[:], tdt[:], 0.0, None, ALU.max)
        VEs.inc(ins)

    ve2b = ve2 + N_XBC + 1

    @block.scalar
    def _(s):
        s.wait_ge(VEs.h, ve2b)
        s.activation(tdt[:], tdt[:], AF.Abs)
        s.activation(tdt[:], tdt[:], AF.Exp, scale=-1.0)
        ins = s.activation(tdt[:], tdt[:], AF.Ln, bias=1.0)
        AEs.inc(ins)
    ae2b = AEs.n

    @block.vector
    def _(v):
        v.wait_ge(AEs.h, ae2b)
        v.tensor_tensor(dt_f[:], dt_f[:], tdt[:], ALU.add)
        ins = v.tensor_scalar(adt[:], dt_f[:], s_acol[:], None, ALU.mult)
        VEs.inc(ins)
        for c in range(NCH):
            ins = v.tensor_tensor_scan(cum_hm[:, CS * c:CS * (c + 1)], adt[:, CS * c:CS * (c + 1)],
                                       z128[:], 0.0, ALU.add, ALU.add)
            VEs.inc(ins)
    ve2c = VEs.n

    peX, aeX, veX = PEs.n, AEs.n, VEs.n
    for c in range(NCH):
        @block.tensor
        def _(t, c=c):
            t.wait_ge(VEs.h, ve2c)
            if c >= 1:
                t.wait_ge(AEs.h, aeX + 3 * c)
            t.transpose(pF[:, 0:128], cum_hm[:, CS * c:CS * (c + 1)], eye[:])
            ins = t.transpose(pG[:, 0:128], dt_f[:, CS * c:CS * (c + 1)], eye[:])
            PEs.inc(ins)
            ins = t.transpose(pE[0:1, 0:128], cum_hm[:, CS * c + 127:CS * c + 128], eye[:])
            PEs.inc(ins)

        @block.scalar
        def _(s, c=c):
            s.wait_ge(PEs.h, peX + 3 * c + 2)
            s.copy(cumT[:, CS * c:CS * (c + 1)], pF[:, 0:128])
            ins = s.copy(dtT_bf[:, CS * c:CS * (c + 1)], pG[:, 0:128])
            AEs.inc(ins)
            ins = s.copy(lrow[:], pE[0:1, 0:128])
            AEs.inc(ins)

        @block.tensor
        def _(t, c=c):
            t.wait_ge(AEs.h, aeX + 3 * c + 2)
            ins = t.matmul(pF[:, 0:128], ones1[:], lrow[:], start=True, stop=True)
            PEs.inc(ins)

        @block.vector
        def _(v, c=c):
            v.wait_ge(PEs.h, peX + 3 * (c + 1))
            v.tensor_tensor(scr_a[:], pF[:, 0:128], cumT[:, CS * c:CS * (c + 1)], ALU.subtract)
            ins = v.tensor_tensor(alsum[:], alsum[:], lrow[:], ALU.add)
            VEs.inc(ins)

        @block.scalar
        def _(s, c=c):
            s.wait_ge(VEs.h, veX + c + 1)
            s.activation(dc_bf[:, CS * c:CS * (c + 1)], scr_a[:], AF.Exp)
            s.activation(al_bf[:, CS * c:CS * (c + 1)], pF[:, 0:128], AF.Exp)
            ins = s.activation(sd_f[:, CS * c:CS * (c + 1)], cumT[:, CS * c:CS * (c + 1)], AF.Exp)
            AEs.inc(ins)

    @block.gpsimd
    def _(g):
        g.wait_ge(VEs.h, ve2c)
        for c in range(NCH):
            ins = g.dma_start(out=cfl_d[c].rearrange("(h i) -> h i", h=128), in_=cum_hm[:, CS * c:CS * (c + 1)])
            SP.inc(ins, 16)

    @block.scalar
    def _(s):
        s.wait_ge(VEs.h, veX + NCH)
        ins = s.activation(alsum[:], alsum[:], AF.Exp)
        AEs.inc(ins)
    aeX2 = AEs.n

    @block.gpsimd
    def _(g):
        g.wait_ge(AEs.h, aeX2)
        ins = g.dma_start(out=alc, in_=alsum[:])
        SP.inc(ins, 16)

    if DEBUG:
        @block.gpsimd
        def _(g):
            g.wait_ge(AEs.h, aeX2)
            g.dma_start(out=dbg["dbg_cum"], in_=cum_hm[:]).then_inc(DBG.h, 16)
            g.dma_start(out=dbg["dbg_dt"], in_=dt_f[:]).then_inc(DBG.h, 16)
        DBG.n += 32

    st2.close()
    st12.close()
    barrier()

    st3 = ExitStack()
    xsb = sb(st3, "xsb", [128, DIN], BF16)
    incb = sb(st3, "incb", [128, DIN], BF16)
    sreb = sb(st3, "sreb", [128, DIN], BF16)
    big1 = sb(st3, "big1", [128, DIN], BF16)
    st3a = ExitStack()
    xTb = sb(st3a, "xTb", [128, DIN], BF16)
    xs2b = sb(st3a, "xs2b", [128, DIN], BF16)
    btb = sb(st3a, "btb", [128, G * NST], BF16)

    lx3, pe3, ae3, ve3, sp3 = LX.n, PEs.n, AEs.n, VEs.n, SP.n

    @block.sync
    def _(sy):
        sy.wait_ge(SP.h, sp3)
        for c in range(NCH):
            if c >= 1:
                sy.wait_ge(VEs.h, ve3 + 1 + 4 * c)
                sy.wait_ge(PEs.h, pe3 + 16 * c)
            ins = sy.dma_start(out=xTb[:], in_=x_cm[:, CS * c:CS * (c + 1)], transpose=True)
            LX.inc(ins, 16)
            ins = sy.dma_start(out=btb[:], in_=b_cm[:, CS * c:CS * (c + 1)], transpose=True)
            LX.inc(ins, 16)

    @block.vector
    def _(v):
        ins = v.memset(incb[:], 0.0)
        VEs.inc(ins)
        for c in range(NCH):
            v.wait_ge(LX.h, lx3 + 32 * (c + 1))
            if c >= 1:
                v.wait_ge(SP.h, sp3 + 32 * c)
            xv = xsb[:].rearrange("p (h d) -> p h d", h=NH)
            ins = v.tensor_tensor(xv, xTb[:].rearrange("p (h d) -> p h d", h=NH),
                                  bv(dtT_bf[:, CS * c:CS * (c + 1)], 128, NH, HD), ALU.mult)
            VEs.inc(ins)
            ins = v.tensor_tensor(xs2b[:].rearrange("p (h d) -> p h d", h=NH), xv,
                                  bv(dc_bf[:, CS * c:CS * (c + 1)], 128, NH, HD), ALU.mult)
            VEs.inc(ins)
            v.wait_ge(AEs.h, ae3 + 16 * (c + 1))
            ins = v.tensor_tensor(xTb[:].rearrange("p (h d) -> p h d", h=NH),
                                  incb[:].rearrange("p (h d) -> p h d", h=NH),
                                  bv(al_bf[:, CS * c:CS * (c + 1)], 128, NH, HD), ALU.mult)
            VEs.inc(ins)
            ins = v.tensor_tensor(incb[:], xTb[:], big1[:], ALU.add)
            VEs.inc(ins)

    @block.tensor
    def _(t):
        for c in range(NCH):
            t.wait_ge(VEs.h, ve3 + 1 + 4 * c + 2)
            for hb in range(16):
                p = pA if hb % 2 == 0 else pB
                m = 16 * c + hb
                if m >= 2:
                    t.wait_ge(AEs.h, ae3 + m - 1)
                for hh in range(8):
                    h = 8 * hb + hh
                    ins = t.matmul(p[:, 64 * hh:64 * (hh + 1)], btb[:, NST * (h // 16):NST * (h // 16 + 1)],
                                   xs2b[:, HD * h:HD * (h + 1)], start=True, stop=True)
                PEs.inc(ins)

    @block.scalar
    def _(s):
        for c in range(NCH):
            for hb in range(16):
                p = pA if hb % 2 == 0 else pB
                s.wait_ge(PEs.h, pe3 + 16 * c + hb + 1)
                if hb == 0 and c >= 1:
                    s.wait_ge(VEs.h, ve3 + 1 + 4 * c)
                    s.wait_ge(SP.h, sp3 + 32 * c)
                ins = s.copy(big1[:, 512 * hb:512 * (hb + 1)], p[:])
                AEs.inc(ins)

    @block.gpsimd
    def _(g):
        for c in range(NCH):
            g.wait_ge(VEs.h, ve3 + 1 + 4 * c + 1)
            ins = g.dma_start(out=xs_d[c], in_=xsb[:])
            SP.inc(ins, 16)
            g.wait_ge(AEs.h, ae3 + 16 * (c + 1))
            ins = g.dma_start(out=st_d[c], in_=big1[:])
            SP.inc(ins, 16)
        g.wait_ge(VEs.h, ve3 + 1 + 4 * NCH)
        ins = g.dma_start(out=contrib, in_=incb[:])
        SP.inc(ins, 16)
        g.wait_ge(SP.h, SP.n)
        ins = g.collective_compute("AllGather", ALU.bypass, replica_groups=[list(range(NCORE))],
                                   ins=[contrib.opt()], outs=[gath.opt()])
        CC.inc(ins)
        ins = g.collective_compute("AllGather", ALU.bypass, replica_groups=[list(range(NCORE))],
                                   ins=[alc.opt()], outs=[alg.opt()])
        CC.inc(ins)

    st3a.close()
    barrier()

    st3b = ExitStack()
    Eb = sb(st3b, "Eb", [128, NH * CS], BF16)
    g2m = sb(st3b, "g2m", [128, G * CS], BF16)
    t1b = [sb(st3b, f"t1b{i}", [128, 512], BF16) for i in range(2)]
    t1e = [sb(st3b, f"t1e{i}", [128, 512], BF16) for i in range(2)]
    t32b = sb(st3b, "t32b", [128, 512], F32)
    t1f = sb(st3b, "t1f", [128, 512], F32)
    cflb = sb(st3b, "cflb", [1, 4096], F32)

    peM, aeM, veM, lxM = PEs.n, AEs.n, VEs.n, LX.n

    @block.sync
    def _(sy):
        sy.wait_ge(CC.h, 2)
        ins = sy.dma_start(out=s_ag[:], in_=alg)
        LX.inc(ins, 16)

    @block.vector
    def _(v):
        v.wait_ge(LX.h, lxM + 16)
        ins = v.tensor_scalar(s_ag[:], s_ag[:], s_m2[:], s_m2c[:], ALU.mult, ALU.add)
        VEs.inc(ins)

    @block.tensor
    def _(t):
        t.wait_ge(VEs.h, veM + 1)
        ins = t.transpose(pF[:, 0:8], s_ag[:], eye[0:8, 0:8])
        PEs.inc(ins)

    @block.scalar
    def _(s):
        s.wait_ge(PEs.h, peM + 1)
        ins = s.copy(a_hm[:], pF[:, 0:8])
        AEs.inc(ins)

    @block.vector
    def _(v):
        v.wait_ge(AEs.h, aeM + 1)
        v.memset(whm[:, 7:8], 1.0)
        for gg in range(6, -1, -1):
            v.tensor_tensor(whm[:, gg:gg + 1], whm[:, gg + 1:gg + 2], a_hm[:, gg + 1:gg + 2], ALU.mult)
        ins = v.tensor_tensor(whm[:], whm[:], s_m2h[:], ALU.mult)
        VEs.inc(ins)
        ins = v.memset(incb[:], 0.0)
        VEs.inc(ins)

    peM2, aeM2, veM2, lxM2 = PEs.n, AEs.n, VEs.n, LX.n
    for gg in range(NCORE):
        @block.tensor
        def _(t, gg=gg):
            t.wait_ge(VEs.h, veM + 2)
            t.wait_ge(AEs.h, aeM2 + 2 * gg)
            ins = t.transpose(pG[0:1, 0:128], whm[:, gg:gg + 1], eye[:])
            PEs.inc(ins)

        @block.scalar
        def _(s, gg=gg):
            s.wait_ge(PEs.h, peM2 + 2 * gg + 1)
            ins = s.copy(lrow[:], pG[0:1, 0:128])
            AEs.inc(ins)

        @block.tensor
        def _(t, gg=gg):
            t.wait_ge(AEs.h, aeM2 + 2 * gg + 1)
            ins = t.matmul(pF[:, 0:128], ones1[:], lrow[:], start=True, stop=True)
            PEs.inc(ins)

        @block.scalar
        def _(s, gg=gg):
            s.wait_ge(PEs.h, peM2 + 2 * gg + 2)
            if gg >= 2:
                s.wait_ge(VEs.h, veM2 + 2 * (gg - 1))
            ins = s.copy(wexp[gg % 2][:], pF[:, 0:128])
            AEs.inc(ins)

        @block.sync
        def _(sy, gg=gg):
            if gg >= 2:
                sy.wait_ge(VEs.h, veM2 + 2 * (gg - 1))
            ins = sy.dma_start(out=(sreb if gg % 2 == 0 else big1)[:], in_=gath[128 * gg:128 * (gg + 1), :])
            LX.inc(ins, 16)

        @block.vector
        def _(v, gg=gg):
            v.wait_ge(AEs.h, aeM2 + 2 * gg + 2)
            v.wait_ge(LX.h, lxM2 + 16 * (gg + 1))
            src = (sreb if gg % 2 == 0 else big1)
            ins = v.tensor_tensor(Eb[:, 0:DIN].rearrange("p (h d) -> p h d", h=NH),
                                  src[:].rearrange("p (h d) -> p h d", h=NH),
                                  bv(wexp[gg % 2][:], 128, NH, HD), ALU.mult)
            VEs.inc(ins)
            ins = v.tensor_tensor(incb[:], incb[:], Eb[:, 0:DIN], ALU.add)
            VEs.inc(ins)

    barrier()

    lx4, pe4, ae4, ve4, sp4 = LX.n, PEs.n, AEs.n, VEs.n, SP.n

    dve_seq = []
    for c in range(NCH):
        if c >= 1:
            dve_seq.append(("incu", c))
        for g in range(8):
            dve_seq.append(("g2m", c, g))
        for s_ in range(32):
            dve_seq.append(("min", c, s_))
            if s_ >= 1:
                dve_seq.append(("emul", c, s_ - 1))
        dve_seq.append(("emul", c, 31))
        for hb in range(16):
            dve_seq.append(("mrg", c, hb))
    dve_at = {tg: ve4 + k + 1 for k, tg in enumerate(dve_seq)}

    pe_seq = []
    for c in range(NCH):
        for g in range(8):
            pe_seq.append(("g2", c, g))
        for s_ in range(32):
            pe_seq.append(("bc", c, s_))
        for hb in range(16):
            pe_seq.append(("yb", c, hb))
    pe_at = {tg: pe4 + k + 1 for k, tg in enumerate(pe_seq)}

    ae_at = {}
    k = 0
    for c in range(NCH):
        for s_ in range(32):
            k += 1
            ae_at[("exp", c, s_)] = ae4 + k

    lx_at = {}
    cnt = lx4
    for c in range(NCH):
        cnt += 16
        lx_at[("xs", c)] = cnt
        cnt += 16
        lx_at[("cf", c, 0)] = cnt
        if c >= 1:
            cnt += 16
            lx_at[("sre", c)] = cnt
        for q in range(1, 4):
            cnt += 16
            lx_at[("cf", c, q)] = cnt

    @block.sync
    def _(sy):
        sy.wait_ge(SP.h, sp4)
        for c in range(NCH):
            if c >= 1:
                sy.wait_ge(PEs.h, pe_at[("yb", c - 1, 15)])
            ins = sy.dma_start(out=xsb[:], in_=xs_d[c])
            LX.inc(ins, 16)
            if c >= 1:
                sy.wait_ge(PEs.h, pe_at[("bc", c - 1, 31)])
            ins = sy.dma_start(out=cflb[0:1, 0:4096], in_=cfl_d[c, 0:4096].rearrange("(a b) -> a b", a=1))
            LX.inc(ins, 16)
            if c >= 1:
                ins = sy.dma_start(out=sreb[:], in_=st_d[c - 1])
                LX.inc(ins, 16)
            for q in range(1, 4):
                sy.wait_ge(PEs.h, pe_at[("bc", c, 8 * q - 1)])
                ins = sy.dma_start(out=cflb[0:1, 0:4096],
                                   in_=cfl_d[c, 4096 * q:4096 * (q + 1)].rearrange("(a b) -> a b", a=1))
                LX.inc(ins, 16)

    @block.vector
    def _(v):
        for c in range(NCH):
            if c >= 1:
                v.wait_ge(LX.h, lx_at[("sre", c)])
                v.wait_ge(PEs.h, pe_at[("yb", c - 1, 15)])
                v.tensor_tensor(Eb[:, 0:DIN].rearrange("p (h d) -> p h d", h=NH),
                                incb[:].rearrange("p (h d) -> p h d", h=NH),
                                bv(al_bf[:, CS * (c - 1):CS * c], 128, NH, HD), ALU.mult)
                ins = v.tensor_tensor(incb[:], Eb[:, 0:DIN], sreb[:], ALU.add)
                VEs.inc(ins)
            for g in range(8):
                v.wait_ge(PEs.h, pe_at[("g2", c, g)])
                ins = v.tensor_tensor(g2m[:, CS * g:CS * (g + 1)], (pF if g % 2 == 0 else pG)[:, 0:128],
                                      umask[:], ALU.mult)
                VEs.inc(ins)
            for s_ in range(32):
                v.wait_ge(PEs.h, pe_at[("bc", c, s_)])
                p = pF if s_ % 2 == 0 else pG
                v.tensor_tensor(t32b[:].rearrange("p (hh i) -> p hh i", hh=4),
                                p[:].rearrange("p (hh i) -> p hh i", hh=4),
                                bv(cumT[:, CS * c + 4 * s_:CS * c + 4 * (s_ + 1)], 128, 4, 128), ALU.subtract)
                if s_ >= 2:
                    v.wait_ge(AEs.h, ae_at[("exp", c, s_ - 2)])
                elif c >= 1:
                    v.wait_ge(AEs.h, ae_at[("exp", c - 1, 30 + s_)])
                ins = v.tensor_scalar(t1b[s_ % 2][:], t32b[:], 0.0, None, ALU.min)
                VEs.inc(ins)
                if s_ >= 1:
                    sj = s_ - 1
                    v.wait_ge(AEs.h, ae_at[("exp", c, sj)])
                    gg = sj // 4
                    ins = v.tensor_tensor(Eb[:, 512 * sj:512 * (sj + 1)].rearrange("p (hh i) -> p hh i", hh=4),
                                          t1e[sj % 2][:].rearrange("p (hh i) -> p hh i", hh=4),
                                          g2m[:, CS * gg:CS * (gg + 1)].unsqueeze(1).broadcast_to((128, 4, 128)),
                                          ALU.mult)
                    VEs.inc(ins)
            sj = 31
            v.wait_ge(AEs.h, ae_at[("exp", c, sj)])
            gg = sj // 4
            ins = v.tensor_tensor(Eb[:, 512 * sj:512 * (sj + 1)].rearrange("p (hh i) -> p hh i", hh=4),
                                  t1e[sj % 2][:].rearrange("p (hh i) -> p hh i", hh=4),
                                  g2m[:, CS * gg:CS * (gg + 1)].unsqueeze(1).broadcast_to((128, 4, 128)),
                                  ALU.mult)
            VEs.inc(ins)
            for hb in range(16):
                v.wait_ge(PEs.h, pe_at[("yb", c, hb)])
                pyd = pA if hb % 2 == 0 else pC
                pyo = pB if hb % 2 == 0 else pD
                v.tensor_tensor(t1f[:].rearrange("p (hh d) -> p hh d", hh=8),
                                pyo[:].rearrange("p (hh d) -> p hh d", hh=8),
                                bv(sd_f[:, CS * c + 8 * hb:CS * c + 8 * (hb + 1)], 128, 8, HD), ALU.mult)
                if c >= 1 and hb == 0:
                    v.wait_ge(SP.h, sp4 + 16 * c)
                ins = v.tensor_tensor(big1[:, 512 * hb:512 * (hb + 1)], t1f[:], pyd[:], ALU.add)
                VEs.inc(ins)
    assert VEs.n == ve4 + len(dve_seq), (VEs.n, ve4, len(dve_seq))

    @block.tensor
    def _(t):
        for c in range(NCH):
            for g in range(8):
                p = pF if g % 2 == 0 else pG
                if (c, g) not in ((0, 0), (0, 1)):
                    prev = ("g2m", c, g - 2) if g >= 2 else ("min", c - 1, 30 + g)
                    t.wait_ge(VEs.h, dve_at[prev])
                ins = t.matmul(p[:, 0:128], bcm[:, T * g + CS * c:T * g + CS * (c + 1)],
                               ccm[:, T * g + CS * c:T * g + CS * (c + 1)], start=True, stop=True)
                PEs.inc(ins)
            for s_ in range(32):
                p = pF if s_ % 2 == 0 else pG
                prev = ("min", c, s_ - 2) if s_ >= 2 else ("g2m", c, 6 + s_)
                t.wait_ge(VEs.h, dve_at[prev])
                t.wait_ge(LX.h, lx_at[("cf", c, s_ // 8)])
                off = 512 * (s_ % 8)
                ins = t.matmul(p[:, 0:512], ones1[:], cflb[0:1, off:off + 512], start=True, stop=True)
                PEs.inc(ins)
            for hb in range(16):
                pyd = pA if hb % 2 == 0 else pC
                pyo = pB if hb % 2 == 0 else pD
                t.wait_ge(VEs.h, dve_at[("emul", c, 2 * hb + 1)])
                if c >= 1 or hb >= 2:
                    prev = ("mrg", c, hb - 2) if hb >= 2 else ("mrg", c - 1, 14 + hb)
                    t.wait_ge(VEs.h, dve_at[prev])
                if hb == 0 and c >= 1:
                    t.wait_ge(VEs.h, dve_at[("incu", c)])
                t.wait_ge(LX.h, lx_at[("xs", c)])
                for hh in range(8):
                    h = 8 * hb + hh
                    t.matmul(pyd[:, 64 * hh:64 * (hh + 1)], Eb[:, CS * h:CS * (h + 1)],
                             xsb[:, HD * h:HD * (h + 1)], start=True, stop=True)
                    ins = t.matmul(pyo[:, 64 * hh:64 * (hh + 1)],
                                   ccm[:, T * (h // 16) + CS * c:T * (h // 16) + CS * (c + 1)],
                                   incb[:, HD * h:HD * (h + 1)], start=True, stop=True)
                PEs.inc(ins)
    assert PEs.n == pe4 + len(pe_seq)

    @block.scalar
    def _(s):
        for c in range(NCH):
            for sl in range(32):
                s.wait_ge(VEs.h, dve_at[("min", c, sl)])
                if sl >= 2:
                    s.wait_ge(VEs.h, dve_at[("emul", c, sl - 2)])
                elif c >= 1:
                    s.wait_ge(VEs.h, dve_at[("emul", c - 1, 30 + sl)])
                ins = s.activation(t1e[sl % 2][:], t1b[sl % 2][:], AF.Exp)
                AEs.inc(ins)
    assert AEs.n == ae4 + 32 * NCH

    @block.gpsimd
    def _(g):
        for c in range(NCH):
            g.wait_ge(VEs.h, dve_at[("mrg", c, 15)])
            ins = g.dma_start(out=yf_d[CS * c:CS * (c + 1), :], in_=big1[:])
            SP.inc(ins, 16)

    if DEBUG:
        @block.gpsimd
        def _(g):
            g.wait_ge(SP.h, SP.n)
            g.dma_start(out=dbg["dbg_xcm"], in_=x_cm).then_inc(DBG.h, 16)
            g.dma_start(out=dbg["dbg_bcm"], in_=b_cm).then_inc(DBG.h, 16)
            g.dma_start(out=dbg["dbg_yf"], in_=yf_d).then_inc(DBG.h, 16)
            g.dma_start(out=dbg["dbg_st"], in_=st_d.rearrange("c p d -> (c p) d")).then_inc(DBG.h, 16)
            g.dma_start(out=dbg["dbg_gath"], in_=gath).then_inc(DBG.h, 16)
            g.dma_start(out=dbg["dbg_xn"], in_=xn_d).then_inc(DBG.h, 16)
        DBG.n += 96

    st3b.close()
    st3.close()
    st23.close()
    barrier()

    st4 = ExitStack()
    pass_st4 = None
    st4a = ExitStack()
    xn2 = sb(st4a, "xn2", [128, 32 * TH], BF16)
    wb4 = [sb(st4a, f"wb4{i}", [128, 4096], BF16) for i in range(3)]
    ytl = [sb(st4a, f"ytl{i}", [128, T], BF16) for i in range(3)]
    xre = [sb(st4a, f"xre{i}", [128, T], BF16) for i in range(3)]
    sgb = [sb(st4a, f"sgb{i}", [128, T], BF16) for i in range(2)]
    y2b = [sb(st4a, f"y2b{i}", [128, T], BF16) for i in range(2)]
    ysq = [sb(st4a, f"ysq{i}", [128, T], BF16) for i in range(2)]
    ynm = [sb(st4a, f"ynm{i}", [128, T], BF16) for i in range(2)]
    yob = [sb(st4a, f"yob{i}", [128, T], BF16) for i in range(2)]
    yGg = sb(st4a, "yGg", [128, 8 * T], BF16)
    rsb = sb(st4a, "rsb", [128, T], BF16)
    vr2 = sb(st4a, "vr2", [1, T], F32)
    s_dex = sb(st4a, "s_dex", [128, 64], BF16)
    s_nw = sb(st4a, "s_nw", [128, 64], F32)

    lx5, pe5, ae5, ve5, sp5, lw5 = LX.n, PEs.n, AEs.n, VEs.n, SP.n, LW.n

    pe_seq4 = []
    for gt in range(64):
        pe_seq4.append(("gmm", gt))
        pe_seq4.append(("vmm", gt))
        if gt % 8 == 7:
            pe_seq4.append(("rbc", gt // 8))
    pe_at4 = {tg: pe5 + k + 1 for k, tg in enumerate(pe_seq4)}

    ae_seq4 = []
    for gt in range(64):
        ae_seq4.append(("silu", gt))
        if gt % 8 == 7:
            gr = gt // 8
            ae_seq4.append(("vev", gr))
            ae_seq4.append(("rbce", gr))
            for q in range(8):
                ae_seq4.append(("nw", 8 * gr + q))
    ae_at4 = {tg: ae5 + k + 1 for k, tg in enumerate(ae_seq4)}

    ve_seq4 = []
    for gt in range(64):
        ve_seq4.append(("y2", gt))
        ve_seq4.append(("yG", gt))
        ve_seq4.append(("sq", gt))
        if gt % 8 == 7:
            for q in range(8):
                ve_seq4.append(("ynm", 8 * (gt // 8) + q))
    ve_at4 = {tg: ve5 + k + 1 for k, tg in enumerate(ve_seq4)}

    @block.sync
    def _(sy):
        sy.wait_ge(SP.h, sp5)
        ins = sy.dma_start(out=xn2[:], in_=xn_d)
        LX.inc(ins, 16)
        ins = sy.dma_start(out=s_nw[:], in_=nwi)
        LX.inc(ins, 16)
        for gt in range(64):
            if gt >= 3:
                sy.wait_ge(VEs.h, ve_at4[("y2", gt - 3)])
            ins = sy.dma_start(out=ytl[gt % 3][:], in_=yf_d[:, 128 * gt:128 * (gt + 1)], transpose=True)
            LX.inc(ins, 16)
            ins = sy.dma_start(out=xre[gt % 3][:], in_=x_cm[128 * gt:128 * (gt + 1), :])
            LX.inc(ins, 16)

    @block.gpsimd
    def _(g):
        g.wait_ge(SP.h, sp5)
        ins = g.dma_start(out=s_dex[:], in_=dexpi)
        LW.inc(ins, 16)
        for gt in range(64):
            if gt >= 3:
                g.wait_ge(PEs.h, pe_at4[("gmm", gt - 3)])
            ins = g.dma_start(out=wb4[gt % 3][:], in_=wp[gt])
            LW.inc(ins, 16)

    @block.tensor
    def _(t):
        for gt in range(64):
            gr = gt // 8
            t.wait_ge(LW.h, lw5 + 16 * (gt + 2))
            t.wait_ge(LX.h, lx5 + 32)
            if gt >= 2:
                t.wait_ge(AEs.h, ae_at4[("silu", gt - 2)])
            a = pA if gt % 2 == 0 else pC
            b = pB if gt % 2 == 0 else pD
            w = wb4[gt % 3]
            for k in range(32):
                s0, s1 = (k == 0), (k == 31)
                t.matmul(a[:], w[:, 128 * k:128 * (k + 1)], xn2[:, TH * k + 3:TH * k + 515], start=s0, stop=s1)
                ins = t.matmul(b[:], w[:, 128 * k:128 * (k + 1)], xn2[:, TH * k + 515:TH * k + 1027], start=s0, stop=s1)
            PEs.inc(ins)
            t.wait_ge(VEs.h, ve_at4[("sq", gt)])
            if gt % 8 == 0 and gr >= 1:
                t.wait_ge(AEs.h, ae_at4[("rbce", gr - 1)])
            t.matmul(pF[0:1, :], onescol_bf[:], ysq[gt % 2][:, 0:512], start=(gt % 8 == 0), stop=(gt % 8 == 7))
            ins = t.matmul(pG[0:1, :], onescol_bf[:], ysq[gt % 2][:, 512:1024], start=(gt % 8 == 0), stop=(gt % 8 == 7))
            PEs.inc(ins)
            if gt % 8 == 7:
                t.wait_ge(AEs.h, ae_at4[("vev", gr)])
                t.matmul(pF[:], ones1[:], vr2[0:1, 0:512], start=True, stop=True)
                ins = t.matmul(pG[:], ones1[:], vr2[0:1, 512:1024], start=True, stop=True)
                PEs.inc(ins)

    @block.scalar
    def _(s):
        for gt in range(64):
            gr = gt // 8
            s.wait_ge(PEs.h, pe_at4[("gmm", gt)])
            if gt >= 2:
                s.wait_ge(VEs.h, ve_at4[("yG", gt - 2)])
            a = pA if gt % 2 == 0 else pC
            b = pB if gt % 2 == 0 else pD
            s.activation(sgb[gt % 2][:, 0:512], a[:], AF.Silu)
            ins = s.activation(sgb[gt % 2][:, 512:1024], b[:], AF.Silu)
            AEs.inc(ins)
            if gt % 8 == 7:
                s.wait_ge(PEs.h, pe_at4[("vmm", gt)])
                s.copy(vr2[0:1, 0:512], pF[0:1, :])
                s.copy(vr2[0:1, 512:1024], pG[0:1, :])
                s.activation(vr2[:], vr2[:], AF.Ln, scale=1.0 / 1024.0, bias=EPS)
                ins = s.activation(vr2[:], vr2[:], AF.Exp, scale=-0.5)
                AEs.inc(ins)
                s.wait_ge(PEs.h, pe_at4[("rbc", gr)])
                s.copy(rsb[:, 0:512], pF[:])
                ins = s.copy(rsb[:, 512:1024], pG[:])
                AEs.inc(ins)
                for q in range(8):
                    gq = 8 * gr + q
                    s.wait_ge(VEs.h, ve_at4[("ynm", gq)])
                    ins = s.copy(lrow[:], lrow[:])
                    AEs.inc(ins)
                    ins = s.dma_start(out=yo_d[128 * gq:128 * (gq + 1), :], in_=yob[gq % 2][:])
                    SP.inc(ins, 16)
                if DEBUG:
                    for q in range(8):
                        gq = 8 * gr + q
                        ins = s.dma_start(out=dbg["dbg_yg"][128 * gq:128 * (gq + 1), :], in_=yGg[:, T * q:T * (q + 1)])
                        DBG.inc(ins, 16)
                    ins = s.dma_start(out=dbg["dbg_rs"][128 * gr:128 * (gr + 1), :], in_=rsb[:])
                    DBG.inc(ins, 16)
    assert AEs.n == ae5 + len(ae_seq4)

    @block.vector
    def _(v):
        for gt in range(64):
            gr = gt // 8
            v.wait_ge(LX.h, lx5 + 32 + 32 * (gt + 1))
            v.wait_ge(LW.h, lw5 + 16)
            ins = v.scalar_tensor_tensor(y2b[gt % 2][:], xre[gt % 3][:], s_dex[:, gt:gt + 1],
                                         ytl[gt % 3][:], ALU.mult, ALU.add)
            VEs.inc(ins)
            v.wait_ge(AEs.h, ae_at4[("silu", gt)])
            if gt >= 16:
                v.wait_ge(AEs.h, ae_at4[("nw", gt - 16)])
            ins = v.tensor_tensor(yGg[:, T * (gt % 8):T * (gt % 8 + 1)], y2b[gt % 2][:], sgb[gt % 2][:], ALU.mult)
            VEs.inc(ins)
            if gt >= 2:
                v.wait_ge(PEs.h, pe_at4[("vmm", gt - 2)])
            ins = v.tensor_tensor(ysq[gt % 2][:], yGg[:, T * (gt % 8):T * (gt % 8 + 1)],
                                  yGg[:, T * (gt % 8):T * (gt % 8 + 1)], ALU.mult)
            VEs.inc(ins)
            if gt % 8 == 7:
                v.wait_ge(AEs.h, ae_at4[("rbce", gr)])
                for q in range(8):
                    gq = 8 * gr + q
                    if gq >= 2:
                        v.wait_ge(SP.h, sp5 + 16 * (gq - 1))
                    ins = v.scalar_tensor_tensor(yob[gq % 2][:], yGg[:, T * q:T * (q + 1)],
                                                 s_nw[:, gq:gq + 1], rsb[:], ALU.mult, ALU.mult)
                    VEs.inc(ins)
    assert VEs.n == ve5 + len(ve_seq4)

    st4a.close()
    barrier()

    st4b = ExitStack()
    yh = sb(st4b, "yh", [128, 64 * 512], BF16)
    xnh = sb(st4b, "xnh", [128, 32 * 512], BF16)
    wob = [sb(st4b, f"wob{i}", [128, 8192], BF16) for i in range(2)]
    tb = [sb(st4b, f"tb{i}", [128, 4096], BF16) for i in range(2)]
    scb = [sb(st4b, f"scb{i}", [128, 4096], BF16) for i in range(2)]
    fb = [sb(st4b, f"fb{i}", [128, 4096], BF16) for i in range(2)]
    xrb = [sb(st4b, f"xrb{i}", [128, 512], F32) for i in range(2)]
    oat = [sb(st4b, f"oat{i}", [128, 512], F32) for i in range(4)]

    lx6, pe6, ae6, ve6, lw6, sp6, lwb6 = LX.n, PEs.n, AEs.n, VEs.n, LW.n, SP.n, LWB.n

    lx_xr = {}
    cnt = lx6
    lxh = []
    for th in range(2):
        cnt += 96 * 16
        lxh.append(cnt)
        for ot in range(32):
            cnt += 16
            lx_xr[(th, ot)] = cnt
    lxh = [lxh[0], lxh[1]]

    @block.sync
    def _(sy):
        sy.wait_ge(SP.h, sp6)
        for th in range(2):
            if th == 1:
                sy.wait_ge(PEs.h, pe6 + 32)
            for k in range(64):
                ins = sy.dma_start(out=yh[:, 512 * k:512 * (k + 1)],
                                   in_=yo_d[128 * k:128 * (k + 1), 512 * th:512 * (th + 1)])
                LX.inc(ins, 16)
            for k in range(32):
                ins = sy.dma_start(out=xnh[:, 512 * k:512 * (k + 1)],
                                   in_=xn_d[0:128, TH * k + 3 + 512 * th:TH * k + 3 + 512 * (th + 1)])
                LX.inc(ins, 16)
            for ot in range(32):
                i = 32 * th + ot
                if i >= 2:
                    sy.wait_ge(VEs.h, ve6 + 2 * (i - 1))
                ins = sy.dma_start(out=xrb[i % 2][:], in_=xt[128 * ot:128 * (ot + 1), 3 + 512 * th:3 + 512 * (th + 1)])
                LX.inc(ins, 16)

    @block.gpsimd
    def _(g):
        g.wait_ge(SP.h, sp6)
        for th in range(2):
            for ot in range(32):
                i = 32 * th + ot
                if i >= 2:
                    g.wait_ge(PEs.h, pe6 + i - 1)
                ins = g.dma_start(out=wob[i % 2][:], in_=wop[ot])
                LWB.inc(ins, 16)
                if i >= 2:
                    g.wait_ge(VEs.h, ve6 + 2 * (i - 2) + 1)
                ins = g.dma_start(out=tb[i % 2][:], in_=tern[ot])
                LW.inc(ins, 16)
                ins = g.dma_start(out=scb[i % 2][:], in_=scf[ot])
                LW.inc(ins, 16)

    @block.vector
    def _(v):
        for th in range(2):
            for ot in range(32):
                i = 32 * th + ot
                v.wait_ge(LW.h, lw6 + 32 * (i + 1))
                if i >= 2:
                    v.wait_ge(PEs.h, pe6 + i - 1)
                ins = v.tensor_tensor(fb[i % 2][:], tb[i % 2][:], scb[i % 2][:], ALU.mult)
                VEs.inc(ins)
                v.wait_ge(PEs.h, pe6 + i + 1)
                v.wait_ge(LX.h, lx_xr[(th, ot)])
                if i >= 4:
                    v.wait_ge(STO.h, 16 * (i - 3))
                p = pA if i % 2 == 0 else pB
                ins = v.tensor_tensor(oat[i % 4][:], p[:], xrb[i % 2][:], ALU.add)
                VEs.inc(ins)

    @block.tensor
    def _(t):
        for th in range(2):
            t.wait_ge(LX.h, lxh[th])
            for ot in range(32):
                i = 32 * th + ot
                t.wait_ge(LWB.h, lwb6 + 16 * (i + 1))
                t.wait_ge(VEs.h, ve6 + 2 * i + 1)
                if i >= 2:
                    t.wait_ge(VEs.h, ve6 + 2 * (i - 1))
                p = pA if i % 2 == 0 else pB
                w = wob[i % 2]
                f = fb[i % 2]
                for k in range(64):
                    t.matmul(p[:], w[:, 128 * k:128 * (k + 1)], yh[:, 512 * k:512 * (k + 1)],
                             start=(k == 0), stop=False)
                for k2 in range(32):
                    ins = t.matmul(p[:], f[:, 128 * k2:128 * (k2 + 1)], xnh[:, 512 * k2:512 * (k2 + 1)],
                                   start=False, stop=(k2 == 31))
                PEs.inc(ins)

    @block.scalar
    def _(s):
        for th in range(2):
            for ot in range(32):
                i = 32 * th + ot
                s.wait_ge(VEs.h, ve6 + 2 * (i + 1))
                ins = s.dma_start(out=outT[128 * ot:128 * (ot + 1), 512 * th:512 * (th + 1)], in_=oat[i % 4][:])
                STO.inc(ins, 16)
        s.wait_ge(STO.h, STO.n)

    if DEBUG:
        @block.gpsimd
        def _(g):
            g.wait_ge(STO.h, 1024)
            g.dma_start(out=dbg["dbg_yo"], in_=yo_d).then_inc(DBG.h, 16)
            g.wait_ge(DBG.h, DBG.n + 16)
        DBG.n += 16

    st4b.close()
    st4.close()
    return nc, st


def _prep(inputs):
    x = np.asarray(inputs["L0_attn_hidden_states"])[0]
    W_in = np.asarray(inputs["L0_attn_in_proj_w"])
    conv_w = np.asarray(inputs["L0_attn_conv_w"])
    conv_b = np.asarray(inputs["L0_attn_conv_b"])
    dt_bias = np.asarray(inputs["L0_attn_dt_bias"])
    A_log = np.asarray(inputs["L0_attn_A_log"])
    D = np.asarray(inputs["L0_attn_D"])
    norm_w = np.asarray(inputs["L0_attn_norm_w"])
    W_out = np.asarray(inputs["L0_attn_out_proj_w"])
    ternary = np.asarray(inputs["L0_ffn_ternary"])
    scales = np.asarray(inputs["L0_ffn_scales"])

    xT = np.ascontiguousarray(x.T.astype(np.float32))
    wp = np.ascontiguousarray(W_in.astype(np.float32).reshape(N_OT, 128, 32, 128).transpose(0, 3, 2, 1)).reshape(N_OT, 128, 4096)
    wopk = np.ascontiguousarray(W_out.astype(np.float32).reshape(32, 128, 64, 128).transpose(0, 3, 2, 1)).reshape(32, 128, 8192)
    ternk = np.ascontiguousarray(ternary.reshape(32, 128, 32, 128).transpose(0, 3, 2, 1)).reshape(32, 128, 4096)
    s2 = scales.astype(np.float32).reshape(4096, 32)
    s3 = np.ascontiguousarray(s2.reshape(32, 128, 32).transpose(0, 2, 1))
    scfk = np.ascontiguousarray(np.broadcast_to(s3[:, None, :, :], (32, 128, 32, 128)).reshape(32, 128, 4096))
    cwp = np.ascontiguousarray(conv_w.astype(np.float32).reshape(N_XBC, 128, 4).transpose(1, 0, 2)).reshape(128, N_XBC * 4)
    cbp = np.ascontiguousarray(conv_b.astype(np.float32).reshape(N_XBC, 128).transpose(1, 0))
    dtb = np.ascontiguousarray(dt_bias.astype(np.float32).reshape(128, 1))
    alg_ = np.ascontiguousarray(A_log.astype(np.float32).reshape(128, 1))
    dexp = np.empty((128, 64), np.float32)
    nwp_ = np.empty((128, 64), np.float32)
    ar = np.arange(128)
    for tt in range(64):
        ch = 128 * tt + ar
        dexp[:, tt] = D[ch // 64]
        nwp_[:, tt] = norm_w[ch]
    in_maps = []
    for g in range(NCORE):
        lo = g * T
        if g == 0:
            xs = np.concatenate([np.zeros((HID, 3), np.float32), xT[:, 0:T]], axis=1)
        else:
            xs = np.ascontiguousarray(xT[:, lo - 3:lo + T])
        m2v = (np.arange(8) < g).astype(np.float32).reshape(8, 1)
        d = dict(xt=xs, wp=wp, wop=wopk, tern=ternk, scf=scfk,
                 cwi=cwp, cbi=cbp, dtbi=dtb, alogi=alg_, dexpi=dexp, nwi=nwp_,
                 m2i=np.ascontiguousarray(m2v), m2ci=np.ascontiguousarray(1.0 - m2v),
                 m2hi=np.ascontiguousarray(np.broadcast_to(m2v.reshape(1, 8), (128, 8))))
        in_maps.append(d)
    return in_maps


def kernel(**inputs):
    in_maps = _prep(inputs)
    nc, st = build_nc()
    try:
        res = run_bass_kernel_spmd(nc, in_maps, core_ids=list(range(NCORE)))
    finally:
        st.close()
    outs = [np.asarray(res.results[g]["outT"]) for g in range(NCORE)]
    outT = np.concatenate(outs, axis=1)
    out = np.ascontiguousarray(outT.T).reshape(1, 8192, HID).astype(np.float32)
    if DEBUG:
        kernel._last = res
    return out


# version 10
# speedup vs baseline: 1.0069x; 1.0058x over previous
import numpy as np
import ml_dtypes
from contextlib import ExitStack

import concourse.bass as bass
import concourse.mybir as mybir
from concourse.bass_utils import run_bass_kernel_spmd

F32 = mybir.dt.float32
BF16 = mybir.dt.bfloat16
I8 = mybir.dt.int8
AF = mybir.ActivationFunctionType
ALU = mybir.AluOpType

HID = 4096
T = 1024
TH = T + 3
NH = 128
HD = 64
DIN = 8192
G = 8
NST = 128
CS = 128
NCH = T // CS
N_OT = 145
XBC0 = 64
N_XBC = 80
NCORE = 8
EPS = 1e-5

DEBUG = False
CONSTS = (1e-5, -0.5, -1.0, 1.0, 1.0 / 1024.0)


def bv(ap2d, p, k, reps):
    return ap2d.unsqueeze(2).broadcast_to((p, k, reps))


def build_nc():
    nc = bass.Bass("TRN2", target_bir_lowering=False)

    xt = nc.dram_tensor("xt", [HID, TH], F32, kind="ExternalInput").ap()
    wp = nc.dram_tensor("wp", [N_OT, 128, 4096], F32, kind="ExternalInput").ap()
    wop = nc.dram_tensor("wop", [32, 128, 8192], F32, kind="ExternalInput").ap()
    tern = nc.dram_tensor("tern", [32, 128, 4096], I8, kind="ExternalInput").ap()
    scf = nc.dram_tensor("scf", [32, 128, 4096], F32, kind="ExternalInput").ap()
    cwi = nc.dram_tensor("cwi", [128, N_XBC * 4], F32, kind="ExternalInput").ap()
    cbi = nc.dram_tensor("cbi", [128, N_XBC], F32, kind="ExternalInput").ap()
    dtbi = nc.dram_tensor("dtbi", [128, 1], F32, kind="ExternalInput").ap()
    alogi = nc.dram_tensor("alogi", [128, 1], F32, kind="ExternalInput").ap()
    dexpi = nc.dram_tensor("dexpi", [128, 64], F32, kind="ExternalInput").ap()
    nwi = nc.dram_tensor("nwi", [128, 64], F32, kind="ExternalInput").ap()
    m2i = nc.dram_tensor("m2i", [8, 1], F32, kind="ExternalInput").ap()
    m2ci = nc.dram_tensor("m2ci", [8, 1], F32, kind="ExternalInput").ap()
    m2hi = nc.dram_tensor("m2hi", [128, 8], F32, kind="ExternalInput").ap()

    outT = nc.dram_tensor("outT", [HID, T], F32, kind="ExternalOutput").ap()

    x_cm = nc.dram_tensor("x_cm", [DIN, T], BF16).ap()
    b_cm = nc.dram_tensor("b_cm", [G * NST, T], BF16).ap()
    xn_d = nc.dram_tensor("xn_d", [128, 32 * TH], BF16).ap()
    xs_d = nc.dram_tensor("xs_d", [NCH, 128, DIN], BF16).ap()
    st_d = nc.dram_tensor("st_d", [NCH, 128, DIN], BF16).ap()
    yf_d = nc.dram_tensor("yf_d", [T, DIN], BF16).ap()
    yo_d = nc.dram_tensor("yo_d", [DIN, T], BF16).ap()
    cfl_d = nc.dram_tensor("cfl_d", [NCH, CS * NH], F32).ap()
    contrib = nc.dram_tensor("contrib", [128, DIN], BF16).ap()
    gath = nc.dram_tensor("gath", [NCORE * 128, DIN], BF16).ap()
    alc = nc.dram_tensor("alc", [1, 128], F32).ap()
    wob_d = nc.dram_tensor("wob_d", [32, 128, 8192], BF16).ap()
    alg = nc.dram_tensor("alg", [NCORE, 128], F32).ap()
    dbg = {}
    if DEBUG:
        for nm, shape, dty in [("dbg_xcm", [DIN, T], BF16), ("dbg_bcm", [G * NST, T], BF16),
                               ("dbg_yf", [T, DIN], BF16), ("dbg_yo", [DIN, T], BF16),
                               ("dbg_st", [NCH * 128, DIN], BF16), ("dbg_cum", [128, T], F32),
                               ("dbg_gath", [NCORE * 128, DIN], BF16), ("dbg_xn", [128, 32 * TH], BF16),
                               ("dbg_dt", [128, T], F32), ("dbg_yg", [DIN, T], BF16),
                               ("dbg_rs", [8 * 128, T], BF16)]:
            dbg[nm] = nc.dram_tensor(nm, shape, dty, kind="ExternalOutput").ap()

    st = ExitStack()

    def sb(stk, name, shape, dtype):
        return stk.enter_context(nc.sbuf_tensor(name, shape, dtype))

    ones1 = sb(st, "ones1", [1, 128], F32)
    onescol = sb(st, "onescol", [128, 1], F32)
    onescol_bf = sb(st, "onescol_bf", [128, 1], BF16)
    eye = sb(st, "eye", [128, 128], F32)
    umask = sb(st, "umask", [128, 128], F32)
    scr_a = sb(st, "scr_a", [128, 128], F32)
    scr_b = sb(st, "scr_b", [128, 128], F32)
    s_dtb = sb(st, "s_dtb", [128, 1], F32)
    s_acol = sb(st, "s_acol", [128, 1], F32)
    alsum = sb(st, "alsum", [1, 128], F32)
    lrow = sb(st, "lrow", [1, 128], F32)
    s_m2 = sb(st, "s_m2", [8, 1], F32)
    s_m2c = sb(st, "s_m2c", [8, 1], F32)
    s_m2h = sb(st, "s_m2h", [128, 8], F32)
    s_ag = sb(st, "s_ag", [8, 128], F32)
    a_hm = sb(st, "a_hm", [128, 8], F32)
    whm = sb(st, "whm", [128, 8], F32)
    wexp = [sb(st, f"wexp{i}", [128, 128], BF16) for i in range(2)]
    cst = sb(st, "cst", [128, 8], F32)
    dmy = sb(st, "dmy", [1, 8], F32)

    st23 = ExitStack()
    cumT = sb(st23, "cumT", [128, T], F32)
    dtT_bf = sb(st23, "dtT_bf", [128, T], BF16)
    sd_f = sb(st23, "sd_f", [128, T], F32)
    dc_bf = sb(st23, "dc_bf", [128, T], BF16)
    al_bf = sb(st23, "al_bf", [128, T], BF16)
    bcm = sb(st23, "bcm", [128, G * T], BF16)
    ccm = sb(st23, "ccm", [128, G * T], BF16)

    pA = st.enter_context(nc.psum_tensor("pA", [128, 512], F32))
    pB = st.enter_context(nc.psum_tensor("pB", [128, 512], F32))
    pC = st.enter_context(nc.psum_tensor("pC", [128, 512], F32))
    pD = st.enter_context(nc.psum_tensor("pD", [128, 512], F32))
    pE = st.enter_context(nc.psum_tensor("pE", [128, 512], F32))
    pF = st.enter_context(nc.psum_tensor("pF", [128, 512], F32))
    pG = st.enter_context(nc.psum_tensor("pG", [128, 512], F32))
    pH = st.enter_context(nc.psum_tensor("pH", [128, 512], F32))

    class S:
        def __init__(self, name):
            self.h = st.enter_context(nc.semaphore(name))
            self.n = 0

        def inc(self, instr, k=1):
            instr.then_inc(self.h, k)
            self.n += k
            return self.n

    LW = S("LW")
    LX = S("LX")
    SP = S("SP")
    CC = S("CC")
    PEs = S("PEs")
    VEs = S("VEs")
    AEs = S("AEs")
    GPs = S("GPs")
    BAR = S("BAR")
    STO = S("STO")
    DBG = S("DBG")
    WCL = S("WCL")
    WCS = S("WCS")
    LWB = S("LWB")

    block = st.enter_context(nc.Block())
    barn = [0]

    def barrier():
        barn[0] += 5
        tgt = barn[0]
        for reg in (block.tensor, block.vector, block.scalar, block.gpsimd, block.sync):
            @reg
            def _(e):
                e.sem_inc(BAR.h, 1)
                e.wait_ge(BAR.h, tgt)
        BAR.n = tgt

    @block.gpsimd
    def _(g):
        g.iota(scr_a[:], [[0, 128]], channel_multiplier=1, allow_small_or_imprecise_dtypes=True)
        g.iota(scr_b[:], [[1, 128]], channel_multiplier=0, allow_small_or_imprecise_dtypes=True)
        g.memset(ones1[:], 1.0)
        g.memset(onescol[:], 1.0)
        g.memset(onescol_bf[:], 1.0)
        g.memset(alsum[:], 0.0)
        ins = g.memset(whm[:], 0.0)
        GPs.inc(ins)
        g.dma_start(out=s_dtb[:], in_=dtbi).then_inc(LW.h, 16)
        g.dma_start(out=s_acol[:], in_=alogi).then_inc(LW.h, 16)
        g.dma_start(out=s_m2[:], in_=m2i).then_inc(LW.h, 16)
        g.dma_start(out=s_m2c[:], in_=m2ci).then_inc(LW.h, 16)
        g.dma_start(out=s_m2h[:], in_=m2hi).then_inc(LW.h, 16)
        for ci, cv in enumerate(CONSTS):
            ins = g.memset(cst[:, ci:ci + 1], cv)
        GPs.inc(ins)
    LW.n += 80
    for ci, cv in enumerate(CONSTS):
        nc.const_aps.register(F32, cv, cst[:, ci:ci + 1])

    @block.vector
    def _(v):
        v.wait_ge(GPs.h, GPs.n)
        v.tensor_tensor(eye[:], scr_a[:], scr_b[:], ALU.is_equal)
        ins = v.tensor_tensor(umask[:], scr_a[:], scr_b[:], ALU.is_le)
        VEs.inc(ins)

    @block.scalar
    def _(s):
        s.wait_ge(LW.h, 80)
        s.wait_ge(GPs.h, GPs.n)
        s.activation(s_acol[:], s_acol[:], AF.Exp)
        ins = s.mul(s_acol[:], s_acol[:], -1.0)
        AEs.inc(ins)

    barrier()

    st12 = ExitStack()
    xn_cm = sb(st12, "xn_cm", [128, 32 * TH], BF16)
    st1 = ExitStack()
    xbuf = [sb(st1, f"xbuf{i}", [128, TH], F32) for i in range(2)]
    sqb = [sb(st1, f"sqb{i}", [128, TH], BF16) for i in range(2)]
    xbb = [sb(st1, f"xbb{i}", [128, TH], BF16) for i in range(2)]
    mrow = sb(st1, "mrow", [1, TH], F32)
    vrow = sb(st1, "vrow", [1, TH], F32)
    mbc = sb(st1, "mbc", [128, TH], F32)
    rbc = sb(st1, "rbc", [128, TH], F32)
    tmp1 = sb(st1, "tmp1", [128, TH], F32)

    lx0, ae0, pe0, ve0, lw0 = LX.n, AEs.n, PEs.n, VEs.n, LW.n
    ld_tile1 = {}

    @block.sync
    def _(sy):
        for i in range(0, 32, 2):
            if i >= 2:
                sy.wait_ge(PEs.h, pe0 + i - 1)
                sy.wait_ge(AEs.h, ae0 + i - 1)
            ins = sy.dma_start(out=xbuf[i % 2][:], in_=xt[128 * i:128 * (i + 1), :])
            LX.inc(ins, 16)
            ld_tile1[i] = (LX, LX.n)

    @block.gpsimd
    def _(g):
        for i in range(1, 32, 2):
            if i >= 2:
                g.wait_ge(PEs.h, pe0 + i - 1)
                g.wait_ge(AEs.h, ae0 + i - 1)
            ins = g.dma_start(out=xbuf[i % 2][:], in_=xt[128 * i:128 * (i + 1), :])
            LW.inc(ins, 16)
            ld_tile1[i] = (LW, LW.n)

    @block.scalar
    def _(s):
        for i in range(32):
            sm, val = ld_tile1[i]
            s.wait_ge(sm.h, val)
            if i >= 2:
                s.wait_ge(PEs.h, pe0 + i - 1)
            s.activation(sqb[i % 2][:], xbuf[i % 2][:], AF.Square)
            ins = s.copy(xbb[i % 2][:], xbuf[i % 2][:])
            AEs.inc(ins)

    @block.tensor
    def _(t):
        for i in range(32):
            s0, s1 = (i == 0), (i == 31)
            sm, val = ld_tile1[i]
            t.wait_ge(sm.h, val)
            t.wait_ge(AEs.h, ae0 + i + 1)
            t.matmul(pA[0:1, :], onescol_bf[:], xbb[i % 2][:, 0:512], start=s0, stop=s1)
            t.matmul(pB[0:1, :], onescol_bf[:], xbb[i % 2][:, 512:1024], start=s0, stop=s1)
            t.matmul(pH[0:1, 0:3], onescol_bf[:], xbb[i % 2][:, 1024:1027], start=s0, stop=s1)
            t.matmul(pC[0:1, :], onescol_bf[:], sqb[i % 2][:, 0:512], start=s0, stop=s1)
            t.matmul(pD[0:1, :], onescol_bf[:], sqb[i % 2][:, 512:1024], start=s0, stop=s1)
            ins = t.matmul(pE[0:1, 0:3], onescol_bf[:], sqb[i % 2][:, 1024:1027], start=s0, stop=s1)
            PEs.inc(ins)

    @block.scalar
    def _(s):
        s.wait_ge(PEs.h, pe0 + 32)
        s.copy(mrow[0:1, 0:512], pA[0:1, :])
        s.copy(mrow[0:1, 512:1024], pB[0:1, :])
        s.copy(mrow[0:1, 1024:1027], pH[0:1, 0:3])
        s.copy(vrow[0:1, 0:512], pC[0:1, :])
        s.copy(vrow[0:1, 512:1024], pD[0:1, :])
        ins = s.copy(vrow[0:1, 1024:1027], pE[0:1, 0:3])
        AEs.inc(ins)

    @block.vector
    def _(v):
        v.wait_ge(AEs.h, ae0 + 33)
        v.tensor_scalar(mrow[:], mrow[:], 1.0 / HID, None, ALU.mult)
        v.tensor_scalar(vrow[:], vrow[:], 1.0 / HID, None, ALU.mult)
        v.tensor_tensor(tmp1[0:1, :], mrow[:], mrow[:], ALU.mult)
        ins = v.tensor_tensor(vrow[:], vrow[:], tmp1[0:1, :], ALU.subtract)
        VEs.inc(ins)

    @block.scalar
    def _(s):
        s.wait_ge(VEs.h, ve0 + 1)
        s.activation(vrow[:], vrow[:], AF.Ln, bias=EPS)
        ins = s.activation(vrow[:], vrow[:], AF.Exp, scale=-0.5)
        AEs.inc(ins)

    @block.tensor
    def _(t):
        t.wait_ge(AEs.h, ae0 + 34)
        t.matmul(pA[:], ones1[:], mrow[0:1, 0:512], start=True, stop=True)
        t.matmul(pB[:], ones1[:], mrow[0:1, 512:1024], start=True, stop=True)
        t.matmul(pH[:, 0:3], ones1[:], mrow[0:1, 1024:1027], start=True, stop=True)
        t.matmul(pC[:], ones1[:], vrow[0:1, 0:512], start=True, stop=True)
        t.matmul(pD[:], ones1[:], vrow[0:1, 512:1024], start=True, stop=True)
        ins = t.matmul(pE[:, 0:3], ones1[:], vrow[0:1, 1024:1027], start=True, stop=True)
        PEs.inc(ins)

    @block.scalar
    def _(s):
        s.wait_ge(PEs.h, pe0 + 33)
        s.copy(mbc[:, 0:512], pA[:])
        s.copy(mbc[:, 512:1024], pB[:])
        s.copy(mbc[:, 1024:1027], pH[:, 0:3])
        s.copy(rbc[:, 0:512], pC[:])
        s.copy(rbc[:, 512:1024], pD[:])
        ins = s.copy(rbc[:, 1024:1027], pE[:, 0:3])
        AEs.inc(ins)

    lx1, ve1 = LX.n, VEs.n
    ld_tile2 = {}

    @block.sync
    def _(sy):
        for i in range(0, 32, 2):
            if i >= 2:
                sy.wait_ge(VEs.h, ve1 + i - 1)
            else:
                sy.wait_ge(PEs.h, pe0 + 32)
                sy.wait_ge(AEs.h, ae0 + 32)
            ins = sy.dma_start(out=xbuf[i % 2][:], in_=xt[128 * i:128 * (i + 1), :])
            LX.inc(ins, 16)
            ld_tile2[i] = (LX, LX.n)

    @block.gpsimd
    def _(g):
        for i in range(1, 32, 2):
            if i >= 2:
                g.wait_ge(VEs.h, ve1 + i - 1)
            else:
                g.wait_ge(PEs.h, pe0 + 32)
                g.wait_ge(AEs.h, ae0 + 32)
            ins = g.dma_start(out=xbuf[i % 2][:], in_=xt[128 * i:128 * (i + 1), :])
            LW.inc(ins, 16)
            ld_tile2[i] = (LW, LW.n)

    @block.vector
    def _(v):
        v.wait_ge(AEs.h, ae0 + 35)
        for i in range(32):
            sm, val = ld_tile2[i]
            v.wait_ge(sm.h, val)
            v.tensor_tensor(tmp1[:], xbuf[i % 2][:], mbc[:], ALU.subtract)
            ins = v.tensor_tensor(xn_cm[:, TH * i:TH * (i + 1)], tmp1[:], rbc[:], ALU.mult)
            VEs.inc(ins)

    @block.gpsimd
    def _(g):
        g.wait_ge(VEs.h, ve1 + 32)
        ins = g.dma_start(out=xn_d, in_=xn_cm[:])
        SP.inc(ins, 16)

    st1.close()
    barrier()

    st2 = ExitStack()
    wbuf = [sb(st2, f"wbuf{i}", [128, 4096], BF16) for i in range(3)]
    zx = [sb(st2, f"zx{i}", [128, TH], F32) for i in range(2)]
    cacc = [sb(st2, f"cacc{i}", [128, T], F32) for i in range(2)]
    xout = [sb(st2, f"xout{i}", [128, T], BF16) for i in range(2)]
    s_cw = sb(st2, "s_cw", [128, N_XBC * 4], F32)
    s_cb = sb(st2, "s_cb", [128, N_XBC], F32)
    dt_f = sb(st2, "dt_f", [128, T], F32)
    adt = sb(st2, "adt", [128, T], F32)
    cum_hm = sb(st2, "cum_hm", [128, T], F32)
    tdt = sb(st2, "tdt", [128, T], F32)
    z128 = sb(st2, "z128", [128, 128], F32)

    lw2, pe2, ae2, ve2, sp2 = LW.n, PEs.n, AEs.n, VEs.n, SP.n
    NT2 = N_XBC + 1

    act_seq = []
    for i in range(NT2):
        act_seq.append(("evac", i))
        act_seq.append(("c0", i))
        if i >= 1 and i - 1 < N_XBC:
            act_seq.append(("silu", i - 1))
    act_at = {tg: ae2 + k + 1 for k, tg in enumerate(act_seq)}

    @block.gpsimd
    def _(g):
        g.memset(z128[:], 0.0)
        g.dma_start(out=s_cw[:], in_=cwi).then_inc(LW.h, 16)
        g.dma_start(out=s_cb[:], in_=cbi).then_inc(LW.h, 16)
        for i in range(NT2):
            if i >= 3:
                g.wait_ge(PEs.h, pe2 + i - 2)
            ins = g.dma_start(out=wbuf[i % 3][:], in_=wp[XBC0 + i])
            LW.inc(ins, 16)
    LW.n += 32
    lw2b = lw2 + 32

    @block.sync
    def _(sy):
        for j in range(64):
            sy.wait_ge(AEs.h, act_at[("silu", j)])
            ins = sy.dma_start(out=x_cm[128 * j:128 * (j + 1), :], in_=xout[j % 2][:])
            SP.inc(ins, 16)
        sy.wait_ge(AEs.h, act_at[("silu", 71)])
        for gi in range(8):
            ins = sy.dma_start(out=b_cm[NST * gi:NST * (gi + 1), :], in_=bcm[:, T * gi:T * (gi + 1)])
            SP.inc(ins, 16)

    @block.tensor
    def _(t):
        psets = [(pA, pB), (pC, pD)]
        halos = [pE, pH]
        for i in range(NT2):
            t.wait_ge(LW.h, lw2b + 16 * (i + 1))
            if i >= 2:
                t.wait_ge(AEs.h, act_at[("evac", i - 2)])
            a, b = psets[i % 2]
            hp = halos[i % 2]
            w = wbuf[i % 3]
            if i < N_XBC:
                for k in range(32):
                    s0, s1 = (k == 0), (k == 31)
                    t.matmul(a[:], w[:, 128 * k:128 * (k + 1)], xn_cm[:, TH * k:TH * k + 512], start=s0, stop=s1)
                    t.matmul(b[:], w[:, 128 * k:128 * (k + 1)], xn_cm[:, TH * k + 512:TH * k + 1024], start=s0, stop=s1)
                    ins = t.matmul(hp[:, 0:3], w[:, 128 * k:128 * (k + 1)],
                                   xn_cm[:, TH * k + 1024:TH * k + 1027], start=s0, stop=s1)
            else:
                for k in range(32):
                    s0, s1 = (k == 0), (k == 31)
                    t.matmul(a[:], w[:, 128 * k:128 * (k + 1)], xn_cm[:, TH * k + 3:TH * k + 515], start=s0, stop=s1)
                    ins = t.matmul(b[:], w[:, 128 * k:128 * (k + 1)], xn_cm[:, TH * k + 515:TH * k + 1027], start=s0, stop=s1)
            PEs.inc(ins)

    @block.scalar
    def _(s):
        psets = [(pA, pB), (pC, pD)]
        halos = [pE, pH]
        for i in range(NT2):
            s.wait_ge(PEs.h, pe2 + i + 1)
            if i >= 2:
                s.wait_ge(VEs.h, ve2 + i - 1)
            a, b = psets[i % 2]
            hp = halos[i % 2]
            z = zx[i % 2]
            s.copy(z[:, 0:512], a[:])
            s.copy(z[:, 512:1024], b[:])
            if i < N_XBC:
                ins = s.copy(z[:, 1024:1027], hp[:, 0:3])
                AEs.inc(ins)
                ins = s.activation(cacc[i % 2][:], z[:, 0:T], AF.Copy, scale=s_cw[:, 4 * i:4 * i + 1])
                AEs.inc(ins)
            else:
                ins = s.copy(z[:, 1024:1027], z[:, 1024:1027])
                AEs.inc(ins)
                ins = s.activation(cacc[i % 2][:], z[:, 0:T], AF.Copy)
                AEs.inc(ins)
            if i >= 1 and i - 1 < N_XBC:
                j = i - 1
                s.wait_ge(VEs.h, ve2 + j + 1)
                if j < 64:
                    s.wait_ge(SP.h, sp2 + 16 * max(0, j - 1))
                    dst = xout[j % 2][:]
                elif j < 72:
                    dst = bcm[:, T * (j - 64):T * (j - 63)]
                else:
                    dst = ccm[:, T * (j - 72):T * (j - 71)]
                ins = s.activation(dst, cacc[j % 2][:], AF.Silu, bias=s_cb[:, j:j + 1])
                AEs.inc(ins)
    assert AEs.n == ae2 + len(act_seq), (AEs.n, ae2, len(act_seq))

    @block.vector
    def _(v):
        for i in range(N_XBC):
            v.wait_ge(AEs.h, act_at[("c0", i)])
            z = zx[i % 2]
            c = cacc[i % 2]
            v.scalar_tensor_tensor(c[:], z[:, 1:1 + T], s_cw[:, 4 * i + 1:4 * i + 2], c[:], ALU.mult, ALU.add)
            v.scalar_tensor_tensor(c[:], z[:, 2:2 + T], s_cw[:, 4 * i + 2:4 * i + 3], c[:], ALU.mult, ALU.add)
            ins = v.scalar_tensor_tensor(c[:], z[:, 3:3 + T], s_cw[:, 4 * i + 3:4 * i + 4], c[:], ALU.mult, ALU.add)
            VEs.inc(ins)
        v.wait_ge(AEs.h, act_at[("c0", N_XBC)])
        u = cacc[N_XBC % 2]
        v.tensor_scalar(tdt[:], u[:], s_dtb[:], None, ALU.add)
        ins = v.tensor_scalar(dt_f[:], tdt[:], 0.0, None, ALU.max)
        VEs.inc(ins)

    ve2b = ve2 + N_XBC + 1

    @block.scalar
    def _(s):
        s.wait_ge(VEs.h, ve2b)
        s.activation(tdt[:], tdt[:], AF.Abs)
        s.activation(tdt[:], tdt[:], AF.Exp, scale=-1.0)
        ins = s.activation(tdt[:], tdt[:], AF.Ln, bias=1.0)
        AEs.inc(ins)
    ae2b = AEs.n

    @block.vector
    def _(v):
        v.wait_ge(AEs.h, ae2b)
        v.tensor_tensor(dt_f[:], dt_f[:], tdt[:], ALU.add)
        ins = v.tensor_scalar(adt[:], dt_f[:], s_acol[:], None, ALU.mult)
        VEs.inc(ins)
        for c in range(NCH):
            ins = v.tensor_tensor_scan(cum_hm[:, CS * c:CS * (c + 1)], adt[:, CS * c:CS * (c + 1)],
                                       z128[:], 0.0, ALU.add, ALU.add)
            VEs.inc(ins)
    ve2c = VEs.n

    peX, aeX, veX = PEs.n, AEs.n, VEs.n
    for c in range(NCH):
        @block.tensor
        def _(t, c=c):
            t.wait_ge(VEs.h, ve2c)
            if c >= 1:
                t.wait_ge(AEs.h, aeX + 3 * c)
            t.transpose(pF[:, 0:128], cum_hm[:, CS * c:CS * (c + 1)], eye[:])
            ins = t.transpose(pG[:, 0:128], dt_f[:, CS * c:CS * (c + 1)], eye[:])
            PEs.inc(ins)
            ins = t.transpose(pE[0:1, 0:128], cum_hm[:, CS * c + 127:CS * c + 128], eye[:])
            PEs.inc(ins)

        @block.scalar
        def _(s, c=c):
            s.wait_ge(PEs.h, peX + 3 * c + 2)
            s.copy(cumT[:, CS * c:CS * (c + 1)], pF[:, 0:128])
            ins = s.copy(dtT_bf[:, CS * c:CS * (c + 1)], pG[:, 0:128])
            AEs.inc(ins)
            ins = s.copy(lrow[:], pE[0:1, 0:128])
            AEs.inc(ins)

        @block.tensor
        def _(t, c=c):
            t.wait_ge(AEs.h, aeX + 3 * c + 2)
            ins = t.matmul(pF[:, 0:128], ones1[:], lrow[:], start=True, stop=True)
            PEs.inc(ins)

        @block.vector
        def _(v, c=c):
            v.wait_ge(PEs.h, peX + 3 * (c + 1))
            v.tensor_tensor(scr_a[:], pF[:, 0:128], cumT[:, CS * c:CS * (c + 1)], ALU.subtract)
            ins = v.tensor_tensor(alsum[:], alsum[:], lrow[:], ALU.add)
            VEs.inc(ins)

        @block.scalar
        def _(s, c=c):
            s.wait_ge(VEs.h, veX + c + 1)
            s.activation(dc_bf[:, CS * c:CS * (c + 1)], scr_a[:], AF.Exp)
            s.activation(al_bf[:, CS * c:CS * (c + 1)], pF[:, 0:128], AF.Exp)
            ins = s.activation(sd_f[:, CS * c:CS * (c + 1)], cumT[:, CS * c:CS * (c + 1)], AF.Exp)
            AEs.inc(ins)

    @block.gpsimd
    def _(g):
        g.wait_ge(VEs.h, ve2c)
        for c in range(NCH):
            ins = g.dma_start(out=cfl_d[c].rearrange("(h i) -> h i", h=128), in_=cum_hm[:, CS * c:CS * (c + 1)])
            SP.inc(ins, 16)

    @block.scalar
    def _(s):
        s.wait_ge(VEs.h, veX + NCH)
        ins = s.activation(alsum[:], alsum[:], AF.Exp)
        AEs.inc(ins)
    aeX2 = AEs.n

    @block.gpsimd
    def _(g):
        g.wait_ge(AEs.h, aeX2)
        ins = g.dma_start(out=alc, in_=alsum[:])
        SP.inc(ins, 16)

    if DEBUG:
        @block.gpsimd
        def _(g):
            g.wait_ge(AEs.h, aeX2)
            g.dma_start(out=dbg["dbg_cum"], in_=cum_hm[:]).then_inc(DBG.h, 16)
            g.dma_start(out=dbg["dbg_dt"], in_=dt_f[:]).then_inc(DBG.h, 16)
        DBG.n += 32

    st2.close()
    st12.close()
    barrier()

    st3 = ExitStack()
    xsb = sb(st3, "xsb", [128, DIN], BF16)
    incb = sb(st3, "incb", [128, DIN], BF16)
    sreb = sb(st3, "sreb", [128, DIN], BF16)
    big1 = sb(st3, "big1", [128, DIN], BF16)
    st3a = ExitStack()
    xTb = sb(st3a, "xTb", [128, DIN], BF16)
    xs2b = sb(st3a, "xs2b", [128, DIN], BF16)
    btb = sb(st3a, "btb", [128, G * NST], BF16)

    lx3, pe3, ae3, ve3, sp3 = LX.n, PEs.n, AEs.n, VEs.n, SP.n

    @block.sync
    def _(sy):
        sy.wait_ge(SP.h, sp3)
        for c in range(NCH):
            if c >= 1:
                sy.wait_ge(VEs.h, ve3 + 1 + 4 * c)
                sy.wait_ge(PEs.h, pe3 + 16 * c)
            ins = sy.dma_start(out=xTb[:], in_=x_cm[:, CS * c:CS * (c + 1)], transpose=True)
            LX.inc(ins, 16)
            ins = sy.dma_start(out=btb[:], in_=b_cm[:, CS * c:CS * (c + 1)], transpose=True)
            LX.inc(ins, 16)

    @block.vector
    def _(v):
        ins = v.memset(incb[:], 0.0)
        VEs.inc(ins)
        for c in range(NCH):
            v.wait_ge(LX.h, lx3 + 32 * (c + 1))
            if c >= 1:
                v.wait_ge(SP.h, sp3 + 32 * c)
            xv = xsb[:].rearrange("p (h d) -> p h d", h=NH)
            ins = v.tensor_tensor(xv, xTb[:].rearrange("p (h d) -> p h d", h=NH),
                                  bv(dtT_bf[:, CS * c:CS * (c + 1)], 128, NH, HD), ALU.mult)
            VEs.inc(ins)
            ins = v.tensor_tensor(xs2b[:].rearrange("p (h d) -> p h d", h=NH), xv,
                                  bv(dc_bf[:, CS * c:CS * (c + 1)], 128, NH, HD), ALU.mult)
            VEs.inc(ins)
            v.wait_ge(AEs.h, ae3 + 16 * (c + 1))
            ins = v.tensor_tensor(xTb[:].rearrange("p (h d) -> p h d", h=NH),
                                  incb[:].rearrange("p (h d) -> p h d", h=NH),
                                  bv(al_bf[:, CS * c:CS * (c + 1)], 128, NH, HD), ALU.mult)
            VEs.inc(ins)
            ins = v.tensor_tensor(incb[:], xTb[:], big1[:], ALU.add)
            VEs.inc(ins)

    @block.tensor
    def _(t):
        for c in range(NCH):
            t.wait_ge(VEs.h, ve3 + 1 + 4 * c + 2)
            for hb in range(16):
                p = pA if hb % 2 == 0 else pB
                m = 16 * c + hb
                if m >= 2:
                    t.wait_ge(AEs.h, ae3 + m - 1)
                for hh in range(8):
                    h = 8 * hb + hh
                    ins = t.matmul(p[:, 64 * hh:64 * (hh + 1)], btb[:, NST * (h // 16):NST * (h // 16 + 1)],
                                   xs2b[:, HD * h:HD * (h + 1)], start=True, stop=True)
                PEs.inc(ins)

    @block.scalar
    def _(s):
        for c in range(NCH):
            for hb in range(16):
                p = pA if hb % 2 == 0 else pB
                s.wait_ge(PEs.h, pe3 + 16 * c + hb + 1)
                if hb == 0 and c >= 1:
                    s.wait_ge(VEs.h, ve3 + 1 + 4 * c)
                    s.wait_ge(SP.h, sp3 + 32 * c)
                ins = s.copy(big1[:, 512 * hb:512 * (hb + 1)], p[:])
                AEs.inc(ins)

    @block.gpsimd
    def _(g):
        for c in range(NCH):
            g.wait_ge(VEs.h, ve3 + 1 + 4 * c + 1)
            ins = g.dma_start(out=xs_d[c], in_=xsb[:])
            SP.inc(ins, 16)
            g.wait_ge(AEs.h, ae3 + 16 * (c + 1))
            ins = g.dma_start(out=st_d[c], in_=big1[:])
            SP.inc(ins, 16)
        g.wait_ge(VEs.h, ve3 + 1 + 4 * NCH)
        ins = g.dma_start(out=contrib, in_=incb[:])
        SP.inc(ins, 16)
        g.wait_ge(SP.h, SP.n)
        ins = g.collective_compute("AllGather", ALU.bypass, replica_groups=[list(range(NCORE))],
                                   ins=[contrib.opt()], outs=[gath.opt()])
        CC.inc(ins)
        ins = g.collective_compute("AllGather", ALU.bypass, replica_groups=[list(range(NCORE))],
                                   ins=[alc.opt()], outs=[alg.opt()])
        CC.inc(ins)

    st3a.close()
    barrier()

    st3b = ExitStack()
    Eb = sb(st3b, "Eb", [128, NH * CS], BF16)
    g2m = sb(st3b, "g2m", [128, G * CS], BF16)
    t1b = [sb(st3b, f"t1b{i}", [128, 512], BF16) for i in range(2)]
    t1e = [sb(st3b, f"t1e{i}", [128, 512], BF16) for i in range(2)]
    t32b = sb(st3b, "t32b", [128, 512], F32)
    t1f = sb(st3b, "t1f", [128, 512], F32)
    cflb = sb(st3b, "cflb", [1, 4096], F32)

    peM, aeM, veM, lxM = PEs.n, AEs.n, VEs.n, LX.n

    @block.sync
    def _(sy):
        sy.wait_ge(CC.h, 2)
        ins = sy.dma_start(out=s_ag[:], in_=alg)
        LX.inc(ins, 16)

    @block.vector
    def _(v):
        v.wait_ge(LX.h, lxM + 16)
        ins = v.tensor_scalar(s_ag[:], s_ag[:], s_m2[:], s_m2c[:], ALU.mult, ALU.add)
        VEs.inc(ins)

    @block.tensor
    def _(t):
        t.wait_ge(VEs.h, veM + 1)
        ins = t.transpose(pF[:, 0:8], s_ag[:], eye[0:8, 0:8])
        PEs.inc(ins)

    @block.scalar
    def _(s):
        s.wait_ge(PEs.h, peM + 1)
        ins = s.copy(a_hm[:], pF[:, 0:8])
        AEs.inc(ins)

    @block.vector
    def _(v):
        v.wait_ge(AEs.h, aeM + 1)
        v.memset(whm[:, 7:8], 1.0)
        for gg in range(6, -1, -1):
            v.tensor_tensor(whm[:, gg:gg + 1], whm[:, gg + 1:gg + 2], a_hm[:, gg + 1:gg + 2], ALU.mult)
        ins = v.tensor_tensor(whm[:], whm[:], s_m2h[:], ALU.mult)
        VEs.inc(ins)
        ins = v.memset(incb[:], 0.0)
        VEs.inc(ins)

    peM2, aeM2, veM2, lxM2 = PEs.n, AEs.n, VEs.n, LX.n
    for gg in range(NCORE):
        @block.tensor
        def _(t, gg=gg):
            t.wait_ge(VEs.h, veM + 2)
            t.wait_ge(AEs.h, aeM2 + 2 * gg)
            ins = t.transpose(pG[0:1, 0:128], whm[:, gg:gg + 1], eye[:])
            PEs.inc(ins)

        @block.scalar
        def _(s, gg=gg):
            s.wait_ge(PEs.h, peM2 + 2 * gg + 1)
            ins = s.copy(lrow[:], pG[0:1, 0:128])
            AEs.inc(ins)

        @block.tensor
        def _(t, gg=gg):
            t.wait_ge(AEs.h, aeM2 + 2 * gg + 1)
            ins = t.matmul(pF[:, 0:128], ones1[:], lrow[:], start=True, stop=True)
            PEs.inc(ins)

        @block.scalar
        def _(s, gg=gg):
            s.wait_ge(PEs.h, peM2 + 2 * gg + 2)
            if gg >= 2:
                s.wait_ge(VEs.h, veM2 + 2 * (gg - 1))
            ins = s.copy(wexp[gg % 2][:], pF[:, 0:128])
            AEs.inc(ins)

        @block.sync
        def _(sy, gg=gg):
            if gg >= 2:
                sy.wait_ge(VEs.h, veM2 + 2 * (gg - 1))
            ins = sy.dma_start(out=(sreb if gg % 2 == 0 else big1)[:], in_=gath[128 * gg:128 * (gg + 1), :])
            LX.inc(ins, 16)

        @block.vector
        def _(v, gg=gg):
            v.wait_ge(AEs.h, aeM2 + 2 * gg + 2)
            v.wait_ge(LX.h, lxM2 + 16 * (gg + 1))
            src = (sreb if gg % 2 == 0 else big1)
            ins = v.tensor_tensor(Eb[:, 0:DIN].rearrange("p (h d) -> p h d", h=NH),
                                  src[:].rearrange("p (h d) -> p h d", h=NH),
                                  bv(wexp[gg % 2][:], 128, NH, HD), ALU.mult)
            VEs.inc(ins)
            ins = v.tensor_tensor(incb[:], incb[:], Eb[:, 0:DIN], ALU.add)
            VEs.inc(ins)

    barrier()

    lx4, pe4, ae4, ve4, sp4 = LX.n, PEs.n, AEs.n, VEs.n, SP.n

    dve_seq = []
    for c in range(NCH):
        if c >= 1:
            dve_seq.append(("incu", c))
        for g in range(8):
            dve_seq.append(("g2m", c, g))
        for s_ in range(32):
            dve_seq.append(("min", c, s_))
            if s_ >= 1:
                dve_seq.append(("emul", c, s_ - 1))
        dve_seq.append(("emul", c, 31))
        for hb in range(16):
            dve_seq.append(("mrg", c, hb))
    dve_at = {tg: ve4 + k + 1 for k, tg in enumerate(dve_seq)}

    pe_seq = []
    for c in range(NCH):
        for g in range(8):
            pe_seq.append(("g2", c, g))
        for s_ in range(32):
            pe_seq.append(("bc", c, s_))
        for hb in range(16):
            pe_seq.append(("yb", c, hb))
    pe_at = {tg: pe4 + k + 1 for k, tg in enumerate(pe_seq)}

    ae_at = {}
    k = 0
    for c in range(NCH):
        for s_ in range(32):
            k += 1
            ae_at[("exp", c, s_)] = ae4 + k

    lx_at = {}
    cnt = lx4
    for c in range(NCH):
        cnt += 16
        lx_at[("xs", c)] = cnt
        cnt += 16
        lx_at[("cf", c, 0)] = cnt
        if c >= 1:
            cnt += 16
            lx_at[("sre", c)] = cnt
        for q in range(1, 4):
            cnt += 16
            lx_at[("cf", c, q)] = cnt

    @block.sync
    def _(sy):
        sy.wait_ge(SP.h, sp4)
        for c in range(NCH):
            if c >= 1:
                sy.wait_ge(PEs.h, pe_at[("yb", c - 1, 15)])
            ins = sy.dma_start(out=xsb[:], in_=xs_d[c])
            LX.inc(ins, 16)
            if c >= 1:
                sy.wait_ge(PEs.h, pe_at[("bc", c - 1, 31)])
            ins = sy.dma_start(out=cflb[0:1, 0:4096], in_=cfl_d[c, 0:4096].rearrange("(a b) -> a b", a=1))
            LX.inc(ins, 16)
            if c >= 1:
                ins = sy.dma_start(out=sreb[:], in_=st_d[c - 1])
                LX.inc(ins, 16)
            for q in range(1, 4):
                sy.wait_ge(PEs.h, pe_at[("bc", c, 8 * q - 1)])
                ins = sy.dma_start(out=cflb[0:1, 0:4096],
                                   in_=cfl_d[c, 4096 * q:4096 * (q + 1)].rearrange("(a b) -> a b", a=1))
                LX.inc(ins, 16)

    @block.vector
    def _(v):
        for c in range(NCH):
            if c >= 1:
                v.wait_ge(LX.h, lx_at[("sre", c)])
                v.wait_ge(PEs.h, pe_at[("yb", c - 1, 15)])
                v.tensor_tensor(Eb[:, 0:DIN].rearrange("p (h d) -> p h d", h=NH),
                                incb[:].rearrange("p (h d) -> p h d", h=NH),
                                bv(al_bf[:, CS * (c - 1):CS * c], 128, NH, HD), ALU.mult)
                ins = v.tensor_tensor(incb[:], Eb[:, 0:DIN], sreb[:], ALU.add)
                VEs.inc(ins)
            for g in range(8):
                v.wait_ge(PEs.h, pe_at[("g2", c, g)])
                ins = v.tensor_tensor(g2m[:, CS * g:CS * (g + 1)], (pF if g % 2 == 0 else pG)[:, 0:128],
                                      umask[:], ALU.mult)
                VEs.inc(ins)
            for s_ in range(32):
                v.wait_ge(PEs.h, pe_at[("bc", c, s_)])
                p = pF if s_ % 2 == 0 else pG
                v.tensor_tensor(t32b[:].rearrange("p (hh i) -> p hh i", hh=4),
                                p[:].rearrange("p (hh i) -> p hh i", hh=4),
                                bv(cumT[:, CS * c + 4 * s_:CS * c + 4 * (s_ + 1)], 128, 4, 128), ALU.subtract)
                if s_ >= 2:
                    v.wait_ge(AEs.h, ae_at[("exp", c, s_ - 2)])
                elif c >= 1:
                    v.wait_ge(AEs.h, ae_at[("exp", c - 1, 30 + s_)])
                ins = v.tensor_scalar(t1b[s_ % 2][:], t32b[:], 0.0, None, ALU.min)
                VEs.inc(ins)
                if s_ >= 1:
                    sj = s_ - 1
                    v.wait_ge(AEs.h, ae_at[("exp", c, sj)])
                    gg = sj // 4
                    ins = v.tensor_tensor(Eb[:, 512 * sj:512 * (sj + 1)].rearrange("p (hh i) -> p hh i", hh=4),
                                          t1e[sj % 2][:].rearrange("p (hh i) -> p hh i", hh=4),
                                          g2m[:, CS * gg:CS * (gg + 1)].unsqueeze(1).broadcast_to((128, 4, 128)),
                                          ALU.mult)
                    VEs.inc(ins)
            sj = 31
            v.wait_ge(AEs.h, ae_at[("exp", c, sj)])
            gg = sj // 4
            ins = v.tensor_tensor(Eb[:, 512 * sj:512 * (sj + 1)].rearrange("p (hh i) -> p hh i", hh=4),
                                  t1e[sj % 2][:].rearrange("p (hh i) -> p hh i", hh=4),
                                  g2m[:, CS * gg:CS * (gg + 1)].unsqueeze(1).broadcast_to((128, 4, 128)),
                                  ALU.mult)
            VEs.inc(ins)
            for hb in range(16):
                v.wait_ge(PEs.h, pe_at[("yb", c, hb)])
                pyd = pA if hb % 2 == 0 else pC
                pyo = pB if hb % 2 == 0 else pD
                v.tensor_tensor(t1f[:].rearrange("p (hh d) -> p hh d", hh=8),
                                pyo[:].rearrange("p (hh d) -> p hh d", hh=8),
                                bv(sd_f[:, CS * c + 8 * hb:CS * c + 8 * (hb + 1)], 128, 8, HD), ALU.mult)
                if c >= 1 and hb == 0:
                    v.wait_ge(SP.h, sp4 + 16 * c)
                ins = v.tensor_tensor(big1[:, 512 * hb:512 * (hb + 1)], t1f[:], pyd[:], ALU.add)
                VEs.inc(ins)
    assert VEs.n == ve4 + len(dve_seq), (VEs.n, ve4, len(dve_seq))

    @block.tensor
    def _(t):
        for c in range(NCH):
            for g in range(8):
                p = pF if g % 2 == 0 else pG
                if (c, g) not in ((0, 0), (0, 1)):
                    prev = ("g2m", c, g - 2) if g >= 2 else ("min", c - 1, 30 + g)
                    t.wait_ge(VEs.h, dve_at[prev])
                ins = t.matmul(p[:, 0:128], bcm[:, T * g + CS * c:T * g + CS * (c + 1)],
                               ccm[:, T * g + CS * c:T * g + CS * (c + 1)], start=True, stop=True)
                PEs.inc(ins)
            for s_ in range(32):
                p = pF if s_ % 2 == 0 else pG
                prev = ("min", c, s_ - 2) if s_ >= 2 else ("g2m", c, 6 + s_)
                t.wait_ge(VEs.h, dve_at[prev])
                t.wait_ge(LX.h, lx_at[("cf", c, s_ // 8)])
                off = 512 * (s_ % 8)
                ins = t.matmul(p[:, 0:512], ones1[:], cflb[0:1, off:off + 512], start=True, stop=True)
                PEs.inc(ins)
            for hb in range(16):
                pyd = pA if hb % 2 == 0 else pC
                pyo = pB if hb % 2 == 0 else pD
                t.wait_ge(VEs.h, dve_at[("emul", c, 2 * hb + 1)])
                if c >= 1 or hb >= 2:
                    prev = ("mrg", c, hb - 2) if hb >= 2 else ("mrg", c - 1, 14 + hb)
                    t.wait_ge(VEs.h, dve_at[prev])
                if hb == 0 and c >= 1:
                    t.wait_ge(VEs.h, dve_at[("incu", c)])
                t.wait_ge(LX.h, lx_at[("xs", c)])
                for hh in range(8):
                    h = 8 * hb + hh
                    t.matmul(pyd[:, 64 * hh:64 * (hh + 1)], Eb[:, CS * h:CS * (h + 1)],
                             xsb[:, HD * h:HD * (h + 1)], start=True, stop=True)
                    ins = t.matmul(pyo[:, 64 * hh:64 * (hh + 1)],
                                   ccm[:, T * (h // 16) + CS * c:T * (h // 16) + CS * (c + 1)],
                                   incb[:, HD * h:HD * (h + 1)], start=True, stop=True)
                PEs.inc(ins)
    assert PEs.n == pe4 + len(pe_seq)

    @block.scalar
    def _(s):
        for c in range(NCH):
            for sl in range(32):
                s.wait_ge(VEs.h, dve_at[("min", c, sl)])
                if sl >= 2:
                    s.wait_ge(VEs.h, dve_at[("emul", c, sl - 2)])
                elif c >= 1:
                    s.wait_ge(VEs.h, dve_at[("emul", c - 1, 30 + sl)])
                ins = s.activation(t1e[sl % 2][:], t1b[sl % 2][:], AF.Exp)
                AEs.inc(ins)
    assert AEs.n == ae4 + 32 * NCH

    @block.gpsimd
    def _(g):
        for c in range(NCH):
            g.wait_ge(VEs.h, dve_at[("mrg", c, 15)])
            ins = g.dma_start(out=yf_d[CS * c:CS * (c + 1), :], in_=big1[:])
            SP.inc(ins, 16)

    if DEBUG:
        @block.gpsimd
        def _(g):
            g.wait_ge(SP.h, SP.n)
            g.dma_start(out=dbg["dbg_xcm"], in_=x_cm).then_inc(DBG.h, 16)
            g.dma_start(out=dbg["dbg_bcm"], in_=b_cm).then_inc(DBG.h, 16)
            g.dma_start(out=dbg["dbg_yf"], in_=yf_d).then_inc(DBG.h, 16)
            g.dma_start(out=dbg["dbg_st"], in_=st_d.rearrange("c p d -> (c p) d")).then_inc(DBG.h, 16)
            g.dma_start(out=dbg["dbg_gath"], in_=gath).then_inc(DBG.h, 16)
            g.dma_start(out=dbg["dbg_xn"], in_=xn_d).then_inc(DBG.h, 16)
        DBG.n += 96

    st3b.close()
    st3.close()
    st23.close()
    barrier()

    st4 = ExitStack()
    yh = sb(st4, "yh", [128, 64 * 512], BF16)
    st4a = ExitStack()
    xn2 = sb(st4a, "xn2", [128, 32 * TH], BF16)
    wb4 = [sb(st4a, f"wb4{i}", [128, 4096], BF16) for i in range(3)]
    ytl = [sb(st4a, f"ytl{i}", [128, T], BF16) for i in range(3)]
    xre = [sb(st4a, f"xre{i}", [128, T], BF16) for i in range(3)]
    sgb = [sb(st4a, f"sgb{i}", [128, T], BF16) for i in range(2)]
    y2b = [sb(st4a, f"y2b{i}", [128, T], BF16) for i in range(2)]
    ysq = [sb(st4a, f"ysq{i}", [128, T], BF16) for i in range(2)]
    yob = [sb(st4a, f"yob{i}", [128, T], BF16) for i in range(2)]
    yGg = sb(st4a, "yGg", [128, 8 * T], BF16)
    rsb = sb(st4a, "rsb", [128, T], BF16)
    vr2 = sb(st4a, "vr2", [1, T], F32)
    s_dex = sb(st4a, "s_dex", [128, 64], BF16)
    s_nw = sb(st4a, "s_nw", [128, 64], F32)

    lx5, pe5, ae5, ve5, sp5, lw5 = LX.n, PEs.n, AEs.n, VEs.n, SP.n, LW.n

    pe_seq4 = []
    for gt in range(64):
        pe_seq4.append(("gmm", gt))
        pe_seq4.append(("vmm", gt))
        if gt % 8 == 7:
            pe_seq4.append(("rbc", gt // 8))
    pe_at4 = {tg: pe5 + k + 1 for k, tg in enumerate(pe_seq4)}

    ae_seq4 = []
    for gt in range(64):
        ae_seq4.append(("silu", gt))
        if gt % 8 == 7:
            gr = gt // 8
            ae_seq4.append(("vev", gr))
            ae_seq4.append(("rbce", gr))
            for q in range(8):
                ae_seq4.append(("nw", 8 * gr + q))
    ae_at4 = {tg: ae5 + k + 1 for k, tg in enumerate(ae_seq4)}

    ve_seq4 = []
    for gt in range(64):
        ve_seq4.append(("y2", gt))
        ve_seq4.append(("yG", gt))
        ve_seq4.append(("sq", gt))
        if gt % 8 == 7:
            for q in range(8):
                ve_seq4.append(("ynm", 8 * (gt // 8) + q))
    ve_at4 = {tg: ve5 + k + 1 for k, tg in enumerate(ve_seq4)}

    @block.sync
    def _(sy):
        sy.wait_ge(SP.h, sp5)
        ins = sy.dma_start(out=xn2[:], in_=xn_d)
        LX.inc(ins, 16)
        ins = sy.dma_start(out=s_nw[:], in_=nwi)
        LX.inc(ins, 16)
        for gt in range(64):
            if gt >= 3:
                sy.wait_ge(VEs.h, ve_at4[("y2", gt - 3)])
            ins = sy.dma_start(out=ytl[gt % 3][:], in_=yf_d[:, 128 * gt:128 * (gt + 1)], transpose=True)
            LX.inc(ins, 16)
            ins = sy.dma_start(out=xre[gt % 3][:], in_=x_cm[128 * gt:128 * (gt + 1), :])
            LX.inc(ins, 16)
        for k in range(64):
            sy.wait_ge(SP.h, sp5 + 16 * (k + 1))
            ins = sy.dma_start(out=yh[:, 512 * k:512 * (k + 1)], in_=yo_d[128 * k:128 * (k + 1), 0:512])
            LX.inc(ins, 16)

    @block.gpsimd
    def _(g):
        g.wait_ge(SP.h, sp5)
        ins = g.dma_start(out=s_dex[:], in_=dexpi)
        LW.inc(ins, 16)
        for gt in range(64):
            if gt >= 3:
                g.wait_ge(PEs.h, pe_at4[("gmm", gt - 3)])
            ins = g.dma_start(out=wb4[gt % 3][:], in_=wp[gt])
            LW.inc(ins, 16)

    @block.tensor
    def _(t):
        for gt in range(64):
            gr = gt // 8
            t.wait_ge(LW.h, lw5 + 16 * (gt + 2))
            t.wait_ge(LX.h, lx5 + 32)
            if gt >= 2:
                t.wait_ge(AEs.h, ae_at4[("silu", gt - 2)])
            a = pA if gt % 2 == 0 else pC
            b = pB if gt % 2 == 0 else pD
            w = wb4[gt % 3]
            for k in range(32):
                s0, s1 = (k == 0), (k == 31)
                t.matmul(a[:], w[:, 128 * k:128 * (k + 1)], xn2[:, TH * k + 3:TH * k + 515], start=s0, stop=s1)
                ins = t.matmul(b[:], w[:, 128 * k:128 * (k + 1)], xn2[:, TH * k + 515:TH * k + 1027], start=s0, stop=s1)
            PEs.inc(ins)
            t.wait_ge(VEs.h, ve_at4[("sq", gt)])
            if gt % 8 == 0 and gr >= 1:
                t.wait_ge(AEs.h, ae_at4[("rbce", gr - 1)])
            t.matmul(pF[0:1, :], onescol_bf[:], ysq[gt % 2][:, 0:512], start=(gt % 8 == 0), stop=(gt % 8 == 7))
            ins = t.matmul(pG[0:1, :], onescol_bf[:], ysq[gt % 2][:, 512:1024], start=(gt % 8 == 0), stop=(gt % 8 == 7))
            PEs.inc(ins)
            if gt % 8 == 7:
                t.wait_ge(AEs.h, ae_at4[("vev", gr)])
                t.matmul(pF[:], ones1[:], vr2[0:1, 0:512], start=True, stop=True)
                ins = t.matmul(pG[:], ones1[:], vr2[0:1, 512:1024], start=True, stop=True)
                PEs.inc(ins)

    @block.scalar
    def _(s):
        for gt in range(64):
            gr = gt // 8
            s.wait_ge(PEs.h, pe_at4[("gmm", gt)])
            if gt >= 2:
                s.wait_ge(VEs.h, ve_at4[("yG", gt - 2)])
            a = pA if gt % 2 == 0 else pC
            b = pB if gt % 2 == 0 else pD
            s.activation(sgb[gt % 2][:, 0:512], a[:], AF.Silu)
            ins = s.activation(sgb[gt % 2][:, 512:1024], b[:], AF.Silu)
            AEs.inc(ins)
            if gt % 8 == 7:
                s.wait_ge(PEs.h, pe_at4[("vmm", gt)])
                s.copy(vr2[0:1, 0:512], pF[0:1, :])
                s.copy(vr2[0:1, 512:1024], pG[0:1, :])
                s.activation(vr2[:], vr2[:], AF.Ln, scale=1.0 / 1024.0, bias=EPS)
                ins = s.activation(vr2[:], vr2[:], AF.Exp, scale=-0.5)
                AEs.inc(ins)
                s.wait_ge(PEs.h, pe_at4[("rbc", gr)])
                s.copy(rsb[:, 0:512], pF[:])
                ins = s.copy(rsb[:, 512:1024], pG[:])
                AEs.inc(ins)
                for q in range(8):
                    gq = 8 * gr + q
                    s.wait_ge(VEs.h, ve_at4[("ynm", gq)])
                    ins = s.copy(lrow[:], lrow[:])
                    AEs.inc(ins)
                    ins = s.dma_start(out=yo_d[128 * gq:128 * (gq + 1), :], in_=yob[gq % 2][:])
                    SP.inc(ins, 16)
                if DEBUG:
                    for q in range(8):
                        gq = 8 * gr + q
                        ins = s.dma_start(out=dbg["dbg_yg"][128 * gq:128 * (gq + 1), :], in_=yGg[:, T * q:T * (q + 1)])
                        DBG.inc(ins, 16)
                    ins = s.dma_start(out=dbg["dbg_rs"][128 * gr:128 * (gr + 1), :], in_=rsb[:])
                    DBG.inc(ins, 16)
    assert AEs.n == ae5 + len(ae_seq4)

    @block.vector
    def _(v):
        for gt in range(64):
            gr = gt // 8
            v.wait_ge(LX.h, lx5 + 32 + 32 * (gt + 1))
            v.wait_ge(LW.h, lw5 + 16)
            ins = v.scalar_tensor_tensor(y2b[gt % 2][:], xre[gt % 3][:], s_dex[:, gt:gt + 1],
                                         ytl[gt % 3][:], ALU.mult, ALU.add)
            VEs.inc(ins)
            v.wait_ge(AEs.h, ae_at4[("silu", gt)])
            if gt >= 16:
                v.wait_ge(AEs.h, ae_at4[("nw", gt - 16)])
            ins = v.tensor_tensor(yGg[:, T * (gt % 8):T * (gt % 8 + 1)], y2b[gt % 2][:], sgb[gt % 2][:], ALU.mult)
            VEs.inc(ins)
            if gt >= 2:
                v.wait_ge(PEs.h, pe_at4[("vmm", gt - 2)])
            ins = v.tensor_tensor(ysq[gt % 2][:], yGg[:, T * (gt % 8):T * (gt % 8 + 1)],
                                  yGg[:, T * (gt % 8):T * (gt % 8 + 1)], ALU.mult)
            VEs.inc(ins)
            if gt % 8 == 7:
                v.wait_ge(AEs.h, ae_at4[("rbce", gr)])
                for q in range(8):
                    gq = 8 * gr + q
                    if gq >= 2:
                        v.wait_ge(SP.h, sp5 + 16 * (gq - 1))
                    ins = v.scalar_tensor_tensor(yob[gq % 2][:], yGg[:, T * q:T * (q + 1)],
                                                 s_nw[:, gq:gq + 1], rsb[:], ALU.mult, ALU.mult)
                    VEs.inc(ins)
    assert VEs.n == ve5 + len(ve_seq4)

    st4a.close()
    barrier()

    st4b = ExitStack()
    xnh = sb(st4b, "xnh", [128, 32 * 512], BF16)
    wob = [sb(st4b, f"wob{i}", [128, 8192], BF16) for i in range(2)]
    tb = [sb(st4b, f"tb{i}", [128, 4096], BF16) for i in range(2)]
    scb = [sb(st4b, f"scb{i}", [128, 4096], BF16) for i in range(2)]
    fb = [sb(st4b, f"fb{i}", [128, 4096], BF16) for i in range(2)]
    xrb = [sb(st4b, f"xrb{i}", [128, 512], F32) for i in range(2)]
    oat = [sb(st4b, f"oat{i}", [128, 512], F32) for i in range(4)]

    lx6, pe6, ae6, ve6, lw6, sp6, lwb6 = LX.n, PEs.n, AEs.n, VEs.n, LW.n, SP.n, LWB.n

    lx_xr = {}
    cnt = lx6
    lxh = []
    lxff = []
    for th in range(2):
        cnt += (32 if th == 0 else 96) * 16
        lxh.append(cnt)
        for ot in range(32):
            cnt += 16
            lx_xr[(th, ot)] = cnt
    lxff = [lxh[0], lxh[1]]

    @block.sync
    def _(sy):
        sy.wait_ge(SP.h, sp6)
        for th in range(2):
            if th == 1:
                sy.wait_ge(PEs.h, pe6 + 32)
                for k in range(64):
                    ins = sy.dma_start(out=yh[:, 512 * k:512 * (k + 1)],
                                       in_=yo_d[128 * k:128 * (k + 1), 512:1024])
                    LX.inc(ins, 16)
            for k in range(32):
                ins = sy.dma_start(out=xnh[:, 512 * k:512 * (k + 1)],
                                   in_=xn_d[0:128, TH * k + 3 + 512 * th:TH * k + 3 + 512 * (th + 1)])
                LX.inc(ins, 16)
            for ot in range(32):
                i = 32 * th + ot
                if i >= 2:
                    sy.wait_ge(VEs.h, ve6 + 2 * (i - 1))
                ins = sy.dma_start(out=xrb[i % 2][:], in_=xt[128 * ot:128 * (ot + 1), 3 + 512 * th:3 + 512 * (th + 1)])
                LX.inc(ins, 16)

    @block.gpsimd
    def _(g):
        g.wait_ge(SP.h, sp6)
        for th in range(2):
            for ot in range(32):
                i = 32 * th + ot
                if i >= 2:
                    g.wait_ge(PEs.h, pe6 + i - 1)
                ins = g.dma_start(out=wob[i % 2][:], in_=wop[ot])
                LWB.inc(ins, 16)
                if i >= 2:
                    g.wait_ge(VEs.h, ve6 + 2 * (i - 2) + 1)
                ins = g.dma_start(out=tb[i % 2][:], in_=tern[ot])
                LW.inc(ins, 16)
                ins = g.dma_start(out=scb[i % 2][:], in_=scf[ot])
                LW.inc(ins, 16)

    @block.vector
    def _(v):
        for th in range(2):
            for ot in range(32):
                i = 32 * th + ot
                v.wait_ge(LW.h, lw6 + 32 * (i + 1))
                if i >= 2:
                    v.wait_ge(PEs.h, pe6 + i - 1)
                ins = v.tensor_tensor(fb[i % 2][:], tb[i % 2][:], scb[i % 2][:], ALU.mult)
                VEs.inc(ins)
                v.wait_ge(PEs.h, pe6 + i + 1)
                v.wait_ge(LX.h, lx_xr[(th, ot)])
                if i >= 4:
                    v.wait_ge(STO.h, 16 * (i - 3))
                p = pA if i % 2 == 0 else pB
                ins = v.tensor_tensor(oat[i % 4][:], p[:], xrb[i % 2][:], ALU.add)
                VEs.inc(ins)

    @block.tensor
    def _(t):
        for th in range(2):
            t.wait_ge(LX.h, lxh[th])
            for ot in range(32):
                i = 32 * th + ot
                t.wait_ge(LWB.h, lwb6 + 16 * (i + 1))
                t.wait_ge(VEs.h, ve6 + 2 * i + 1)
                if i >= 2:
                    t.wait_ge(VEs.h, ve6 + 2 * (i - 1))
                p = pA if i % 2 == 0 else pB
                w = wob[i % 2]
                f = fb[i % 2]
                for k in range(64):
                    t.matmul(p[:], w[:, 128 * k:128 * (k + 1)], yh[:, 512 * k:512 * (k + 1)],
                             start=(k == 0), stop=False)
                for k2 in range(32):
                    ins = t.matmul(p[:], f[:, 128 * k2:128 * (k2 + 1)], xnh[:, 512 * k2:512 * (k2 + 1)],
                                   start=False, stop=(k2 == 31))
                PEs.inc(ins)

    @block.scalar
    def _(s):
        for th in range(2):
            for ot in range(32):
                i = 32 * th + ot
                s.wait_ge(VEs.h, ve6 + 2 * (i + 1))
                ins = s.dma_start(out=outT[128 * ot:128 * (ot + 1), 512 * th:512 * (th + 1)], in_=oat[i % 4][:])
                STO.inc(ins, 16)
        s.wait_ge(STO.h, STO.n)

    if DEBUG:
        @block.gpsimd
        def _(g):
            g.wait_ge(STO.h, 1024)
            g.dma_start(out=dbg["dbg_yo"], in_=yo_d).then_inc(DBG.h, 16)
            g.wait_ge(DBG.h, DBG.n + 16)
        DBG.n += 16

    st4b.close()
    st4.close()
    return nc, st


def _prep(inputs):
    x = np.asarray(inputs["L0_attn_hidden_states"])[0]
    W_in = np.asarray(inputs["L0_attn_in_proj_w"])
    conv_w = np.asarray(inputs["L0_attn_conv_w"])
    conv_b = np.asarray(inputs["L0_attn_conv_b"])
    dt_bias = np.asarray(inputs["L0_attn_dt_bias"])
    A_log = np.asarray(inputs["L0_attn_A_log"])
    D = np.asarray(inputs["L0_attn_D"])
    norm_w = np.asarray(inputs["L0_attn_norm_w"])
    W_out = np.asarray(inputs["L0_attn_out_proj_w"])
    ternary = np.asarray(inputs["L0_ffn_ternary"])
    scales = np.asarray(inputs["L0_ffn_scales"])

    xT = np.ascontiguousarray(x.T.astype(np.float32))
    wp = np.ascontiguousarray(W_in.astype(np.float32).reshape(N_OT, 128, 32, 128).transpose(0, 3, 2, 1)).reshape(N_OT, 128, 4096)
    wopk = np.ascontiguousarray(W_out.astype(np.float32).reshape(32, 128, 64, 128).transpose(0, 3, 2, 1)).reshape(32, 128, 8192)
    ternk = np.ascontiguousarray(ternary.reshape(32, 128, 32, 128).transpose(0, 3, 2, 1)).reshape(32, 128, 4096)
    s2 = scales.astype(np.float32).reshape(4096, 32)
    s3 = np.ascontiguousarray(s2.reshape(32, 128, 32).transpose(0, 2, 1))
    scfk = np.ascontiguousarray(np.broadcast_to(s3[:, None, :, :], (32, 128, 32, 128)).reshape(32, 128, 4096))
    cwp = np.ascontiguousarray(conv_w.astype(np.float32).reshape(N_XBC, 128, 4).transpose(1, 0, 2)).reshape(128, N_XBC * 4)
    cbp = np.ascontiguousarray(conv_b.astype(np.float32).reshape(N_XBC, 128).transpose(1, 0))
    dtb = np.ascontiguousarray(dt_bias.astype(np.float32).reshape(128, 1))
    alg_ = np.ascontiguousarray(A_log.astype(np.float32).reshape(128, 1))
    dexp = np.empty((128, 64), np.float32)
    nwp_ = np.empty((128, 64), np.float32)
    ar = np.arange(128)
    for tt in range(64):
        ch = 128 * tt + ar
        dexp[:, tt] = D[ch // 64]
        nwp_[:, tt] = norm_w[ch]
    in_maps = []
    for g in range(NCORE):
        lo = g * T
        if g == 0:
            xs = np.concatenate([np.zeros((HID, 3), np.float32), xT[:, 0:T]], axis=1)
        else:
            xs = np.ascontiguousarray(xT[:, lo - 3:lo + T])
        m2v = (np.arange(8) < g).astype(np.float32).reshape(8, 1)
        d = dict(xt=xs, wp=wp, wop=wopk, tern=ternk, scf=scfk,
                 cwi=cwp, cbi=cbp, dtbi=dtb, alogi=alg_, dexpi=dexp, nwi=nwp_,
                 m2i=np.ascontiguousarray(m2v), m2ci=np.ascontiguousarray(1.0 - m2v),
                 m2hi=np.ascontiguousarray(np.broadcast_to(m2v.reshape(1, 8), (128, 8))))
        in_maps.append(d)
    return in_maps


def kernel(**inputs):
    in_maps = _prep(inputs)
    nc, st = build_nc()
    try:
        res = run_bass_kernel_spmd(nc, in_maps, core_ids=list(range(NCORE)))
    finally:
        st.close()
    outs = [np.asarray(res.results[g]["outT"]) for g in range(NCORE)]
    outT = np.concatenate(outs, axis=1)
    out = np.ascontiguousarray(outT.T).reshape(1, 8192, HID).astype(np.float32)
    if DEBUG:
        kernel._last = res
    return out


# version 11
# speedup vs baseline: 1.0133x; 1.0063x over previous
import numpy as np
import ml_dtypes
from contextlib import ExitStack

import concourse.bass as bass
import concourse.mybir as mybir
from concourse.bass_utils import run_bass_kernel_spmd

F32 = mybir.dt.float32
BF16 = mybir.dt.bfloat16
I8 = mybir.dt.int8
AF = mybir.ActivationFunctionType
ALU = mybir.AluOpType

HID = 4096
T = 1024
TH = T + 3
NH = 128
HD = 64
DIN = 8192
G = 8
NST = 128
CS = 128
NCH = T // CS
N_OT = 145
XBC0 = 64
N_XBC = 80
NCORE = 8
EPS = 1e-5

DEBUG = False
CONSTS = (1e-5, -0.5, -1.0, 1.0, 1.0 / 1024.0)


def bv(ap2d, p, k, reps):
    return ap2d.unsqueeze(2).broadcast_to((p, k, reps))


def build_nc():
    nc = bass.Bass("TRN2", target_bir_lowering=False)

    xt = nc.dram_tensor("xt", [HID, TH], F32, kind="ExternalInput").ap()
    wp = nc.dram_tensor("wp", [N_OT, 128, 4096], F32, kind="ExternalInput").ap()
    wop = nc.dram_tensor("wop", [32, 128, 8192], F32, kind="ExternalInput").ap()
    tern = nc.dram_tensor("tern", [32, 128, 4096], I8, kind="ExternalInput").ap()
    scf = nc.dram_tensor("scf", [32, 128, 4096], F32, kind="ExternalInput").ap()
    cwi = nc.dram_tensor("cwi", [128, N_XBC * 4], F32, kind="ExternalInput").ap()
    cbi = nc.dram_tensor("cbi", [128, N_XBC], F32, kind="ExternalInput").ap()
    dtbi = nc.dram_tensor("dtbi", [128, 1], F32, kind="ExternalInput").ap()
    alogi = nc.dram_tensor("alogi", [128, 1], F32, kind="ExternalInput").ap()
    dexpi = nc.dram_tensor("dexpi", [128, 64], F32, kind="ExternalInput").ap()
    nwi = nc.dram_tensor("nwi", [128, 64], F32, kind="ExternalInput").ap()
    m2i = nc.dram_tensor("m2i", [8, 1], F32, kind="ExternalInput").ap()
    m2ci = nc.dram_tensor("m2ci", [8, 1], F32, kind="ExternalInput").ap()
    m2hi = nc.dram_tensor("m2hi", [128, 8], F32, kind="ExternalInput").ap()

    outT = nc.dram_tensor("outT", [HID, T], F32, kind="ExternalOutput").ap()

    x_cm = nc.dram_tensor("x_cm", [DIN, T], BF16).ap()
    b_cm = nc.dram_tensor("b_cm", [G * NST, T], BF16).ap()
    xn_d = nc.dram_tensor("xn_d", [128, 32 * TH], BF16).ap()
    xs_d = nc.dram_tensor("xs_d", [NCH, 128, DIN], BF16).ap()
    st_d = nc.dram_tensor("st_d", [NCH, 128, DIN], BF16).ap()
    yf_d = nc.dram_tensor("yf_d", [T, DIN], BF16).ap()
    yo_d = nc.dram_tensor("yo_d", [DIN, T], BF16).ap()
    cfl_d = nc.dram_tensor("cfl_d", [NCH, CS * NH], F32).ap()
    contrib = nc.dram_tensor("contrib", [128, DIN], BF16).ap()
    gath = nc.dram_tensor("gath", [NCORE * 128, DIN], BF16).ap()
    alc = nc.dram_tensor("alc", [1, 128], F32).ap()
    wob_d = nc.dram_tensor("wob_d", [32, 128, 8192], BF16).ap()
    alg = nc.dram_tensor("alg", [NCORE, 128], F32).ap()
    dbg = {}
    if DEBUG:
        for nm, shape, dty in [("dbg_xcm", [DIN, T], BF16), ("dbg_bcm", [G * NST, T], BF16),
                               ("dbg_yf", [T, DIN], BF16), ("dbg_yo", [DIN, T], BF16),
                               ("dbg_st", [NCH * 128, DIN], BF16), ("dbg_cum", [128, T], F32),
                               ("dbg_gath", [NCORE * 128, DIN], BF16), ("dbg_xn", [128, 32 * TH], BF16),
                               ("dbg_dt", [128, T], F32), ("dbg_yg", [DIN, T], BF16),
                               ("dbg_rs", [8 * 128, T], BF16)]:
            dbg[nm] = nc.dram_tensor(nm, shape, dty, kind="ExternalOutput").ap()

    st = ExitStack()

    def sb(stk, name, shape, dtype):
        return stk.enter_context(nc.sbuf_tensor(name, shape, dtype))

    ones1 = sb(st, "ones1", [1, 128], F32)
    onescol = sb(st, "onescol", [128, 1], F32)
    onescol_bf = sb(st, "onescol_bf", [128, 1], BF16)
    eye = sb(st, "eye", [128, 128], F32)
    umask = sb(st, "umask", [128, 128], F32)
    scr_a = sb(st, "scr_a", [128, 128], F32)
    scr_b = sb(st, "scr_b", [128, 128], F32)
    s_dtb = sb(st, "s_dtb", [128, 1], F32)
    s_acol = sb(st, "s_acol", [128, 1], F32)
    alsum = sb(st, "alsum", [1, 128], F32)
    lrow = sb(st, "lrow", [1, 128], F32)
    s_m2 = sb(st, "s_m2", [8, 1], F32)
    s_m2c = sb(st, "s_m2c", [8, 1], F32)
    s_m2h = sb(st, "s_m2h", [128, 8], F32)
    s_ag = sb(st, "s_ag", [8, 128], F32)
    a_hm = sb(st, "a_hm", [128, 8], F32)
    whm = sb(st, "whm", [128, 8], F32)
    wexp = [sb(st, f"wexp{i}", [128, 128], BF16) for i in range(2)]
    cst = sb(st, "cst", [128, 8], F32)
    dmy = sb(st, "dmy", [1, 8], F32)

    st23 = ExitStack()
    cumT = sb(st23, "cumT", [128, T], F32)
    dtT_bf = sb(st23, "dtT_bf", [128, T], BF16)
    sd_f = sb(st23, "sd_f", [128, T], F32)
    dc_bf = sb(st23, "dc_bf", [128, T], BF16)
    al_bf = sb(st23, "al_bf", [128, T], BF16)
    bcm = sb(st23, "bcm", [128, G * T], BF16)
    ccm = sb(st23, "ccm", [128, G * T], BF16)

    pA = st.enter_context(nc.psum_tensor("pA", [128, 512], F32))
    pB = st.enter_context(nc.psum_tensor("pB", [128, 512], F32))
    pC = st.enter_context(nc.psum_tensor("pC", [128, 512], F32))
    pD = st.enter_context(nc.psum_tensor("pD", [128, 512], F32))
    pE = st.enter_context(nc.psum_tensor("pE", [128, 512], F32))
    pF = st.enter_context(nc.psum_tensor("pF", [128, 512], F32))
    pG = st.enter_context(nc.psum_tensor("pG", [128, 512], F32))
    pH = st.enter_context(nc.psum_tensor("pH", [128, 512], F32))

    class S:
        def __init__(self, name):
            self.h = st.enter_context(nc.semaphore(name))
            self.n = 0

        def inc(self, instr, k=1):
            instr.then_inc(self.h, k)
            self.n += k
            return self.n

    LW = S("LW")
    LX = S("LX")
    SP = S("SP")
    CC = S("CC")
    PEs = S("PEs")
    VEs = S("VEs")
    AEs = S("AEs")
    GPs = S("GPs")
    BAR = S("BAR")
    STO = S("STO")
    DBG = S("DBG")
    WCL = S("WCL")
    WCS = S("WCS")
    LWB = S("LWB")

    block = st.enter_context(nc.Block())
    barn = [0]

    def barrier():
        barn[0] += 5
        tgt = barn[0]
        for reg in (block.tensor, block.vector, block.scalar, block.gpsimd, block.sync):
            @reg
            def _(e):
                e.sem_inc(BAR.h, 1)
                e.wait_ge(BAR.h, tgt)
        BAR.n = tgt

    @block.gpsimd
    def _(g):
        g.iota(scr_a[:], [[0, 128]], channel_multiplier=1, allow_small_or_imprecise_dtypes=True)
        g.iota(scr_b[:], [[1, 128]], channel_multiplier=0, allow_small_or_imprecise_dtypes=True)
        g.memset(ones1[:], 1.0)
        g.memset(onescol[:], 1.0)
        g.memset(onescol_bf[:], 1.0)
        g.memset(alsum[:], 0.0)
        ins = g.memset(whm[:], 0.0)
        GPs.inc(ins)
        g.dma_start(out=s_dtb[:], in_=dtbi).then_inc(LW.h, 16)
        g.dma_start(out=s_acol[:], in_=alogi).then_inc(LW.h, 16)
        g.dma_start(out=s_m2[:], in_=m2i).then_inc(LW.h, 16)
        g.dma_start(out=s_m2c[:], in_=m2ci).then_inc(LW.h, 16)
        g.dma_start(out=s_m2h[:], in_=m2hi).then_inc(LW.h, 16)
        for ci, cv in enumerate(CONSTS):
            ins = g.memset(cst[:, ci:ci + 1], cv)
        GPs.inc(ins)
    LW.n += 80
    for ci, cv in enumerate(CONSTS):
        nc.const_aps.register(F32, cv, cst[:, ci:ci + 1])

    @block.vector
    def _(v):
        v.wait_ge(GPs.h, GPs.n)
        v.tensor_tensor(eye[:], scr_a[:], scr_b[:], ALU.is_equal)
        ins = v.tensor_tensor(umask[:], scr_a[:], scr_b[:], ALU.is_le)
        VEs.inc(ins)

    @block.scalar
    def _(s):
        s.wait_ge(LW.h, 80)
        s.wait_ge(GPs.h, GPs.n)
        s.activation(s_acol[:], s_acol[:], AF.Exp)
        ins = s.mul(s_acol[:], s_acol[:], -1.0)
        AEs.inc(ins)

    barrier()

    st12 = ExitStack()
    xn_cm = sb(st12, "xn_cm", [128, 32 * TH], BF16)
    st1 = ExitStack()
    xbuf = [sb(st1, f"xbuf{i}", [128, TH], F32) for i in range(2)]
    sqb = [sb(st1, f"sqb{i}", [128, TH], BF16) for i in range(2)]
    xbb = [sb(st1, f"xbb{i}", [128, TH], BF16) for i in range(2)]
    mrow = sb(st1, "mrow", [1, TH], F32)
    vrow = sb(st1, "vrow", [1, TH], F32)
    mbc = sb(st1, "mbc", [128, TH], BF16)
    rbc = sb(st1, "rbc", [128, TH], BF16)
    xb2 = [sb(st1, f"xb2{i}", [128, TH], BF16) for i in range(2)]
    tmp1 = sb(st1, "tmp1", [128, TH], F32)

    lx0, ae0, pe0, ve0, lw0 = LX.n, AEs.n, PEs.n, VEs.n, LW.n
    ld_tile1 = {}

    @block.sync
    def _(sy):
        for i in range(0, 32, 2):
            if i >= 2:
                sy.wait_ge(PEs.h, pe0 + i - 1)
                sy.wait_ge(AEs.h, ae0 + i - 1)
            ins = sy.dma_start(out=xbuf[i % 2][:], in_=xt[128 * i:128 * (i + 1), :])
            LX.inc(ins, 16)
            ld_tile1[i] = (LX, LX.n)

    @block.gpsimd
    def _(g):
        for i in range(1, 32, 2):
            if i >= 2:
                g.wait_ge(PEs.h, pe0 + i - 1)
                g.wait_ge(AEs.h, ae0 + i - 1)
            ins = g.dma_start(out=xbuf[i % 2][:], in_=xt[128 * i:128 * (i + 1), :])
            LW.inc(ins, 16)
            ld_tile1[i] = (LW, LW.n)

    @block.scalar
    def _(s):
        for i in range(32):
            sm, val = ld_tile1[i]
            s.wait_ge(sm.h, val)
            if i >= 2:
                s.wait_ge(PEs.h, pe0 + i - 1)
            s.activation(sqb[i % 2][:], xbuf[i % 2][:], AF.Square)
            ins = s.copy(xbb[i % 2][:], xbuf[i % 2][:])
            AEs.inc(ins)

    @block.tensor
    def _(t):
        for i in range(32):
            s0, s1 = (i == 0), (i == 31)
            sm, val = ld_tile1[i]
            t.wait_ge(sm.h, val)
            t.wait_ge(AEs.h, ae0 + i + 1)
            t.matmul(pA[0:1, :], onescol_bf[:], xbb[i % 2][:, 0:512], start=s0, stop=s1)
            t.matmul(pB[0:1, :], onescol_bf[:], xbb[i % 2][:, 512:1024], start=s0, stop=s1)
            t.matmul(pH[0:1, 0:3], onescol_bf[:], xbb[i % 2][:, 1024:1027], start=s0, stop=s1)
            t.matmul(pC[0:1, :], onescol_bf[:], sqb[i % 2][:, 0:512], start=s0, stop=s1)
            t.matmul(pD[0:1, :], onescol_bf[:], sqb[i % 2][:, 512:1024], start=s0, stop=s1)
            ins = t.matmul(pE[0:1, 0:3], onescol_bf[:], sqb[i % 2][:, 1024:1027], start=s0, stop=s1)
            PEs.inc(ins)

    @block.scalar
    def _(s):
        s.wait_ge(PEs.h, pe0 + 32)
        s.copy(mrow[0:1, 0:512], pA[0:1, :])
        s.copy(mrow[0:1, 512:1024], pB[0:1, :])
        s.copy(mrow[0:1, 1024:1027], pH[0:1, 0:3])
        s.copy(vrow[0:1, 0:512], pC[0:1, :])
        s.copy(vrow[0:1, 512:1024], pD[0:1, :])
        ins = s.copy(vrow[0:1, 1024:1027], pE[0:1, 0:3])
        AEs.inc(ins)

    @block.vector
    def _(v):
        v.wait_ge(AEs.h, ae0 + 33)
        v.tensor_scalar(mrow[:], mrow[:], 1.0 / HID, None, ALU.mult)
        v.tensor_scalar(vrow[:], vrow[:], 1.0 / HID, None, ALU.mult)
        v.tensor_tensor(tmp1[0:1, :], mrow[:], mrow[:], ALU.mult)
        ins = v.tensor_tensor(vrow[:], vrow[:], tmp1[0:1, :], ALU.subtract)
        VEs.inc(ins)

    @block.scalar
    def _(s):
        s.wait_ge(VEs.h, ve0 + 1)
        s.activation(vrow[:], vrow[:], AF.Ln, bias=EPS)
        ins = s.activation(vrow[:], vrow[:], AF.Exp, scale=-0.5)
        AEs.inc(ins)

    @block.tensor
    def _(t):
        t.wait_ge(AEs.h, ae0 + 34)
        t.matmul(pA[:], ones1[:], mrow[0:1, 0:512], start=True, stop=True)
        t.matmul(pB[:], ones1[:], mrow[0:1, 512:1024], start=True, stop=True)
        t.matmul(pH[:, 0:3], ones1[:], mrow[0:1, 1024:1027], start=True, stop=True)
        t.matmul(pC[:], ones1[:], vrow[0:1, 0:512], start=True, stop=True)
        t.matmul(pD[:], ones1[:], vrow[0:1, 512:1024], start=True, stop=True)
        ins = t.matmul(pE[:, 0:3], ones1[:], vrow[0:1, 1024:1027], start=True, stop=True)
        PEs.inc(ins)

    @block.scalar
    def _(s):
        s.wait_ge(PEs.h, pe0 + 33)
        s.copy(mbc[:, 0:512], pA[:])
        s.copy(mbc[:, 512:1024], pB[:])
        s.copy(mbc[:, 1024:1027], pH[:, 0:3])
        s.copy(rbc[:, 0:512], pC[:])
        s.copy(rbc[:, 512:1024], pD[:])
        ins = s.copy(rbc[:, 1024:1027], pE[:, 0:3])
        AEs.inc(ins)

    lx1, ve1 = LX.n, VEs.n
    ld_tile2 = {}

    @block.sync
    def _(sy):
        for i in range(0, 32, 2):
            if i >= 2:
                sy.wait_ge(VEs.h, ve1 + i - 1)
            else:
                sy.wait_ge(PEs.h, pe0 + 32)
                sy.wait_ge(AEs.h, ae0 + 32)
            ins = sy.dma_start(out=xbuf[i % 2][:], in_=xt[128 * i:128 * (i + 1), :])
            LX.inc(ins, 16)
            ld_tile2[i] = (LX, LX.n)

    @block.gpsimd
    def _(g):
        for i in range(1, 32, 2):
            if i >= 2:
                g.wait_ge(VEs.h, ve1 + i - 1)
            else:
                g.wait_ge(PEs.h, pe0 + 32)
                g.wait_ge(AEs.h, ae0 + 32)
            ins = g.dma_start(out=xbuf[i % 2][:], in_=xt[128 * i:128 * (i + 1), :])
            LW.inc(ins, 16)
            ld_tile2[i] = (LW, LW.n)

    @block.scalar
    def _(s):
        for i in range(32):
            sm, val = ld_tile2[i]
            s.wait_ge(sm.h, val)
            if i >= 2:
                s.wait_ge(VEs.h, ve1 + i - 1)
            ins = s.copy(xb2[i % 2][:], xbuf[i % 2][:])
            AEs.inc(ins)

    @block.vector
    def _(v):
        v.wait_ge(AEs.h, ae0 + 35)
        for i in range(32):
            v.wait_ge(AEs.h, ae0 + 35 + i + 1)
            v.tensor_tensor(tmp1[:].bitcast(BF16)[:, 0:TH], xb2[i % 2][:], mbc[:], ALU.subtract)
            ins = v.tensor_tensor(xn_cm[:, TH * i:TH * (i + 1)], tmp1[:].bitcast(BF16)[:, 0:TH], rbc[:], ALU.mult)
            VEs.inc(ins)

    @block.gpsimd
    def _(g):
        g.wait_ge(VEs.h, ve1 + 32)
        ins = g.dma_start(out=xn_d, in_=xn_cm[:])
        SP.inc(ins, 16)

    st1.close()
    barrier()

    st2 = ExitStack()
    wbuf = [sb(st2, f"wbuf{i}", [128, 4096], BF16) for i in range(3)]
    zx = [sb(st2, f"zx{i}", [128, TH], F32) for i in range(2)]
    cacc = [sb(st2, f"cacc{i}", [128, T], F32) for i in range(2)]
    xout = [sb(st2, f"xout{i}", [128, T], BF16) for i in range(2)]
    s_cw = sb(st2, "s_cw", [128, N_XBC * 4], F32)
    s_cb = sb(st2, "s_cb", [128, N_XBC], F32)
    dt_f = sb(st2, "dt_f", [128, T], F32)
    adt = sb(st2, "adt", [128, T], F32)
    cum_hm = sb(st2, "cum_hm", [128, T], F32)
    tdt = sb(st2, "tdt", [128, T], F32)
    z128 = sb(st2, "z128", [128, 128], F32)

    lw2, pe2, ae2, ve2, sp2 = LW.n, PEs.n, AEs.n, VEs.n, SP.n
    NT2 = N_XBC + 1

    act_seq = []
    for i in range(NT2):
        act_seq.append(("evac", i))
        act_seq.append(("c0", i))
        if i >= 1 and i - 1 < N_XBC:
            act_seq.append(("silu", i - 1))
    act_at = {tg: ae2 + k + 1 for k, tg in enumerate(act_seq)}

    @block.gpsimd
    def _(g):
        g.memset(z128[:], 0.0)
        g.dma_start(out=s_cw[:], in_=cwi).then_inc(LW.h, 16)
        g.dma_start(out=s_cb[:], in_=cbi).then_inc(LW.h, 16)
        for i in range(NT2):
            if i >= 3:
                g.wait_ge(PEs.h, pe2 + i - 2)
            ins = g.dma_start(out=wbuf[i % 3][:], in_=wp[XBC0 + i])
            LW.inc(ins, 16)
    LW.n += 32
    lw2b = lw2 + 32

    @block.sync
    def _(sy):
        for j in range(64):
            sy.wait_ge(AEs.h, act_at[("silu", j)])
            ins = sy.dma_start(out=x_cm[128 * j:128 * (j + 1), :], in_=xout[j % 2][:])
            SP.inc(ins, 16)
        sy.wait_ge(AEs.h, act_at[("silu", 71)])
        for gi in range(8):
            ins = sy.dma_start(out=b_cm[NST * gi:NST * (gi + 1), :], in_=bcm[:, T * gi:T * (gi + 1)])
            SP.inc(ins, 16)

    @block.tensor
    def _(t):
        psets = [(pA, pB), (pC, pD)]
        halos = [pE, pH]
        for i in range(NT2):
            t.wait_ge(LW.h, lw2b + 16 * (i + 1))
            if i >= 2:
                t.wait_ge(AEs.h, act_at[("evac", i - 2)])
            a, b = psets[i % 2]
            hp = halos[i % 2]
            w = wbuf[i % 3]
            if i < N_XBC:
                for k in range(32):
                    s0, s1 = (k == 0), (k == 31)
                    t.matmul(a[:], w[:, 128 * k:128 * (k + 1)], xn_cm[:, TH * k:TH * k + 512], start=s0, stop=s1)
                    t.matmul(b[:], w[:, 128 * k:128 * (k + 1)], xn_cm[:, TH * k + 512:TH * k + 1024], start=s0, stop=s1)
                    ins = t.matmul(hp[:, 0:3], w[:, 128 * k:128 * (k + 1)],
                                   xn_cm[:, TH * k + 1024:TH * k + 1027], start=s0, stop=s1)
            else:
                for k in range(32):
                    s0, s1 = (k == 0), (k == 31)
                    t.matmul(a[:], w[:, 128 * k:128 * (k + 1)], xn_cm[:, TH * k + 3:TH * k + 515], start=s0, stop=s1)
                    ins = t.matmul(b[:], w[:, 128 * k:128 * (k + 1)], xn_cm[:, TH * k + 515:TH * k + 1027], start=s0, stop=s1)
            PEs.inc(ins)

    @block.scalar
    def _(s):
        psets = [(pA, pB), (pC, pD)]
        halos = [pE, pH]
        for i in range(NT2):
            s.wait_ge(PEs.h, pe2 + i + 1)
            if i >= 2:
                s.wait_ge(VEs.h, ve2 + i - 1)
            a, b = psets[i % 2]
            hp = halos[i % 2]
            z = zx[i % 2]
            s.copy(z[:, 0:512], a[:])
            s.copy(z[:, 512:1024], b[:])
            if i < N_XBC:
                ins = s.copy(z[:, 1024:1027], hp[:, 0:3])
                AEs.inc(ins)
                ins = s.activation(cacc[i % 2][:], z[:, 0:T], AF.Copy, scale=s_cw[:, 4 * i:4 * i + 1])
                AEs.inc(ins)
            else:
                ins = s.copy(z[:, 1024:1027], z[:, 1024:1027])
                AEs.inc(ins)
                ins = s.activation(cacc[i % 2][:], z[:, 0:T], AF.Copy)
                AEs.inc(ins)
            if i >= 1 and i - 1 < N_XBC:
                j = i - 1
                s.wait_ge(VEs.h, ve2 + j + 1)
                if j < 64:
                    s.wait_ge(SP.h, sp2 + 16 * max(0, j - 1))
                    dst = xout[j % 2][:]
                elif j < 72:
                    dst = bcm[:, T * (j - 64):T * (j - 63)]
                else:
                    dst = ccm[:, T * (j - 72):T * (j - 71)]
                ins = s.activation(dst, cacc[j % 2][:], AF.Silu, bias=s_cb[:, j:j + 1])
                AEs.inc(ins)
    assert AEs.n == ae2 + len(act_seq), (AEs.n, ae2, len(act_seq))

    @block.vector
    def _(v):
        for i in range(N_XBC):
            v.wait_ge(AEs.h, act_at[("c0", i)])
            z = zx[i % 2]
            c = cacc[i % 2]
            v.scalar_tensor_tensor(c[:], z[:, 1:1 + T], s_cw[:, 4 * i + 1:4 * i + 2], c[:], ALU.mult, ALU.add)
            v.scalar_tensor_tensor(c[:], z[:, 2:2 + T], s_cw[:, 4 * i + 2:4 * i + 3], c[:], ALU.mult, ALU.add)
            ins = v.scalar_tensor_tensor(c[:], z[:, 3:3 + T], s_cw[:, 4 * i + 3:4 * i + 4], c[:], ALU.mult, ALU.add)
            VEs.inc(ins)
        v.wait_ge(AEs.h, act_at[("c0", N_XBC)])
        u = cacc[N_XBC % 2]
        v.tensor_scalar(tdt[:], u[:], s_dtb[:], None, ALU.add)
        ins = v.tensor_scalar(dt_f[:], tdt[:], 0.0, None, ALU.max)
        VEs.inc(ins)

    ve2b = ve2 + N_XBC + 1

    @block.scalar
    def _(s):
        s.wait_ge(VEs.h, ve2b)
        s.activation(tdt[:], tdt[:], AF.Abs)
        s.activation(tdt[:], tdt[:], AF.Exp, scale=-1.0)
        ins = s.activation(tdt[:], tdt[:], AF.Ln, bias=1.0)
        AEs.inc(ins)
    ae2b = AEs.n

    @block.vector
    def _(v):
        v.wait_ge(AEs.h, ae2b)
        v.tensor_tensor(dt_f[:], dt_f[:], tdt[:], ALU.add)
        ins = v.tensor_scalar(adt[:], dt_f[:], s_acol[:], None, ALU.mult)
        VEs.inc(ins)
        for c in range(NCH):
            ins = v.tensor_tensor_scan(cum_hm[:, CS * c:CS * (c + 1)], adt[:, CS * c:CS * (c + 1)],
                                       z128[:], 0.0, ALU.add, ALU.add)
            VEs.inc(ins)
    ve2c = VEs.n

    peX, aeX, veX = PEs.n, AEs.n, VEs.n
    for c in range(NCH):
        @block.tensor
        def _(t, c=c):
            t.wait_ge(VEs.h, ve2c)
            if c >= 1:
                t.wait_ge(AEs.h, aeX + 3 * c)
            t.transpose(pF[:, 0:128], cum_hm[:, CS * c:CS * (c + 1)], eye[:])
            ins = t.transpose(pG[:, 0:128], dt_f[:, CS * c:CS * (c + 1)], eye[:])
            PEs.inc(ins)
            ins = t.transpose(pE[0:1, 0:128], cum_hm[:, CS * c + 127:CS * c + 128], eye[:])
            PEs.inc(ins)

        @block.scalar
        def _(s, c=c):
            s.wait_ge(PEs.h, peX + 3 * c + 2)
            s.copy(cumT[:, CS * c:CS * (c + 1)], pF[:, 0:128])
            ins = s.copy(dtT_bf[:, CS * c:CS * (c + 1)], pG[:, 0:128])
            AEs.inc(ins)
            ins = s.copy(lrow[:], pE[0:1, 0:128])
            AEs.inc(ins)

        @block.tensor
        def _(t, c=c):
            t.wait_ge(AEs.h, aeX + 3 * c + 2)
            ins = t.matmul(pF[:, 0:128], ones1[:], lrow[:], start=True, stop=True)
            PEs.inc(ins)

        @block.vector
        def _(v, c=c):
            v.wait_ge(PEs.h, peX + 3 * (c + 1))
            v.tensor_tensor(scr_a[:], pF[:, 0:128], cumT[:, CS * c:CS * (c + 1)], ALU.subtract)
            ins = v.tensor_tensor(alsum[:], alsum[:], lrow[:], ALU.add)
            VEs.inc(ins)

        @block.scalar
        def _(s, c=c):
            s.wait_ge(VEs.h, veX + c + 1)
            s.activation(dc_bf[:, CS * c:CS * (c + 1)], scr_a[:], AF.Exp)
            s.activation(al_bf[:, CS * c:CS * (c + 1)], pF[:, 0:128], AF.Exp)
            ins = s.activation(sd_f[:, CS * c:CS * (c + 1)], cumT[:, CS * c:CS * (c + 1)], AF.Exp)
            AEs.inc(ins)

    @block.gpsimd
    def _(g):
        g.wait_ge(VEs.h, ve2c)
        for c in range(NCH):
            ins = g.dma_start(out=cfl_d[c].rearrange("(h i) -> h i", h=128), in_=cum_hm[:, CS * c:CS * (c + 1)])
            SP.inc(ins, 16)

    @block.scalar
    def _(s):
        s.wait_ge(VEs.h, veX + NCH)
        ins = s.activation(alsum[:], alsum[:], AF.Exp)
        AEs.inc(ins)
    aeX2 = AEs.n

    @block.gpsimd
    def _(g):
        g.wait_ge(AEs.h, aeX2)
        ins = g.dma_start(out=alc, in_=alsum[:])
        SP.inc(ins, 16)

    if DEBUG:
        @block.gpsimd
        def _(g):
            g.wait_ge(AEs.h, aeX2)
            g.dma_start(out=dbg["dbg_cum"], in_=cum_hm[:]).then_inc(DBG.h, 16)
            g.dma_start(out=dbg["dbg_dt"], in_=dt_f[:]).then_inc(DBG.h, 16)
        DBG.n += 32

    st2.close()
    st12.close()
    barrier()

    st3 = ExitStack()
    xsb = sb(st3, "xsb", [128, DIN], BF16)
    incb = sb(st3, "incb", [128, DIN], BF16)
    sreb = sb(st3, "sreb", [128, DIN], BF16)
    big1 = sb(st3, "big1", [128, DIN], BF16)
    st3a = ExitStack()
    xTb = sb(st3a, "xTb", [128, DIN], BF16)
    xs2b = sb(st3a, "xs2b", [128, DIN], BF16)
    btb = sb(st3a, "btb", [128, G * NST], BF16)

    lx3, pe3, ae3, ve3, sp3 = LX.n, PEs.n, AEs.n, VEs.n, SP.n

    @block.sync
    def _(sy):
        sy.wait_ge(SP.h, sp3)
        for c in range(NCH):
            if c >= 1:
                sy.wait_ge(VEs.h, ve3 + 1 + 4 * (c - 1) + 1)
                sy.wait_ge(PEs.h, pe3 + 16 * c)
            ins = sy.dma_start(out=xTb[:], in_=x_cm[:, CS * c:CS * (c + 1)], transpose=True)
            LX.inc(ins, 16)
            ins = sy.dma_start(out=btb[:], in_=b_cm[:, CS * c:CS * (c + 1)], transpose=True)
            LX.inc(ins, 16)

    @block.vector
    def _(v):
        ins = v.memset(incb[:], 0.0)
        VEs.inc(ins)
        for c in range(NCH):
            v.wait_ge(LX.h, lx3 + 32 * (c + 1))
            if c >= 1:
                v.wait_ge(SP.h, sp3 + 32 * c)
            xv = xsb[:].rearrange("p (h d) -> p h d", h=NH)
            ins = v.tensor_tensor(xv, xTb[:].rearrange("p (h d) -> p h d", h=NH),
                                  bv(dtT_bf[:, CS * c:CS * (c + 1)], 128, NH, HD), ALU.mult)
            VEs.inc(ins)
            ins = v.tensor_tensor(xs2b[:].rearrange("p (h d) -> p h d", h=NH), xv,
                                  bv(dc_bf[:, CS * c:CS * (c + 1)], 128, NH, HD), ALU.mult)
            VEs.inc(ins)
            v.wait_ge(AEs.h, ae3 + 16 * (c + 1))
            ins = v.tensor_tensor(xs2b[:].rearrange("p (h d) -> p h d", h=NH),
                                  incb[:].rearrange("p (h d) -> p h d", h=NH),
                                  bv(al_bf[:, CS * c:CS * (c + 1)], 128, NH, HD), ALU.mult)
            VEs.inc(ins)
            ins = v.tensor_tensor(incb[:], xs2b[:], big1[:], ALU.add)
            VEs.inc(ins)

    @block.tensor
    def _(t):
        for c in range(NCH):
            t.wait_ge(VEs.h, ve3 + 1 + 4 * c + 2)
            for hb in range(16):
                p = pA if hb % 2 == 0 else pB
                m = 16 * c + hb
                if m >= 2:
                    t.wait_ge(AEs.h, ae3 + m - 1)
                for hh in range(8):
                    h = 8 * hb + hh
                    ins = t.matmul(p[:, 64 * hh:64 * (hh + 1)], btb[:, NST * (h // 16):NST * (h // 16 + 1)],
                                   xs2b[:, HD * h:HD * (h + 1)], start=True, stop=True)
                PEs.inc(ins)

    @block.scalar
    def _(s):
        for c in range(NCH):
            for hb in range(16):
                p = pA if hb % 2 == 0 else pB
                s.wait_ge(PEs.h, pe3 + 16 * c + hb + 1)
                if hb == 0 and c >= 1:
                    s.wait_ge(VEs.h, ve3 + 1 + 4 * c)
                    s.wait_ge(SP.h, sp3 + 32 * c)
                ins = s.copy(big1[:, 512 * hb:512 * (hb + 1)], p[:])
                AEs.inc(ins)

    @block.gpsimd
    def _(g):
        for c in range(NCH):
            g.wait_ge(VEs.h, ve3 + 1 + 4 * c + 1)
            ins = g.dma_start(out=xs_d[c], in_=xsb[:])
            SP.inc(ins, 16)
            g.wait_ge(AEs.h, ae3 + 16 * (c + 1))
            ins = g.dma_start(out=st_d[c], in_=big1[:])
            SP.inc(ins, 16)
        g.wait_ge(VEs.h, ve3 + 1 + 4 * NCH)
        ins = g.dma_start(out=contrib, in_=incb[:])
        SP.inc(ins, 16)
        g.wait_ge(SP.h, SP.n)
        ins = g.collective_compute("AllGather", ALU.bypass, replica_groups=[list(range(NCORE))],
                                   ins=[contrib.opt()], outs=[gath.opt()])
        CC.inc(ins)
        ins = g.collective_compute("AllGather", ALU.bypass, replica_groups=[list(range(NCORE))],
                                   ins=[alc.opt()], outs=[alg.opt()])
        CC.inc(ins)

    st3a.close()
    barrier()

    st3b = ExitStack()
    Eb = sb(st3b, "Eb", [128, NH * CS], BF16)
    g2m = sb(st3b, "g2m", [128, G * CS], BF16)
    t1b = [sb(st3b, f"t1b{i}", [128, 512], BF16) for i in range(2)]
    t1e = [sb(st3b, f"t1e{i}", [128, 512], BF16) for i in range(2)]
    t32b = sb(st3b, "t32b", [128, 512], F32)
    t1f = sb(st3b, "t1f", [128, 512], F32)
    cflb = sb(st3b, "cflb", [1, 4096], F32)

    peM, aeM, veM, lxM = PEs.n, AEs.n, VEs.n, LX.n

    @block.sync
    def _(sy):
        sy.wait_ge(CC.h, 2)
        ins = sy.dma_start(out=s_ag[:], in_=alg)
        LX.inc(ins, 16)

    @block.vector
    def _(v):
        v.wait_ge(LX.h, lxM + 16)
        ins = v.tensor_scalar(s_ag[:], s_ag[:], s_m2[:], s_m2c[:], ALU.mult, ALU.add)
        VEs.inc(ins)

    @block.tensor
    def _(t):
        t.wait_ge(VEs.h, veM + 1)
        ins = t.transpose(pF[:, 0:8], s_ag[:], eye[0:8, 0:8])
        PEs.inc(ins)

    @block.scalar
    def _(s):
        s.wait_ge(PEs.h, peM + 1)
        ins = s.copy(a_hm[:], pF[:, 0:8])
        AEs.inc(ins)

    @block.vector
    def _(v):
        v.wait_ge(AEs.h, aeM + 1)
        v.memset(whm[:, 7:8], 1.0)
        for gg in range(6, -1, -1):
            v.tensor_tensor(whm[:, gg:gg + 1], whm[:, gg + 1:gg + 2], a_hm[:, gg + 1:gg + 2], ALU.mult)
        ins = v.tensor_tensor(whm[:], whm[:], s_m2h[:], ALU.mult)
        VEs.inc(ins)
        ins = v.memset(incb[:], 0.0)
        VEs.inc(ins)

    peM2, aeM2, veM2, lxM2 = PEs.n, AEs.n, VEs.n, LX.n
    for gg in range(NCORE):
        @block.tensor
        def _(t, gg=gg):
            t.wait_ge(VEs.h, veM + 2)
            t.wait_ge(AEs.h, aeM2 + 2 * gg)
            ins = t.transpose(pG[0:1, 0:128], whm[:, gg:gg + 1], eye[:])
            PEs.inc(ins)

        @block.scalar
        def _(s, gg=gg):
            s.wait_ge(PEs.h, peM2 + 2 * gg + 1)
            ins = s.copy(lrow[:], pG[0:1, 0:128])
            AEs.inc(ins)

        @block.tensor
        def _(t, gg=gg):
            t.wait_ge(AEs.h, aeM2 + 2 * gg + 1)
            ins = t.matmul(pF[:, 0:128], ones1[:], lrow[:], start=True, stop=True)
            PEs.inc(ins)

        @block.scalar
        def _(s, gg=gg):
            s.wait_ge(PEs.h, peM2 + 2 * gg + 2)
            if gg >= 2:
                s.wait_ge(VEs.h, veM2 + 2 * (gg - 1))
            ins = s.copy(wexp[gg % 2][:], pF[:, 0:128])
            AEs.inc(ins)

        @block.sync
        def _(sy, gg=gg):
            if gg >= 2:
                sy.wait_ge(VEs.h, veM2 + 2 * (gg - 1))
            ins = sy.dma_start(out=(sreb if gg % 2 == 0 else big1)[:], in_=gath[128 * gg:128 * (gg + 1), :])
            LX.inc(ins, 16)

        @block.vector
        def _(v, gg=gg):
            v.wait_ge(AEs.h, aeM2 + 2 * gg + 2)
            v.wait_ge(LX.h, lxM2 + 16 * (gg + 1))
            src = (sreb if gg % 2 == 0 else big1)
            ins = v.tensor_tensor(Eb[:, 0:DIN].rearrange("p (h d) -> p h d", h=NH),
                                  src[:].rearrange("p (h d) -> p h d", h=NH),
                                  bv(wexp[gg % 2][:], 128, NH, HD), ALU.mult)
            VEs.inc(ins)
            ins = v.tensor_tensor(incb[:], incb[:], Eb[:, 0:DIN], ALU.add)
            VEs.inc(ins)

    barrier()

    lx4, pe4, ae4, ve4, sp4 = LX.n, PEs.n, AEs.n, VEs.n, SP.n

    dve_seq = []
    for c in range(NCH):
        if c >= 1:
            dve_seq.append(("incu", c))
        for g in range(8):
            dve_seq.append(("g2m", c, g))
        for s_ in range(32):
            dve_seq.append(("min", c, s_))
            if s_ >= 1:
                dve_seq.append(("emul", c, s_ - 1))
        dve_seq.append(("emul", c, 31))
        for hb in range(16):
            dve_seq.append(("mrg", c, hb))
    dve_at = {tg: ve4 + k + 1 for k, tg in enumerate(dve_seq)}

    pe_seq = []
    for c in range(NCH):
        for g in range(8):
            pe_seq.append(("g2", c, g))
        for s_ in range(32):
            pe_seq.append(("bc", c, s_))
        for hb in range(16):
            pe_seq.append(("yb", c, hb))
    pe_at = {tg: pe4 + k + 1 for k, tg in enumerate(pe_seq)}

    ae_at = {}
    k = 0
    for c in range(NCH):
        for s_ in range(32):
            k += 1
            ae_at[("exp", c, s_)] = ae4 + k

    lx_at = {}
    cnt = lx4
    for c in range(NCH):
        cnt += 16
        lx_at[("xs", c)] = cnt
        cnt += 16
        lx_at[("cf", c, 0)] = cnt
        if c >= 1:
            cnt += 16
            lx_at[("sre", c)] = cnt
        for q in range(1, 4):
            cnt += 16
            lx_at[("cf", c, q)] = cnt

    @block.sync
    def _(sy):
        sy.wait_ge(SP.h, sp4)
        for c in range(NCH):
            if c >= 1:
                sy.wait_ge(PEs.h, pe_at[("yb", c - 1, 15)])
            ins = sy.dma_start(out=xsb[:], in_=xs_d[c])
            LX.inc(ins, 16)
            if c >= 1:
                sy.wait_ge(PEs.h, pe_at[("bc", c - 1, 31)])
            ins = sy.dma_start(out=cflb[0:1, 0:4096], in_=cfl_d[c, 0:4096].rearrange("(a b) -> a b", a=1))
            LX.inc(ins, 16)
            if c >= 1:
                ins = sy.dma_start(out=sreb[:], in_=st_d[c - 1])
                LX.inc(ins, 16)
            for q in range(1, 4):
                sy.wait_ge(PEs.h, pe_at[("bc", c, 8 * q - 1)])
                ins = sy.dma_start(out=cflb[0:1, 0:4096],
                                   in_=cfl_d[c, 4096 * q:4096 * (q + 1)].rearrange("(a b) -> a b", a=1))
                LX.inc(ins, 16)

    @block.vector
    def _(v):
        for c in range(NCH):
            if c >= 1:
                v.wait_ge(LX.h, lx_at[("sre", c)])
                v.wait_ge(PEs.h, pe_at[("yb", c - 1, 15)])
                v.tensor_tensor(Eb[:, 0:DIN].rearrange("p (h d) -> p h d", h=NH),
                                incb[:].rearrange("p (h d) -> p h d", h=NH),
                                bv(al_bf[:, CS * (c - 1):CS * c], 128, NH, HD), ALU.mult)
                ins = v.tensor_tensor(incb[:], Eb[:, 0:DIN], sreb[:], ALU.add)
                VEs.inc(ins)
            for g in range(8):
                v.wait_ge(PEs.h, pe_at[("g2", c, g)])
                ins = v.tensor_tensor(g2m[:, CS * g:CS * (g + 1)], (pF if g % 2 == 0 else pG)[:, 0:128],
                                      umask[:], ALU.mult)
                VEs.inc(ins)
            for s_ in range(32):
                v.wait_ge(PEs.h, pe_at[("bc", c, s_)])
                p = pF if s_ % 2 == 0 else pG
                v.tensor_tensor(t32b[:].rearrange("p (hh i) -> p hh i", hh=4),
                                p[:].rearrange("p (hh i) -> p hh i", hh=4),
                                bv(cumT[:, CS * c + 4 * s_:CS * c + 4 * (s_ + 1)], 128, 4, 128), ALU.subtract)
                if s_ >= 2:
                    v.wait_ge(AEs.h, ae_at[("exp", c, s_ - 2)])
                elif c >= 1:
                    v.wait_ge(AEs.h, ae_at[("exp", c - 1, 30 + s_)])
                ins = v.tensor_scalar(t1b[s_ % 2][:], t32b[:], 0.0, None, ALU.min)
                VEs.inc(ins)
                if s_ >= 1:
                    sj = s_ - 1
                    v.wait_ge(AEs.h, ae_at[("exp", c, sj)])
                    gg = sj // 4
                    ins = v.tensor_tensor(Eb[:, 512 * sj:512 * (sj + 1)].rearrange("p (hh i) -> p hh i", hh=4),
                                          t1e[sj % 2][:].rearrange("p (hh i) -> p hh i", hh=4),
                                          g2m[:, CS * gg:CS * (gg + 1)].unsqueeze(1).broadcast_to((128, 4, 128)),
                                          ALU.mult)
                    VEs.inc(ins)
            sj = 31
            v.wait_ge(AEs.h, ae_at[("exp", c, sj)])
            gg = sj // 4
            ins = v.tensor_tensor(Eb[:, 512 * sj:512 * (sj + 1)].rearrange("p (hh i) -> p hh i", hh=4),
                                  t1e[sj % 2][:].rearrange("p (hh i) -> p hh i", hh=4),
                                  g2m[:, CS * gg:CS * (gg + 1)].unsqueeze(1).broadcast_to((128, 4, 128)),
                                  ALU.mult)
            VEs.inc(ins)
            for hb in range(16):
                v.wait_ge(PEs.h, pe_at[("yb", c, hb)])
                pyd = pA if hb % 2 == 0 else pC
                pyo = pB if hb % 2 == 0 else pD
                v.tensor_tensor(t1f[:].rearrange("p (hh d) -> p hh d", hh=8),
                                pyo[:].rearrange("p (hh d) -> p hh d", hh=8),
                                bv(sd_f[:, CS * c + 8 * hb:CS * c + 8 * (hb + 1)], 128, 8, HD), ALU.mult)
                if c >= 1 and hb == 0:
                    v.wait_ge(SP.h, sp4 + 16 * c)
                ins = v.tensor_tensor(big1[:, 512 * hb:512 * (hb + 1)], t1f[:], pyd[:], ALU.add)
                VEs.inc(ins)
    assert VEs.n == ve4 + len(dve_seq), (VEs.n, ve4, len(dve_seq))

    @block.tensor
    def _(t):
        for c in range(NCH):
            for g in range(8):
                p = pF if g % 2 == 0 else pG
                if (c, g) not in ((0, 0), (0, 1)):
                    prev = ("g2m", c, g - 2) if g >= 2 else ("min", c - 1, 30 + g)
                    t.wait_ge(VEs.h, dve_at[prev])
                ins = t.matmul(p[:, 0:128], bcm[:, T * g + CS * c:T * g + CS * (c + 1)],
                               ccm[:, T * g + CS * c:T * g + CS * (c + 1)], start=True, stop=True)
                PEs.inc(ins)
            for s_ in range(32):
                p = pF if s_ % 2 == 0 else pG
                prev = ("min", c, s_ - 2) if s_ >= 2 else ("g2m", c, 6 + s_)
                t.wait_ge(VEs.h, dve_at[prev])
                t.wait_ge(LX.h, lx_at[("cf", c, s_ // 8)])
                off = 512 * (s_ % 8)
                ins = t.matmul(p[:, 0:512], ones1[:], cflb[0:1, off:off + 512], start=True, stop=True)
                PEs.inc(ins)
            for hb in range(16):
                pyd = pA if hb % 2 == 0 else pC
                pyo = pB if hb % 2 == 0 else pD
                t.wait_ge(VEs.h, dve_at[("emul", c, 2 * hb + 1)])
                if c >= 1 or hb >= 2:
                    prev = ("mrg", c, hb - 2) if hb >= 2 else ("mrg", c - 1, 14 + hb)
                    t.wait_ge(VEs.h, dve_at[prev])
                if hb == 0 and c >= 1:
                    t.wait_ge(VEs.h, dve_at[("incu", c)])
                t.wait_ge(LX.h, lx_at[("xs", c)])
                for hh in range(8):
                    h = 8 * hb + hh
                    t.matmul(pyd[:, 64 * hh:64 * (hh + 1)], Eb[:, CS * h:CS * (h + 1)],
                             xsb[:, HD * h:HD * (h + 1)], start=True, stop=True)
                    ins = t.matmul(pyo[:, 64 * hh:64 * (hh + 1)],
                                   ccm[:, T * (h // 16) + CS * c:T * (h // 16) + CS * (c + 1)],
                                   incb[:, HD * h:HD * (h + 1)], start=True, stop=True)
                PEs.inc(ins)
    assert PEs.n == pe4 + len(pe_seq)

    @block.scalar
    def _(s):
        for c in range(NCH):
            for sl in range(32):
                s.wait_ge(VEs.h, dve_at[("min", c, sl)])
                if sl >= 2:
                    s.wait_ge(VEs.h, dve_at[("emul", c, sl - 2)])
                elif c >= 1:
                    s.wait_ge(VEs.h, dve_at[("emul", c - 1, 30 + sl)])
                ins = s.activation(t1e[sl % 2][:], t1b[sl % 2][:], AF.Exp)
                AEs.inc(ins)
    assert AEs.n == ae4 + 32 * NCH

    @block.gpsimd
    def _(g):
        for c in range(NCH):
            g.wait_ge(VEs.h, dve_at[("mrg", c, 15)])
            ins = g.dma_start(out=yf_d[CS * c:CS * (c + 1), :], in_=big1[:])
            SP.inc(ins, 16)

    if DEBUG:
        @block.gpsimd
        def _(g):
            g.wait_ge(SP.h, SP.n)
            g.dma_start(out=dbg["dbg_xcm"], in_=x_cm).then_inc(DBG.h, 16)
            g.dma_start(out=dbg["dbg_bcm"], in_=b_cm).then_inc(DBG.h, 16)
            g.dma_start(out=dbg["dbg_yf"], in_=yf_d).then_inc(DBG.h, 16)
            g.dma_start(out=dbg["dbg_st"], in_=st_d.rearrange("c p d -> (c p) d")).then_inc(DBG.h, 16)
            g.dma_start(out=dbg["dbg_gath"], in_=gath).then_inc(DBG.h, 16)
            g.dma_start(out=dbg["dbg_xn"], in_=xn_d).then_inc(DBG.h, 16)
        DBG.n += 96

    st3b.close()
    st3.close()
    st23.close()
    barrier()

    st4 = ExitStack()
    yh = sb(st4, "yh", [128, 64 * 512], BF16)
    st4a = ExitStack()
    xn2 = sb(st4a, "xn2", [128, 32 * TH], BF16)
    wb4 = [sb(st4a, f"wb4{i}", [128, 4096], BF16) for i in range(3)]
    ytl = [sb(st4a, f"ytl{i}", [128, T], BF16) for i in range(3)]
    xre = [sb(st4a, f"xre{i}", [128, T], BF16) for i in range(3)]
    sgb = [sb(st4a, f"sgb{i}", [128, T], BF16) for i in range(2)]
    y2b = [sb(st4a, f"y2b{i}", [128, T], BF16) for i in range(2)]
    ysq = [sb(st4a, f"ysq{i}", [128, T], BF16) for i in range(2)]
    yob = [sb(st4a, f"yob{i}", [128, T], BF16) for i in range(2)]
    yGg = sb(st4a, "yGg", [128, 8 * T], BF16)
    rsb = sb(st4a, "rsb", [128, T], BF16)
    vr2 = sb(st4a, "vr2", [1, T], F32)
    s_dex = sb(st4a, "s_dex", [128, 64], BF16)
    s_nw = sb(st4a, "s_nw", [128, 64], F32)

    lx5, pe5, ae5, ve5, sp5, lw5 = LX.n, PEs.n, AEs.n, VEs.n, SP.n, LW.n

    pe_seq4 = []
    for gt in range(64):
        pe_seq4.append(("gmm", gt))
        pe_seq4.append(("vmm", gt))
        if gt % 8 == 7:
            pe_seq4.append(("rbc", gt // 8))
    pe_at4 = {tg: pe5 + k + 1 for k, tg in enumerate(pe_seq4)}

    ae_seq4 = []
    for gt in range(64):
        ae_seq4.append(("silu", gt))
        if gt % 8 == 7:
            gr = gt // 8
            ae_seq4.append(("vev", gr))
            ae_seq4.append(("rbce", gr))
            for q in range(8):
                ae_seq4.append(("nw", 8 * gr + q))
    ae_at4 = {tg: ae5 + k + 1 for k, tg in enumerate(ae_seq4)}

    ve_seq4 = []
    for gt in range(64):
        ve_seq4.append(("y2", gt))
        ve_seq4.append(("yG", gt))
        ve_seq4.append(("sq", gt))
        if gt % 8 == 7:
            for q in range(8):
                ve_seq4.append(("ynm", 8 * (gt // 8) + q))
    ve_at4 = {tg: ve5 + k + 1 for k, tg in enumerate(ve_seq4)}

    @block.sync
    def _(sy):
        sy.wait_ge(SP.h, sp5)
        ins = sy.dma_start(out=xn2[:], in_=xn_d)
        LX.inc(ins, 16)
        ins = sy.dma_start(out=s_nw[:], in_=nwi)
        LX.inc(ins, 16)
        for gt in range(64):
            if gt >= 3:
                sy.wait_ge(VEs.h, ve_at4[("y2", gt - 3)])
            ins = sy.dma_start(out=ytl[gt % 3][:], in_=yf_d[:, 128 * gt:128 * (gt + 1)], transpose=True)
            LX.inc(ins, 16)
            ins = sy.dma_start(out=xre[gt % 3][:], in_=x_cm[128 * gt:128 * (gt + 1), :])
            LX.inc(ins, 16)
        for k in range(64):
            sy.wait_ge(SP.h, sp5 + 16 * (k + 1))
            ins = sy.dma_start(out=yh[:, 512 * k:512 * (k + 1)], in_=yo_d[128 * k:128 * (k + 1), 0:512])
            LX.inc(ins, 16)

    @block.gpsimd
    def _(g):
        g.wait_ge(SP.h, sp5)
        ins = g.dma_start(out=s_dex[:], in_=dexpi)
        LW.inc(ins, 16)
        for gt in range(64):
            if gt >= 3:
                g.wait_ge(PEs.h, pe_at4[("gmm", gt - 3)])
            ins = g.dma_start(out=wb4[gt % 3][:], in_=wp[gt])
            LW.inc(ins, 16)

    @block.tensor
    def _(t):
        for gt in range(64):
            gr = gt // 8
            t.wait_ge(LW.h, lw5 + 16 * (gt + 2))
            t.wait_ge(LX.h, lx5 + 32)
            if gt >= 2:
                t.wait_ge(AEs.h, ae_at4[("silu", gt - 2)])
            a = pA if gt % 2 == 0 else pC
            b = pB if gt % 2 == 0 else pD
            w = wb4[gt % 3]
            for k in range(32):
                s0, s1 = (k == 0), (k == 31)
                t.matmul(a[:], w[:, 128 * k:128 * (k + 1)], xn2[:, TH * k + 3:TH * k + 515], start=s0, stop=s1)
                ins = t.matmul(b[:], w[:, 128 * k:128 * (k + 1)], xn2[:, TH * k + 515:TH * k + 1027], start=s0, stop=s1)
            PEs.inc(ins)
            t.wait_ge(VEs.h, ve_at4[("sq", gt)])
            if gt % 8 == 0 and gr >= 1:
                t.wait_ge(AEs.h, ae_at4[("rbce", gr - 1)])
            t.matmul(pF[0:1, :], onescol_bf[:], ysq[gt % 2][:, 0:512], start=(gt % 8 == 0), stop=(gt % 8 == 7))
            ins = t.matmul(pG[0:1, :], onescol_bf[:], ysq[gt % 2][:, 512:1024], start=(gt % 8 == 0), stop=(gt % 8 == 7))
            PEs.inc(ins)
            if gt % 8 == 7:
                t.wait_ge(AEs.h, ae_at4[("vev", gr)])
                t.matmul(pF[:], ones1[:], vr2[0:1, 0:512], start=True, stop=True)
                ins = t.matmul(pG[:], ones1[:], vr2[0:1, 512:1024], start=True, stop=True)
                PEs.inc(ins)

    @block.scalar
    def _(s):
        for gt in range(64):
            gr = gt // 8
            s.wait_ge(PEs.h, pe_at4[("gmm", gt)])
            if gt >= 2:
                s.wait_ge(VEs.h, ve_at4[("yG", gt - 2)])
            a = pA if gt % 2 == 0 else pC
            b = pB if gt % 2 == 0 else pD
            s.activation(sgb[gt % 2][:, 0:512], a[:], AF.Silu)
            ins = s.activation(sgb[gt % 2][:, 512:1024], b[:], AF.Silu)
            AEs.inc(ins)
            if gt % 8 == 7:
                s.wait_ge(PEs.h, pe_at4[("vmm", gt)])
                s.copy(vr2[0:1, 0:512], pF[0:1, :])
                s.copy(vr2[0:1, 512:1024], pG[0:1, :])
                s.activation(vr2[:], vr2[:], AF.Ln, scale=1.0 / 1024.0, bias=EPS)
                ins = s.activation(vr2[:], vr2[:], AF.Exp, scale=-0.5)
                AEs.inc(ins)
                s.wait_ge(PEs.h, pe_at4[("rbc", gr)])
                s.copy(rsb[:, 0:512], pF[:])
                ins = s.copy(rsb[:, 512:1024], pG[:])
                AEs.inc(ins)
                for q in range(8):
                    gq = 8 * gr + q
                    s.wait_ge(VEs.h, ve_at4[("ynm", gq)])
                    ins = s.copy(lrow[:], lrow[:])
                    AEs.inc(ins)
                    ins = s.dma_start(out=yo_d[128 * gq:128 * (gq + 1), :], in_=yob[gq % 2][:])
                    SP.inc(ins, 16)
                if DEBUG:
                    for q in range(8):
                        gq = 8 * gr + q
                        ins = s.dma_start(out=dbg["dbg_yg"][128 * gq:128 * (gq + 1), :], in_=yGg[:, T * q:T * (q + 1)])
                        DBG.inc(ins, 16)
                    ins = s.dma_start(out=dbg["dbg_rs"][128 * gr:128 * (gr + 1), :], in_=rsb[:])
                    DBG.inc(ins, 16)
    assert AEs.n == ae5 + len(ae_seq4)

    @block.vector
    def _(v):
        for gt in range(64):
            gr = gt // 8
            v.wait_ge(LX.h, lx5 + 32 + 32 * (gt + 1))
            v.wait_ge(LW.h, lw5 + 16)
            ins = v.scalar_tensor_tensor(y2b[gt % 2][:], xre[gt % 3][:], s_dex[:, gt:gt + 1],
                                         ytl[gt % 3][:], ALU.mult, ALU.add)
            VEs.inc(ins)
            v.wait_ge(AEs.h, ae_at4[("silu", gt)])
            if gt >= 16:
                v.wait_ge(AEs.h, ae_at4[("nw", gt - 16)])
            ins = v.tensor_tensor(yGg[:, T * (gt % 8):T * (gt % 8 + 1)], y2b[gt % 2][:], sgb[gt % 2][:], ALU.mult)
            VEs.inc(ins)
            if gt >= 2:
                v.wait_ge(PEs.h, pe_at4[("vmm", gt - 2)])
            ins = v.tensor_tensor(ysq[gt % 2][:], yGg[:, T * (gt % 8):T * (gt % 8 + 1)],
                                  yGg[:, T * (gt % 8):T * (gt % 8 + 1)], ALU.mult)
            VEs.inc(ins)
            if gt % 8 == 7:
                v.wait_ge(AEs.h, ae_at4[("rbce", gr)])
                for q in range(8):
                    gq = 8 * gr + q
                    if gq >= 2:
                        v.wait_ge(SP.h, sp5 + 16 * (gq - 1))
                    ins = v.scalar_tensor_tensor(yob[gq % 2][:], yGg[:, T * q:T * (q + 1)],
                                                 s_nw[:, gq:gq + 1], rsb[:], ALU.mult, ALU.mult)
                    VEs.inc(ins)
    assert VEs.n == ve5 + len(ve_seq4)

    st4a.close()
    barrier()

    st4b = ExitStack()
    xnh = sb(st4b, "xnh", [128, 32 * 512], BF16)
    wob = [sb(st4b, f"wob{i}", [128, 8192], BF16) for i in range(2)]
    tb = [sb(st4b, f"tb{i}", [128, 4096], BF16) for i in range(2)]
    scb = [sb(st4b, f"scb{i}", [128, 4096], BF16) for i in range(2)]
    fb = [sb(st4b, f"fb{i}", [128, 4096], BF16) for i in range(2)]
    xrb = [sb(st4b, f"xrb{i}", [128, 512], F32) for i in range(2)]
    oat = [sb(st4b, f"oat{i}", [128, 512], F32) for i in range(4)]

    lx6, pe6, ae6, ve6, lw6, sp6, lwb6 = LX.n, PEs.n, AEs.n, VEs.n, LW.n, SP.n, LWB.n

    lx_xr = {}
    cnt = lx6
    lxh = []
    lxff = []
    for th in range(2):
        cnt += (32 if th == 0 else 96) * 16
        lxh.append(cnt)
        for ot in range(32):
            cnt += 16
            lx_xr[(th, ot)] = cnt
    lxff = [lxh[0], lxh[1]]

    @block.sync
    def _(sy):
        sy.wait_ge(SP.h, sp6)
        for th in range(2):
            if th == 1:
                sy.wait_ge(PEs.h, pe6 + 32)
                for k in range(64):
                    ins = sy.dma_start(out=yh[:, 512 * k:512 * (k + 1)],
                                       in_=yo_d[128 * k:128 * (k + 1), 512:1024])
                    LX.inc(ins, 16)
            for k in range(32):
                ins = sy.dma_start(out=xnh[:, 512 * k:512 * (k + 1)],
                                   in_=xn_d[0:128, TH * k + 3 + 512 * th:TH * k + 3 + 512 * (th + 1)])
                LX.inc(ins, 16)
            for ot in range(32):
                i = 32 * th + ot
                if i >= 2:
                    sy.wait_ge(VEs.h, ve6 + 2 * (i - 1))
                ins = sy.dma_start(out=xrb[i % 2][:], in_=xt[128 * ot:128 * (ot + 1), 3 + 512 * th:3 + 512 * (th + 1)])
                LX.inc(ins, 16)

    @block.gpsimd
    def _(g):
        g.wait_ge(SP.h, sp6)
        for th in range(2):
            for ot in range(32):
                i = 32 * th + ot
                if i >= 2:
                    g.wait_ge(PEs.h, pe6 + i - 1)
                ins = g.dma_start(out=wob[i % 2][:], in_=wop[ot])
                LWB.inc(ins, 16)
                if i >= 2:
                    g.wait_ge(VEs.h, ve6 + 2 * (i - 2) + 1)
                ins = g.dma_start(out=tb[i % 2][:], in_=tern[ot])
                LW.inc(ins, 16)
                ins = g.dma_start(out=scb[i % 2][:], in_=scf[ot])
                LW.inc(ins, 16)

    @block.vector
    def _(v):
        for th in range(2):
            for ot in range(32):
                i = 32 * th + ot
                v.wait_ge(LW.h, lw6 + 32 * (i + 1))
                if i >= 2:
                    v.wait_ge(PEs.h, pe6 + i - 1)
                ins = v.tensor_tensor(fb[i % 2][:], tb[i % 2][:], scb[i % 2][:], ALU.mult)
                VEs.inc(ins)
                v.wait_ge(PEs.h, pe6 + i + 1)
                v.wait_ge(LX.h, lx_xr[(th, ot)])
                if i >= 4:
                    v.wait_ge(STO.h, 16 * (i - 3))
                p = pA if i % 2 == 0 else pB
                ins = v.tensor_tensor(oat[i % 4][:], p[:], xrb[i % 2][:], ALU.add)
                VEs.inc(ins)

    @block.tensor
    def _(t):
        for th in range(2):
            t.wait_ge(LX.h, lxh[th])
            for ot in range(32):
                i = 32 * th + ot
                t.wait_ge(LWB.h, lwb6 + 16 * (i + 1))
                t.wait_ge(VEs.h, ve6 + 2 * i + 1)
                if i >= 2:
                    t.wait_ge(VEs.h, ve6 + 2 * (i - 1))
                p = pA if i % 2 == 0 else pB
                w = wob[i % 2]
                f = fb[i % 2]
                for k in range(64):
                    t.matmul(p[:], w[:, 128 * k:128 * (k + 1)], yh[:, 512 * k:512 * (k + 1)],
                             start=(k == 0), stop=False)
                for k2 in range(32):
                    ins = t.matmul(p[:], f[:, 128 * k2:128 * (k2 + 1)], xnh[:, 512 * k2:512 * (k2 + 1)],
                                   start=False, stop=(k2 == 31))
                PEs.inc(ins)

    @block.scalar
    def _(s):
        for th in range(2):
            for ot in range(32):
                i = 32 * th + ot
                s.wait_ge(VEs.h, ve6 + 2 * (i + 1))
                ins = s.dma_start(out=outT[128 * ot:128 * (ot + 1), 512 * th:512 * (th + 1)], in_=oat[i % 4][:])
                STO.inc(ins, 16)
        s.wait_ge(STO.h, STO.n)

    if DEBUG:
        @block.gpsimd
        def _(g):
            g.wait_ge(STO.h, 1024)
            g.dma_start(out=dbg["dbg_yo"], in_=yo_d).then_inc(DBG.h, 16)
            g.wait_ge(DBG.h, DBG.n + 16)
        DBG.n += 16

    st4b.close()
    st4.close()
    return nc, st


def _prep(inputs):
    x = np.asarray(inputs["L0_attn_hidden_states"])[0]
    W_in = np.asarray(inputs["L0_attn_in_proj_w"])
    conv_w = np.asarray(inputs["L0_attn_conv_w"])
    conv_b = np.asarray(inputs["L0_attn_conv_b"])
    dt_bias = np.asarray(inputs["L0_attn_dt_bias"])
    A_log = np.asarray(inputs["L0_attn_A_log"])
    D = np.asarray(inputs["L0_attn_D"])
    norm_w = np.asarray(inputs["L0_attn_norm_w"])
    W_out = np.asarray(inputs["L0_attn_out_proj_w"])
    ternary = np.asarray(inputs["L0_ffn_ternary"])
    scales = np.asarray(inputs["L0_ffn_scales"])

    xT = np.ascontiguousarray(x.T.astype(np.float32))
    wp = np.ascontiguousarray(W_in.astype(np.float32).reshape(N_OT, 128, 32, 128).transpose(0, 3, 2, 1)).reshape(N_OT, 128, 4096)
    wopk = np.ascontiguousarray(W_out.astype(np.float32).reshape(32, 128, 64, 128).transpose(0, 3, 2, 1)).reshape(32, 128, 8192)
    ternk = np.ascontiguousarray(ternary.reshape(32, 128, 32, 128).transpose(0, 3, 2, 1)).reshape(32, 128, 4096)
    s2 = scales.astype(np.float32).reshape(4096, 32)
    s3 = np.ascontiguousarray(s2.reshape(32, 128, 32).transpose(0, 2, 1))
    scfk = np.ascontiguousarray(np.broadcast_to(s3[:, None, :, :], (32, 128, 32, 128)).reshape(32, 128, 4096))
    cwp = np.ascontiguousarray(conv_w.astype(np.float32).reshape(N_XBC, 128, 4).transpose(1, 0, 2)).reshape(128, N_XBC * 4)
    cbp = np.ascontiguousarray(conv_b.astype(np.float32).reshape(N_XBC, 128).transpose(1, 0))
    dtb = np.ascontiguousarray(dt_bias.astype(np.float32).reshape(128, 1))
    alg_ = np.ascontiguousarray(A_log.astype(np.float32).reshape(128, 1))
    dexp = np.empty((128, 64), np.float32)
    nwp_ = np.empty((128, 64), np.float32)
    ar = np.arange(128)
    for tt in range(64):
        ch = 128 * tt + ar
        dexp[:, tt] = D[ch // 64]
        nwp_[:, tt] = norm_w[ch]
    in_maps = []
    for g in range(NCORE):
        lo = g * T
        if g == 0:
            xs = np.concatenate([np.zeros((HID, 3), np.float32), xT[:, 0:T]], axis=1)
        else:
            xs = np.ascontiguousarray(xT[:, lo - 3:lo + T])
        m2v = (np.arange(8) < g).astype(np.float32).reshape(8, 1)
        d = dict(xt=xs, wp=wp, wop=wopk, tern=ternk, scf=scfk,
                 cwi=cwp, cbi=cbp, dtbi=dtb, alogi=alg_, dexpi=dexp, nwi=nwp_,
                 m2i=np.ascontiguousarray(m2v), m2ci=np.ascontiguousarray(1.0 - m2v),
                 m2hi=np.ascontiguousarray(np.broadcast_to(m2v.reshape(1, 8), (128, 8))))
        in_maps.append(d)
    return in_maps


def kernel(**inputs):
    in_maps = _prep(inputs)
    nc, st = build_nc()
    try:
        res = run_bass_kernel_spmd(nc, in_maps, core_ids=list(range(NCORE)))
    finally:
        st.close()
    outs = [np.asarray(res.results[g]["outT"]) for g in range(NCORE)]
    outT = np.concatenate(outs, axis=1)
    out = np.ascontiguousarray(outT.T).reshape(1, 8192, HID).astype(np.float32)
    if DEBUG:
        kernel._last = res
    return out


# version 12
# speedup vs baseline: 1.0217x; 1.0083x over previous
import numpy as np
import ml_dtypes
from contextlib import ExitStack

import concourse.bass as bass
import concourse.mybir as mybir
from concourse.bass_utils import run_bass_kernel_spmd

F32 = mybir.dt.float32
BF16 = mybir.dt.bfloat16
I8 = mybir.dt.int8
AF = mybir.ActivationFunctionType
ALU = mybir.AluOpType

HID = 4096
T = 1024
TH = T + 3
NH = 128
HD = 64
DIN = 8192
G = 8
NST = 128
CS = 128
NCH = T // CS
N_OT = 145
XBC0 = 64
N_XBC = 80
NCORE = 8
EPS = 1e-5

DEBUG = False
CONSTS = (1e-5, -0.5, -1.0, 1.0, 1.0 / 1024.0)


def bv(ap2d, p, k, reps):
    return ap2d.unsqueeze(2).broadcast_to((p, k, reps))


def build_nc():
    nc = bass.Bass("TRN2", target_bir_lowering=False)

    xt = nc.dram_tensor("xt", [HID, TH], F32, kind="ExternalInput").ap()
    wp = nc.dram_tensor("wp", [N_OT, 128, 4096], F32, kind="ExternalInput").ap()
    wop = nc.dram_tensor("wop", [32, 128, 8192], F32, kind="ExternalInput").ap()
    tern = nc.dram_tensor("tern", [32, 128, 4096], I8, kind="ExternalInput").ap()
    scf = nc.dram_tensor("scf", [32, 128, 4096], F32, kind="ExternalInput").ap()
    cwi = nc.dram_tensor("cwi", [128, N_XBC * 4], F32, kind="ExternalInput").ap()
    cbi = nc.dram_tensor("cbi", [128, N_XBC], F32, kind="ExternalInput").ap()
    dtbi = nc.dram_tensor("dtbi", [128, 1], F32, kind="ExternalInput").ap()
    alogi = nc.dram_tensor("alogi", [128, 1], F32, kind="ExternalInput").ap()
    dexpi = nc.dram_tensor("dexpi", [128, 64], F32, kind="ExternalInput").ap()
    nwi = nc.dram_tensor("nwi", [128, 64], F32, kind="ExternalInput").ap()
    m2i = nc.dram_tensor("m2i", [8, 1], F32, kind="ExternalInput").ap()
    m2ci = nc.dram_tensor("m2ci", [8, 1], F32, kind="ExternalInput").ap()
    m2hi = nc.dram_tensor("m2hi", [128, 8], F32, kind="ExternalInput").ap()

    outT = nc.dram_tensor("outT", [HID, T], F32, kind="ExternalOutput").ap()

    x_cm = nc.dram_tensor("x_cm", [DIN, T], BF16).ap()
    b_cm = nc.dram_tensor("b_cm", [G * NST, T], BF16).ap()
    xn_d = nc.dram_tensor("xn_d", [128, 32 * TH], BF16).ap()
    xs_d = nc.dram_tensor("xs_d", [NCH, 128, DIN], BF16).ap()
    st_d = nc.dram_tensor("st_d", [NCH, 128, DIN], BF16).ap()
    yf_d = nc.dram_tensor("yf_d", [T, DIN], BF16).ap()
    yo_d = nc.dram_tensor("yo_d", [DIN, T], BF16).ap()
    cfl_d = nc.dram_tensor("cfl_d", [NCH, CS * NH], F32).ap()
    contrib = nc.dram_tensor("contrib", [128, DIN], BF16).ap()
    gath = nc.dram_tensor("gath", [NCORE * 128, DIN], BF16).ap()
    alc = nc.dram_tensor("alc", [1, 128], F32).ap()
    wob_d = nc.dram_tensor("wob_d", [32, 128, 8192], BF16).ap()
    alg = nc.dram_tensor("alg", [NCORE, 128], F32).ap()
    dbg = {}
    if DEBUG:
        for nm, shape, dty in [("dbg_xcm", [DIN, T], BF16), ("dbg_bcm", [G * NST, T], BF16),
                               ("dbg_yf", [T, DIN], BF16), ("dbg_yo", [DIN, T], BF16),
                               ("dbg_st", [NCH * 128, DIN], BF16), ("dbg_cum", [128, T], F32),
                               ("dbg_gath", [NCORE * 128, DIN], BF16), ("dbg_xn", [128, 32 * TH], BF16),
                               ("dbg_dt", [128, T], F32), ("dbg_yg", [DIN, T], BF16),
                               ("dbg_rs", [8 * 128, T], BF16)]:
            dbg[nm] = nc.dram_tensor(nm, shape, dty, kind="ExternalOutput").ap()

    st = ExitStack()

    def sb(stk, name, shape, dtype):
        return stk.enter_context(nc.sbuf_tensor(name, shape, dtype))

    ones1 = sb(st, "ones1", [1, 128], F32)
    onescol = sb(st, "onescol", [128, 1], F32)
    onescol_bf = sb(st, "onescol_bf", [128, 1], BF16)
    eye = sb(st, "eye", [128, 128], F32)
    umask = sb(st, "umask", [128, 128], F32)
    scr_a = sb(st, "scr_a", [128, 128], F32)
    scr_b = sb(st, "scr_b", [128, 128], F32)
    s_dtb = sb(st, "s_dtb", [128, 1], F32)
    s_acol = sb(st, "s_acol", [128, 1], F32)
    alsum = sb(st, "alsum", [1, 128], F32)
    lrow = sb(st, "lrow", [1, 128], F32)
    s_m2 = sb(st, "s_m2", [8, 1], F32)
    s_m2c = sb(st, "s_m2c", [8, 1], F32)
    s_m2h = sb(st, "s_m2h", [128, 8], F32)
    s_ag = sb(st, "s_ag", [8, 128], F32)
    a_hm = sb(st, "a_hm", [128, 8], F32)
    whm = sb(st, "whm", [128, 8], F32)
    wexp = [sb(st, f"wexp{i}", [128, 128], BF16) for i in range(2)]
    cst = sb(st, "cst", [128, 8], F32)
    dmy = sb(st, "dmy", [1, 8], F32)

    st23 = ExitStack()
    cumT = sb(st23, "cumT", [128, T], F32)
    dtT_bf = sb(st23, "dtT_bf", [128, T], BF16)
    sd_f = sb(st23, "sd_f", [128, T], F32)
    dc_bf = sb(st23, "dc_bf", [128, T], BF16)
    al_bf = sb(st23, "al_bf", [128, T], BF16)
    bcm = sb(st23, "bcm", [128, G * T], BF16)
    ccm = sb(st23, "ccm", [128, G * T], BF16)

    pA = st.enter_context(nc.psum_tensor("pA", [128, 512], F32))
    pB = st.enter_context(nc.psum_tensor("pB", [128, 512], F32))
    pC = st.enter_context(nc.psum_tensor("pC", [128, 512], F32))
    pD = st.enter_context(nc.psum_tensor("pD", [128, 512], F32))
    pE = st.enter_context(nc.psum_tensor("pE", [128, 512], F32))
    pF = st.enter_context(nc.psum_tensor("pF", [128, 512], F32))
    pG = st.enter_context(nc.psum_tensor("pG", [128, 512], F32))
    pH = st.enter_context(nc.psum_tensor("pH", [128, 512], F32))

    class S:
        def __init__(self, name):
            self.h = st.enter_context(nc.semaphore(name))
            self.n = 0

        def inc(self, instr, k=1):
            instr.then_inc(self.h, k)
            self.n += k
            return self.n

    LW = S("LW")
    LX = S("LX")
    SP = S("SP")
    CC = S("CC")
    PEs = S("PEs")
    VEs = S("VEs")
    AEs = S("AEs")
    GPs = S("GPs")
    BAR = S("BAR")
    STO = S("STO")
    DBG = S("DBG")
    WCL = S("WCL")
    WCS = S("WCS")
    LWB = S("LWB")

    block = st.enter_context(nc.Block())
    barn = [0]

    def barrier():
        barn[0] += 5
        tgt = barn[0]
        for reg in (block.tensor, block.vector, block.scalar, block.gpsimd, block.sync):
            @reg
            def _(e):
                e.sem_inc(BAR.h, 1)
                e.wait_ge(BAR.h, tgt)
        BAR.n = tgt

    @block.gpsimd
    def _(g):
        g.iota(scr_a[:], [[0, 128]], channel_multiplier=1, allow_small_or_imprecise_dtypes=True)
        g.iota(scr_b[:], [[1, 128]], channel_multiplier=0, allow_small_or_imprecise_dtypes=True)
        g.memset(ones1[:], 1.0)
        g.memset(onescol[:], 1.0)
        g.memset(onescol_bf[:], 1.0)
        g.memset(alsum[:], 0.0)
        ins = g.memset(whm[:], 0.0)
        GPs.inc(ins)
        g.dma_start(out=s_dtb[:], in_=dtbi).then_inc(LW.h, 16)
        g.dma_start(out=s_acol[:], in_=alogi).then_inc(LW.h, 16)
        g.dma_start(out=s_m2[:], in_=m2i).then_inc(LW.h, 16)
        g.dma_start(out=s_m2c[:], in_=m2ci).then_inc(LW.h, 16)
        g.dma_start(out=s_m2h[:], in_=m2hi).then_inc(LW.h, 16)
        for ci, cv in enumerate(CONSTS):
            ins = g.memset(cst[:, ci:ci + 1], cv)
        GPs.inc(ins)
    LW.n += 80
    for ci, cv in enumerate(CONSTS):
        nc.const_aps.register(F32, cv, cst[:, ci:ci + 1])

    @block.vector
    def _(v):
        v.wait_ge(GPs.h, GPs.n)
        v.tensor_tensor(eye[:], scr_a[:], scr_b[:], ALU.is_equal)
        ins = v.tensor_tensor(umask[:], scr_a[:], scr_b[:], ALU.is_le)
        VEs.inc(ins)

    @block.scalar
    def _(s):
        s.wait_ge(LW.h, 80)
        s.wait_ge(GPs.h, GPs.n)
        s.activation(s_acol[:], s_acol[:], AF.Exp)
        ins = s.mul(s_acol[:], s_acol[:], -1.0)
        AEs.inc(ins)

    barrier()

    st12 = ExitStack()
    xn_cm = sb(st12, "xn_cm", [128, 32 * TH], BF16)
    st1 = ExitStack()
    xbuf = [sb(st1, f"xbuf{i}", [128, TH], F32) for i in range(2)]
    sqb = [sb(st1, f"sqb{i}", [128, TH], BF16) for i in range(2)]
    xbb = [sb(st1, f"xbb{i}", [128, TH], BF16) for i in range(2)]
    mrow = sb(st1, "mrow", [1, TH], F32)
    vrow = sb(st1, "vrow", [1, TH], F32)
    mbc = sb(st1, "mbc", [128, TH], BF16)
    rbc = sb(st1, "rbc", [128, TH], BF16)
    xb2 = [sb(st1, f"xb2{i}", [128, TH], BF16) for i in range(2)]
    tmp1 = sb(st1, "tmp1", [128, TH], F32)

    lx0, ae0, pe0, ve0, lw0 = LX.n, AEs.n, PEs.n, VEs.n, LW.n
    ld_tile1 = {}

    @block.sync
    def _(sy):
        for i in range(0, 32, 2):
            if i >= 2:
                sy.wait_ge(PEs.h, pe0 + i - 1)
                sy.wait_ge(AEs.h, ae0 + i - 1)
            ins = sy.dma_start(out=xbuf[i % 2][:], in_=xt[128 * i:128 * (i + 1), :])
            LX.inc(ins, 16)
            ld_tile1[i] = (LX, LX.n)

    @block.gpsimd
    def _(g):
        for i in range(1, 32, 2):
            if i >= 2:
                g.wait_ge(PEs.h, pe0 + i - 1)
                g.wait_ge(AEs.h, ae0 + i - 1)
            ins = g.dma_start(out=xbuf[i % 2][:], in_=xt[128 * i:128 * (i + 1), :])
            LW.inc(ins, 16)
            ld_tile1[i] = (LW, LW.n)

    @block.scalar
    def _(s):
        for i in range(32):
            sm, val = ld_tile1[i]
            s.wait_ge(sm.h, val)
            if i >= 2:
                s.wait_ge(PEs.h, pe0 + i - 1)
            s.activation(sqb[i % 2][:], xbuf[i % 2][:], AF.Square)
            ins = s.copy(xbb[i % 2][:], xbuf[i % 2][:])
            AEs.inc(ins)

    @block.tensor
    def _(t):
        for i in range(32):
            s0, s1 = (i == 0), (i == 31)
            sm, val = ld_tile1[i]
            t.wait_ge(sm.h, val)
            t.wait_ge(AEs.h, ae0 + i + 1)
            t.matmul(pA[0:1, :], onescol_bf[:], xbb[i % 2][:, 0:512], start=s0, stop=s1)
            t.matmul(pB[0:1, :], onescol_bf[:], xbb[i % 2][:, 512:1024], start=s0, stop=s1)
            t.matmul(pH[0:1, 0:3], onescol_bf[:], xbb[i % 2][:, 1024:1027], start=s0, stop=s1)
            t.matmul(pC[0:1, :], onescol_bf[:], sqb[i % 2][:, 0:512], start=s0, stop=s1)
            t.matmul(pD[0:1, :], onescol_bf[:], sqb[i % 2][:, 512:1024], start=s0, stop=s1)
            ins = t.matmul(pE[0:1, 0:3], onescol_bf[:], sqb[i % 2][:, 1024:1027], start=s0, stop=s1)
            PEs.inc(ins)

    @block.scalar
    def _(s):
        s.wait_ge(PEs.h, pe0 + 32)
        s.copy(mrow[0:1, 0:512], pA[0:1, :])
        s.copy(mrow[0:1, 512:1024], pB[0:1, :])
        s.copy(mrow[0:1, 1024:1027], pH[0:1, 0:3])
        s.copy(vrow[0:1, 0:512], pC[0:1, :])
        s.copy(vrow[0:1, 512:1024], pD[0:1, :])
        ins = s.copy(vrow[0:1, 1024:1027], pE[0:1, 0:3])
        AEs.inc(ins)

    @block.vector
    def _(v):
        v.wait_ge(AEs.h, ae0 + 33)
        v.tensor_scalar(mrow[:], mrow[:], 1.0 / HID, None, ALU.mult)
        v.tensor_scalar(vrow[:], vrow[:], 1.0 / HID, None, ALU.mult)
        v.tensor_tensor(tmp1[0:1, :], mrow[:], mrow[:], ALU.mult)
        ins = v.tensor_tensor(vrow[:], vrow[:], tmp1[0:1, :], ALU.subtract)
        VEs.inc(ins)

    @block.scalar
    def _(s):
        s.wait_ge(VEs.h, ve0 + 1)
        s.activation(vrow[:], vrow[:], AF.Ln, bias=EPS)
        ins = s.activation(vrow[:], vrow[:], AF.Exp, scale=-0.5)
        AEs.inc(ins)

    @block.tensor
    def _(t):
        t.wait_ge(AEs.h, ae0 + 34)
        t.matmul(pA[:], ones1[:], mrow[0:1, 0:512], start=True, stop=True)
        t.matmul(pB[:], ones1[:], mrow[0:1, 512:1024], start=True, stop=True)
        t.matmul(pH[:, 0:3], ones1[:], mrow[0:1, 1024:1027], start=True, stop=True)
        t.matmul(pC[:], ones1[:], vrow[0:1, 0:512], start=True, stop=True)
        t.matmul(pD[:], ones1[:], vrow[0:1, 512:1024], start=True, stop=True)
        ins = t.matmul(pE[:, 0:3], ones1[:], vrow[0:1, 1024:1027], start=True, stop=True)
        PEs.inc(ins)

    @block.scalar
    def _(s):
        s.wait_ge(PEs.h, pe0 + 33)
        s.copy(mbc[:, 0:512], pA[:])
        s.copy(mbc[:, 512:1024], pB[:])
        s.copy(mbc[:, 1024:1027], pH[:, 0:3])
        s.copy(rbc[:, 0:512], pC[:])
        s.copy(rbc[:, 512:1024], pD[:])
        ins = s.copy(rbc[:, 1024:1027], pE[:, 0:3])
        AEs.inc(ins)

    lx1, ve1 = LX.n, VEs.n
    ld_tile2 = {}

    @block.sync
    def _(sy):
        for i in range(0, 32, 2):
            if i >= 2:
                sy.wait_ge(VEs.h, ve1 + i - 1)
            else:
                sy.wait_ge(PEs.h, pe0 + 32)
                sy.wait_ge(AEs.h, ae0 + 32)
            ins = sy.dma_start(out=xbuf[i % 2][:], in_=xt[128 * i:128 * (i + 1), :])
            LX.inc(ins, 16)
            ld_tile2[i] = (LX, LX.n)

    @block.gpsimd
    def _(g):
        for i in range(1, 32, 2):
            if i >= 2:
                g.wait_ge(VEs.h, ve1 + i - 1)
            else:
                g.wait_ge(PEs.h, pe0 + 32)
                g.wait_ge(AEs.h, ae0 + 32)
            ins = g.dma_start(out=xbuf[i % 2][:], in_=xt[128 * i:128 * (i + 1), :])
            LW.inc(ins, 16)
            ld_tile2[i] = (LW, LW.n)

    @block.scalar
    def _(s):
        for i in range(32):
            sm, val = ld_tile2[i]
            s.wait_ge(sm.h, val)
            if i >= 2:
                s.wait_ge(VEs.h, ve1 + i - 1)
            ins = s.copy(xb2[i % 2][:], xbuf[i % 2][:])
            AEs.inc(ins)

    @block.vector
    def _(v):
        v.wait_ge(AEs.h, ae0 + 35)
        for i in range(32):
            v.wait_ge(AEs.h, ae0 + 35 + i + 1)
            v.tensor_tensor(tmp1[:].bitcast(BF16)[:, 0:TH], xb2[i % 2][:], mbc[:], ALU.subtract)
            ins = v.tensor_tensor(xn_cm[:, TH * i:TH * (i + 1)], tmp1[:].bitcast(BF16)[:, 0:TH], rbc[:], ALU.mult)
            VEs.inc(ins)

    @block.gpsimd
    def _(g):
        g.wait_ge(VEs.h, ve1 + 32)
        ins = g.dma_start(out=xn_d, in_=xn_cm[:])
        SP.inc(ins, 16)

    st1.close()
    barrier()

    st2 = ExitStack()
    wbuf = [sb(st2, f"wbuf{i}", [128, 4096], BF16) for i in range(3)]
    zx = [sb(st2, f"zx{i}", [128, TH], F32) for i in range(2)]
    cacc = [sb(st2, f"cacc{i}", [128, T], F32) for i in range(2)]
    xout = [sb(st2, f"xout{i}", [128, T], BF16) for i in range(2)]
    s_cw = sb(st2, "s_cw", [128, N_XBC * 4], F32)
    s_cb = sb(st2, "s_cb", [128, N_XBC], F32)
    dt_f = sb(st2, "dt_f", [128, T], F32)
    adt = sb(st2, "adt", [128, T], F32)
    cum_hm = sb(st2, "cum_hm", [128, T], F32)
    tdt = sb(st2, "tdt", [128, T], F32)
    z128 = sb(st2, "z128", [128, 128], F32)

    lw2, pe2, ae2, ve2, sp2 = LW.n, PEs.n, AEs.n, VEs.n, SP.n
    NT2 = N_XBC + 1

    act_seq = []
    for i in range(NT2):
        act_seq.append(("evac", i))
        act_seq.append(("c0", i))
        if i >= 1 and i - 1 < N_XBC:
            act_seq.append(("silu", i - 1))
    act_at = {tg: ae2 + k + 1 for k, tg in enumerate(act_seq)}

    @block.gpsimd
    def _(g):
        g.memset(z128[:], 0.0)
        g.dma_start(out=s_cw[:], in_=cwi).then_inc(LW.h, 16)
        g.dma_start(out=s_cb[:], in_=cbi).then_inc(LW.h, 16)
        for i in range(NT2):
            if i >= 3:
                g.wait_ge(PEs.h, pe2 + i - 2)
            ins = g.dma_start(out=wbuf[i % 3][:], in_=wp[XBC0 + i])
            LW.inc(ins, 16)
    LW.n += 32
    lw2b = lw2 + 32

    @block.sync
    def _(sy):
        for j in range(64):
            sy.wait_ge(AEs.h, act_at[("silu", j)])
            ins = sy.dma_start(out=x_cm[128 * j:128 * (j + 1), :], in_=xout[j % 2][:])
            SP.inc(ins, 16)
        sy.wait_ge(AEs.h, act_at[("silu", 71)])
        for gi in range(8):
            ins = sy.dma_start(out=b_cm[NST * gi:NST * (gi + 1), :], in_=bcm[:, T * gi:T * (gi + 1)])
            SP.inc(ins, 16)

    @block.tensor
    def _(t):
        psets = [(pA, pB), (pC, pD)]
        halos = [pE, pH]
        for i in range(NT2):
            t.wait_ge(LW.h, lw2b + 16 * (i + 1))
            if i >= 2:
                t.wait_ge(AEs.h, act_at[("evac", i - 2)])
            a, b = psets[i % 2]
            hp = halos[i % 2]
            w = wbuf[i % 3]
            if i < N_XBC:
                for k in range(32):
                    s0, s1 = (k == 0), (k == 31)
                    t.matmul(a[:], w[:, 128 * k:128 * (k + 1)], xn_cm[:, TH * k:TH * k + 512], start=s0, stop=s1)
                    t.matmul(b[:], w[:, 128 * k:128 * (k + 1)], xn_cm[:, TH * k + 512:TH * k + 1024], start=s0, stop=s1)
                    ins = t.matmul(hp[:, 0:3], w[:, 128 * k:128 * (k + 1)],
                                   xn_cm[:, TH * k + 1024:TH * k + 1027], start=s0, stop=s1)
            else:
                for k in range(32):
                    s0, s1 = (k == 0), (k == 31)
                    t.matmul(a[:], w[:, 128 * k:128 * (k + 1)], xn_cm[:, TH * k + 3:TH * k + 515], start=s0, stop=s1)
                    ins = t.matmul(b[:], w[:, 128 * k:128 * (k + 1)], xn_cm[:, TH * k + 515:TH * k + 1027], start=s0, stop=s1)
            PEs.inc(ins)

    @block.scalar
    def _(s):
        psets = [(pA, pB), (pC, pD)]
        halos = [pE, pH]
        for i in range(NT2):
            s.wait_ge(PEs.h, pe2 + i + 1)
            if i >= 2:
                s.wait_ge(VEs.h, ve2 + i - 1)
            a, b = psets[i % 2]
            hp = halos[i % 2]
            z = zx[i % 2]
            s.copy(z[:, 0:512], a[:])
            s.copy(z[:, 512:1024], b[:])
            if i < N_XBC:
                ins = s.copy(z[:, 1024:1027], hp[:, 0:3])
                AEs.inc(ins)
                ins = s.activation(cacc[i % 2][:], z[:, 0:T], AF.Copy, scale=s_cw[:, 4 * i:4 * i + 1])
                AEs.inc(ins)
            else:
                ins = s.copy(z[:, 1024:1027], z[:, 1024:1027])
                AEs.inc(ins)
                ins = s.activation(cacc[i % 2][:], z[:, 0:T], AF.Copy)
                AEs.inc(ins)
            if i >= 1 and i - 1 < N_XBC:
                j = i - 1
                s.wait_ge(VEs.h, ve2 + j + 1)
                if j < 64:
                    s.wait_ge(SP.h, sp2 + 16 * max(0, j - 1))
                    dst = xout[j % 2][:]
                elif j < 72:
                    dst = bcm[:, T * (j - 64):T * (j - 63)]
                else:
                    dst = ccm[:, T * (j - 72):T * (j - 71)]
                ins = s.activation(dst, cacc[j % 2][:], AF.Silu, bias=s_cb[:, j:j + 1])
                AEs.inc(ins)
    assert AEs.n == ae2 + len(act_seq), (AEs.n, ae2, len(act_seq))

    @block.vector
    def _(v):
        for i in range(N_XBC):
            v.wait_ge(AEs.h, act_at[("c0", i)])
            z = zx[i % 2]
            c = cacc[i % 2]
            v.scalar_tensor_tensor(c[:], z[:, 1:1 + T], s_cw[:, 4 * i + 1:4 * i + 2], c[:], ALU.mult, ALU.add)
            v.scalar_tensor_tensor(c[:], z[:, 2:2 + T], s_cw[:, 4 * i + 2:4 * i + 3], c[:], ALU.mult, ALU.add)
            ins = v.scalar_tensor_tensor(c[:], z[:, 3:3 + T], s_cw[:, 4 * i + 3:4 * i + 4], c[:], ALU.mult, ALU.add)
            VEs.inc(ins)
        v.wait_ge(AEs.h, act_at[("c0", N_XBC)])
        u = cacc[N_XBC % 2]
        v.tensor_scalar(tdt[:], u[:], s_dtb[:], None, ALU.add)
        ins = v.tensor_scalar(dt_f[:], tdt[:], 0.0, None, ALU.max)
        VEs.inc(ins)

    ve2b = ve2 + N_XBC + 1

    @block.scalar
    def _(s):
        s.wait_ge(VEs.h, ve2b)
        s.activation(tdt[:], tdt[:], AF.Abs)
        s.activation(tdt[:], tdt[:], AF.Exp, scale=-1.0)
        ins = s.activation(tdt[:], tdt[:], AF.Ln, bias=1.0)
        AEs.inc(ins)
    ae2b = AEs.n

    @block.vector
    def _(v):
        v.wait_ge(AEs.h, ae2b)
        v.tensor_tensor(dt_f[:], dt_f[:], tdt[:], ALU.add)
        ins = v.tensor_scalar(adt[:], dt_f[:], s_acol[:], None, ALU.mult)
        VEs.inc(ins)
        for c in range(NCH):
            ins = v.tensor_tensor_scan(cum_hm[:, CS * c:CS * (c + 1)], adt[:, CS * c:CS * (c + 1)],
                                       z128[:], 0.0, ALU.add, ALU.add)
            VEs.inc(ins)
    ve2c = VEs.n

    peX, aeX, veX = PEs.n, AEs.n, VEs.n
    for c in range(NCH):
        @block.tensor
        def _(t, c=c):
            t.wait_ge(VEs.h, ve2c)
            if c >= 1:
                t.wait_ge(AEs.h, aeX + 3 * c)
            t.transpose(pF[:, 0:128], cum_hm[:, CS * c:CS * (c + 1)], eye[:])
            ins = t.transpose(pG[:, 0:128], dt_f[:, CS * c:CS * (c + 1)], eye[:])
            PEs.inc(ins)
            ins = t.transpose(pE[0:1, 0:128], cum_hm[:, CS * c + 127:CS * c + 128], eye[:])
            PEs.inc(ins)

        @block.scalar
        def _(s, c=c):
            s.wait_ge(PEs.h, peX + 3 * c + 2)
            s.copy(cumT[:, CS * c:CS * (c + 1)], pF[:, 0:128])
            ins = s.copy(dtT_bf[:, CS * c:CS * (c + 1)], pG[:, 0:128])
            AEs.inc(ins)
            ins = s.copy(lrow[:], pE[0:1, 0:128])
            AEs.inc(ins)

        @block.tensor
        def _(t, c=c):
            t.wait_ge(AEs.h, aeX + 3 * c + 2)
            ins = t.matmul(pF[:, 0:128], ones1[:], lrow[:], start=True, stop=True)
            PEs.inc(ins)

        @block.vector
        def _(v, c=c):
            v.wait_ge(PEs.h, peX + 3 * (c + 1))
            v.tensor_tensor(scr_a[:], pF[:, 0:128], cumT[:, CS * c:CS * (c + 1)], ALU.subtract)
            ins = v.tensor_tensor(alsum[:], alsum[:], lrow[:], ALU.add)
            VEs.inc(ins)

        @block.scalar
        def _(s, c=c):
            s.wait_ge(VEs.h, veX + c + 1)
            s.activation(dc_bf[:, CS * c:CS * (c + 1)], scr_a[:], AF.Exp)
            s.activation(al_bf[:, CS * c:CS * (c + 1)], pF[:, 0:128], AF.Exp)
            ins = s.activation(sd_f[:, CS * c:CS * (c + 1)], cumT[:, CS * c:CS * (c + 1)], AF.Exp)
            AEs.inc(ins)

    @block.gpsimd
    def _(g):
        g.wait_ge(VEs.h, ve2c)
        for c in range(NCH):
            ins = g.dma_start(out=cfl_d[c].rearrange("(h i) -> h i", h=128), in_=cum_hm[:, CS * c:CS * (c + 1)])
            SP.inc(ins, 16)

    @block.scalar
    def _(s):
        s.wait_ge(VEs.h, veX + NCH)
        ins = s.activation(alsum[:], alsum[:], AF.Exp)
        AEs.inc(ins)
    aeX2 = AEs.n

    @block.gpsimd
    def _(g):
        g.wait_ge(AEs.h, aeX2)
        ins = g.dma_start(out=alc, in_=alsum[:])
        SP.inc(ins, 16)

    if DEBUG:
        @block.gpsimd
        def _(g):
            g.wait_ge(AEs.h, aeX2)
            g.dma_start(out=dbg["dbg_cum"], in_=cum_hm[:]).then_inc(DBG.h, 16)
            g.dma_start(out=dbg["dbg_dt"], in_=dt_f[:]).then_inc(DBG.h, 16)
        DBG.n += 32

    st2.close()
    st12.close()
    barrier()

    st3 = ExitStack()
    xsb = sb(st3, "xsb", [128, DIN], BF16)
    incb = sb(st3, "incb", [128, DIN], BF16)
    sreb = sb(st3, "sreb", [128, DIN], BF16)
    big1 = sb(st3, "big1", [128, DIN], BF16)
    big2 = sb(st3, "big2", [128, DIN], BF16)
    st3a = ExitStack()
    xTb = sb(st3a, "xTb", [128, DIN], BF16)
    xs2b = sb(st3a, "xs2b", [128, DIN], BF16)
    btb = sb(st3a, "btb", [128, G * NST], BF16)

    lx3, pe3, ae3, ve3, sp3 = LX.n, PEs.n, AEs.n, VEs.n, SP.n
    ve_xs, ve_xs2, ve_sr = {}, {}, {}
    _c = ve3 + 1
    for c in range(NCH):
        _c += 1; ve_xs[c] = _c
        _c += 1; ve_xs2[c] = _c
        if c >= 1:
            _c += 2; ve_sr[c - 1] = _c
    _c += 2; ve_sr[NCH - 1] = _c

    @block.sync
    def _(sy):
        sy.wait_ge(SP.h, sp3)
        for c in range(NCH):
            if c == 1:
                sy.wait_ge(VEs.h, ve_xs[0])
            elif c >= 2:
                sy.wait_ge(VEs.h, ve_sr[c - 2])
            if c >= 1:
                sy.wait_ge(PEs.h, pe3 + 16 * c)
            ins = sy.dma_start(out=xTb[:], in_=x_cm[:, CS * c:CS * (c + 1)], transpose=True)
            LX.inc(ins, 16)
            ins = sy.dma_start(out=btb[:], in_=b_cm[:, CS * c:CS * (c + 1)], transpose=True)
            LX.inc(ins, 16)

    bigs = [big1, big2]

    def sr_step(v, j):
        v.wait_ge(AEs.h, ae3 + 16 * (j + 1))
        ins = v.tensor_tensor(xTb[:].rearrange("p (h d) -> p h d", h=NH),
                              incb[:].rearrange("p (h d) -> p h d", h=NH),
                              bv(al_bf[:, CS * j:CS * (j + 1)], 128, NH, HD), ALU.mult)
        VEs.inc(ins)
        ins = v.tensor_tensor(incb[:], xTb[:], bigs[j % 2][:], ALU.add)
        VEs.inc(ins)

    @block.vector
    def _(v):
        ins = v.memset(incb[:], 0.0)
        VEs.inc(ins)
        for c in range(NCH):
            v.wait_ge(LX.h, lx3 + 32 * (c + 1))
            if c >= 1:
                v.wait_ge(SP.h, sp3 + 32 * c)
            xv = xsb[:].rearrange("p (h d) -> p h d", h=NH)
            ins = v.tensor_tensor(xv, xTb[:].rearrange("p (h d) -> p h d", h=NH),
                                  bv(dtT_bf[:, CS * c:CS * (c + 1)], 128, NH, HD), ALU.mult)
            VEs.inc(ins)
            ins = v.tensor_tensor(xs2b[:].rearrange("p (h d) -> p h d", h=NH), xv,
                                  bv(dc_bf[:, CS * c:CS * (c + 1)], 128, NH, HD), ALU.mult)
            VEs.inc(ins)
            if c >= 1:
                sr_step(v, c - 1)
        sr_step(v, NCH - 1)

    @block.tensor
    def _(t):
        for c in range(NCH):
            t.wait_ge(VEs.h, ve_xs2[c])
            for hb in range(16):
                p = pA if hb % 2 == 0 else pB
                m = 16 * c + hb
                if m >= 2:
                    t.wait_ge(AEs.h, ae3 + m - 1)
                for hh in range(8):
                    h = 8 * hb + hh
                    ins = t.matmul(p[:, 64 * hh:64 * (hh + 1)], btb[:, NST * (h // 16):NST * (h // 16 + 1)],
                                   xs2b[:, HD * h:HD * (h + 1)], start=True, stop=True)
                PEs.inc(ins)

    @block.scalar
    def _(s):
        for c in range(NCH):
            for hb in range(16):
                p = pA if hb % 2 == 0 else pB
                s.wait_ge(PEs.h, pe3 + 16 * c + hb + 1)
                if hb == 0 and c >= 2:
                    s.wait_ge(VEs.h, ve_sr[c - 2])
                if hb == 0 and c >= 1:
                    s.wait_ge(SP.h, sp3 + 32 * c)
                ins = s.copy(bigs[c % 2][:, 512 * hb:512 * (hb + 1)], p[:])
                AEs.inc(ins)

    @block.gpsimd
    def _(g):
        for c in range(NCH):
            g.wait_ge(VEs.h, ve_xs[c])
            ins = g.dma_start(out=xs_d[c], in_=xsb[:])
            SP.inc(ins, 16)
            g.wait_ge(AEs.h, ae3 + 16 * (c + 1))
            ins = g.dma_start(out=st_d[c], in_=bigs[c % 2][:])
            SP.inc(ins, 16)
        g.wait_ge(VEs.h, ve_sr[NCH - 1])
        ins = g.dma_start(out=contrib, in_=incb[:])
        SP.inc(ins, 16)
        g.wait_ge(SP.h, SP.n)
        ins = g.collective_compute("AllGather", ALU.bypass, replica_groups=[list(range(NCORE))],
                                   ins=[contrib.opt()], outs=[gath.opt()])
        CC.inc(ins)
        ins = g.collective_compute("AllGather", ALU.bypass, replica_groups=[list(range(NCORE))],
                                   ins=[alc.opt()], outs=[alg.opt()])
        CC.inc(ins)

    st3a.close()
    barrier()

    st3b = ExitStack()
    Eb = sb(st3b, "Eb", [128, NH * CS], BF16)
    g2m = sb(st3b, "g2m", [128, G * CS], BF16)
    t1b = [sb(st3b, f"t1b{i}", [128, 512], BF16) for i in range(2)]
    t1e = [sb(st3b, f"t1e{i}", [128, 512], BF16) for i in range(2)]
    t32b = sb(st3b, "t32b", [128, 512], F32)
    t1f = sb(st3b, "t1f", [128, 512], F32)
    cflb = sb(st3b, "cflb", [1, 4096], F32)

    peM, aeM, veM, lxM = PEs.n, AEs.n, VEs.n, LX.n

    @block.sync
    def _(sy):
        sy.wait_ge(CC.h, 2)
        ins = sy.dma_start(out=s_ag[:], in_=alg)
        LX.inc(ins, 16)

    @block.vector
    def _(v):
        v.wait_ge(LX.h, lxM + 16)
        ins = v.tensor_scalar(s_ag[:], s_ag[:], s_m2[:], s_m2c[:], ALU.mult, ALU.add)
        VEs.inc(ins)

    @block.tensor
    def _(t):
        t.wait_ge(VEs.h, veM + 1)
        ins = t.transpose(pF[:, 0:8], s_ag[:], eye[0:8, 0:8])
        PEs.inc(ins)

    @block.scalar
    def _(s):
        s.wait_ge(PEs.h, peM + 1)
        ins = s.copy(a_hm[:], pF[:, 0:8])
        AEs.inc(ins)

    @block.vector
    def _(v):
        v.wait_ge(AEs.h, aeM + 1)
        v.memset(whm[:, 7:8], 1.0)
        for gg in range(6, -1, -1):
            v.tensor_tensor(whm[:, gg:gg + 1], whm[:, gg + 1:gg + 2], a_hm[:, gg + 1:gg + 2], ALU.mult)
        ins = v.tensor_tensor(whm[:], whm[:], s_m2h[:], ALU.mult)
        VEs.inc(ins)
        ins = v.memset(incb[:], 0.0)
        VEs.inc(ins)

    peM2, aeM2, veM2, lxM2 = PEs.n, AEs.n, VEs.n, LX.n
    for gg in range(NCORE):
        @block.tensor
        def _(t, gg=gg):
            t.wait_ge(VEs.h, veM + 2)
            t.wait_ge(AEs.h, aeM2 + 2 * gg)
            ins = t.transpose(pG[0:1, 0:128], whm[:, gg:gg + 1], eye[:])
            PEs.inc(ins)

        @block.scalar
        def _(s, gg=gg):
            s.wait_ge(PEs.h, peM2 + 2 * gg + 1)
            ins = s.copy(lrow[:], pG[0:1, 0:128])
            AEs.inc(ins)

        @block.tensor
        def _(t, gg=gg):
            t.wait_ge(AEs.h, aeM2 + 2 * gg + 1)
            ins = t.matmul(pF[:, 0:128], ones1[:], lrow[:], start=True, stop=True)
            PEs.inc(ins)

        @block.scalar
        def _(s, gg=gg):
            s.wait_ge(PEs.h, peM2 + 2 * gg + 2)
            if gg >= 2:
                s.wait_ge(VEs.h, veM2 + 2 * (gg - 1))
            ins = s.copy(wexp[gg % 2][:], pF[:, 0:128])
            AEs.inc(ins)

        @block.sync
        def _(sy, gg=gg):
            if gg >= 2:
                sy.wait_ge(VEs.h, veM2 + 2 * (gg - 1))
            ins = sy.dma_start(out=(sreb if gg % 2 == 0 else big1)[:], in_=gath[128 * gg:128 * (gg + 1), :])
            LX.inc(ins, 16)

        @block.vector
        def _(v, gg=gg):
            v.wait_ge(AEs.h, aeM2 + 2 * gg + 2)
            v.wait_ge(LX.h, lxM2 + 16 * (gg + 1))
            src = (sreb if gg % 2 == 0 else big1)
            ins = v.tensor_tensor(Eb[:, 0:DIN].rearrange("p (h d) -> p h d", h=NH),
                                  src[:].rearrange("p (h d) -> p h d", h=NH),
                                  bv(wexp[gg % 2][:], 128, NH, HD), ALU.mult)
            VEs.inc(ins)
            ins = v.tensor_tensor(incb[:], incb[:], Eb[:, 0:DIN], ALU.add)
            VEs.inc(ins)

    barrier()

    lx4, pe4, ae4, ve4, sp4 = LX.n, PEs.n, AEs.n, VEs.n, SP.n

    dve_seq = []
    for c in range(NCH):
        for g in range(8):
            dve_seq.append(("g2m", c, g))
        for s_ in range(32):
            dve_seq.append(("min", c, s_))
            if s_ >= 1:
                dve_seq.append(("emul", c, s_ - 1))
        dve_seq.append(("emul", c, 31))
        if c >= 1:
            dve_seq.append(("incu", c))
        for hb in range(16):
            dve_seq.append(("mrg", c, hb))
    dve_at = {tg: ve4 + k + 1 for k, tg in enumerate(dve_seq)}

    pe_seq = []
    for c in range(NCH):
        for g in range(8):
            pe_seq.append(("g2", c, g))
        for s_ in range(32):
            pe_seq.append(("bc", c, s_))
        for hb in range(16):
            pe_seq.append(("yb", c, hb))
    pe_at = {tg: pe4 + k + 1 for k, tg in enumerate(pe_seq)}

    ae_at = {}
    k = 0
    for c in range(NCH):
        for s_ in range(32):
            k += 1
            ae_at[("exp", c, s_)] = ae4 + k

    lx_at = {}
    cnt = lx4
    for c in range(NCH):
        cnt += 16
        lx_at[("xs", c)] = cnt
        cnt += 16
        lx_at[("cf", c, 0)] = cnt
        if c >= 1:
            cnt += 16
            lx_at[("sre", c)] = cnt
        for q in range(1, 4):
            cnt += 16
            lx_at[("cf", c, q)] = cnt

    @block.sync
    def _(sy):
        sy.wait_ge(SP.h, sp4)
        for c in range(NCH):
            if c >= 1:
                sy.wait_ge(PEs.h, pe_at[("yb", c - 1, 15)])
            ins = sy.dma_start(out=xsb[:], in_=xs_d[c])
            LX.inc(ins, 16)
            if c >= 1:
                sy.wait_ge(PEs.h, pe_at[("bc", c - 1, 31)])
            ins = sy.dma_start(out=cflb[0:1, 0:4096], in_=cfl_d[c, 0:4096].rearrange("(a b) -> a b", a=1))
            LX.inc(ins, 16)
            if c >= 1:
                ins = sy.dma_start(out=sreb[:], in_=st_d[c - 1])
                LX.inc(ins, 16)
            for q in range(1, 4):
                sy.wait_ge(PEs.h, pe_at[("bc", c, 8 * q - 1)])
                ins = sy.dma_start(out=cflb[0:1, 0:4096],
                                   in_=cfl_d[c, 4096 * q:4096 * (q + 1)].rearrange("(a b) -> a b", a=1))
                LX.inc(ins, 16)

    @block.vector
    def _(v):
        for c in range(NCH):
            for g in range(8):
                v.wait_ge(PEs.h, pe_at[("g2", c, g)])
                ins = v.tensor_tensor(g2m[:, CS * g:CS * (g + 1)], (pF if g % 2 == 0 else pG)[:, 0:128],
                                      umask[:], ALU.mult)
                VEs.inc(ins)
            for s_ in range(32):
                v.wait_ge(PEs.h, pe_at[("bc", c, s_)])
                p = pF if s_ % 2 == 0 else pG
                v.tensor_tensor(t32b[:].rearrange("p (hh i) -> p hh i", hh=4),
                                p[:].rearrange("p (hh i) -> p hh i", hh=4),
                                bv(cumT[:, CS * c + 4 * s_:CS * c + 4 * (s_ + 1)], 128, 4, 128), ALU.subtract)
                if s_ >= 2:
                    v.wait_ge(AEs.h, ae_at[("exp", c, s_ - 2)])
                elif c >= 1:
                    v.wait_ge(AEs.h, ae_at[("exp", c - 1, 30 + s_)])
                ins = v.tensor_scalar(t1b[s_ % 2][:], t32b[:], 0.0, None, ALU.min)
                VEs.inc(ins)
                if s_ >= 1:
                    sj = s_ - 1
                    v.wait_ge(AEs.h, ae_at[("exp", c, sj)])
                    gg = sj // 4
                    ins = v.tensor_tensor(Eb[:, 512 * sj:512 * (sj + 1)].rearrange("p (hh i) -> p hh i", hh=4),
                                          t1e[sj % 2][:].rearrange("p (hh i) -> p hh i", hh=4),
                                          g2m[:, CS * gg:CS * (gg + 1)].unsqueeze(1).broadcast_to((128, 4, 128)),
                                          ALU.mult)
                    VEs.inc(ins)
            sj = 31
            v.wait_ge(AEs.h, ae_at[("exp", c, sj)])
            gg = sj // 4
            ins = v.tensor_tensor(Eb[:, 512 * sj:512 * (sj + 1)].rearrange("p (hh i) -> p hh i", hh=4),
                                  t1e[sj % 2][:].rearrange("p (hh i) -> p hh i", hh=4),
                                  g2m[:, CS * gg:CS * (gg + 1)].unsqueeze(1).broadcast_to((128, 4, 128)),
                                  ALU.mult)
            VEs.inc(ins)
            if c >= 1:
                v.wait_ge(LX.h, lx_at[("sre", c)])
                v.wait_ge(PEs.h, pe_at[("yb", c - 1, 15)])
                v.wait_ge(SP.h, sp4 + 16 * c)
                v.tensor_tensor(big1[:].rearrange("p (h d) -> p h d", h=NH),
                                incb[:].rearrange("p (h d) -> p h d", h=NH),
                                bv(al_bf[:, CS * (c - 1):CS * c], 128, NH, HD), ALU.mult)
                ins = v.tensor_tensor(incb[:], big1[:], sreb[:], ALU.add)
                VEs.inc(ins)
            for hb in range(16):
                v.wait_ge(PEs.h, pe_at[("yb", c, hb)])
                pyd = pA if hb % 2 == 0 else pC
                pyo = pB if hb % 2 == 0 else pD
                v.tensor_tensor(t1f[:].rearrange("p (hh d) -> p hh d", hh=8),
                                pyo[:].rearrange("p (hh d) -> p hh d", hh=8),
                                bv(sd_f[:, CS * c + 8 * hb:CS * c + 8 * (hb + 1)], 128, 8, HD), ALU.mult)
                ins = v.tensor_tensor(big1[:, 512 * hb:512 * (hb + 1)], t1f[:], pyd[:], ALU.add)
                VEs.inc(ins)
    assert VEs.n == ve4 + len(dve_seq), (VEs.n, ve4, len(dve_seq))

    @block.tensor
    def _(t):
        for c in range(NCH):
            for g in range(8):
                p = pF if g % 2 == 0 else pG
                if (c, g) not in ((0, 0), (0, 1)):
                    prev = ("g2m", c, g - 2) if g >= 2 else ("min", c - 1, 30 + g)
                    t.wait_ge(VEs.h, dve_at[prev])
                ins = t.matmul(p[:, 0:128], bcm[:, T * g + CS * c:T * g + CS * (c + 1)],
                               ccm[:, T * g + CS * c:T * g + CS * (c + 1)], start=True, stop=True)
                PEs.inc(ins)
            for s_ in range(32):
                p = pF if s_ % 2 == 0 else pG
                prev = ("min", c, s_ - 2) if s_ >= 2 else ("g2m", c, 6 + s_)
                t.wait_ge(VEs.h, dve_at[prev])
                t.wait_ge(LX.h, lx_at[("cf", c, s_ // 8)])
                off = 512 * (s_ % 8)
                ins = t.matmul(p[:, 0:512], ones1[:], cflb[0:1, off:off + 512], start=True, stop=True)
                PEs.inc(ins)
            for hb in range(16):
                pyd = pA if hb % 2 == 0 else pC
                pyo = pB if hb % 2 == 0 else pD
                t.wait_ge(VEs.h, dve_at[("emul", c, 2 * hb + 1)])
                if c >= 1 or hb >= 2:
                    prev = ("mrg", c, hb - 2) if hb >= 2 else ("mrg", c - 1, 14 + hb)
                    t.wait_ge(VEs.h, dve_at[prev])
                if hb == 0 and c >= 1:
                    t.wait_ge(VEs.h, dve_at[("incu", c)])
                t.wait_ge(LX.h, lx_at[("xs", c)])
                for hh in range(8):
                    h = 8 * hb + hh
                    t.matmul(pyd[:, 64 * hh:64 * (hh + 1)], Eb[:, CS * h:CS * (h + 1)],
                             xsb[:, HD * h:HD * (h + 1)], start=True, stop=True)
                    ins = t.matmul(pyo[:, 64 * hh:64 * (hh + 1)],
                                   ccm[:, T * (h // 16) + CS * c:T * (h // 16) + CS * (c + 1)],
                                   incb[:, HD * h:HD * (h + 1)], start=True, stop=True)
                PEs.inc(ins)
    assert PEs.n == pe4 + len(pe_seq)

    @block.scalar
    def _(s):
        for c in range(NCH):
            for sl in range(32):
                s.wait_ge(VEs.h, dve_at[("min", c, sl)])
                if sl >= 2:
                    s.wait_ge(VEs.h, dve_at[("emul", c, sl - 2)])
                elif c >= 1:
                    s.wait_ge(VEs.h, dve_at[("emul", c - 1, 30 + sl)])
                ins = s.activation(t1e[sl % 2][:], t1b[sl % 2][:], AF.Exp)
                AEs.inc(ins)
    assert AEs.n == ae4 + 32 * NCH

    @block.gpsimd
    def _(g):
        for c in range(NCH):
            g.wait_ge(VEs.h, dve_at[("mrg", c, 15)])
            ins = g.dma_start(out=yf_d[CS * c:CS * (c + 1), :], in_=big1[:])
            SP.inc(ins, 16)

    if DEBUG:
        @block.gpsimd
        def _(g):
            g.wait_ge(SP.h, SP.n)
            g.dma_start(out=dbg["dbg_xcm"], in_=x_cm).then_inc(DBG.h, 16)
            g.dma_start(out=dbg["dbg_bcm"], in_=b_cm).then_inc(DBG.h, 16)
            g.dma_start(out=dbg["dbg_yf"], in_=yf_d).then_inc(DBG.h, 16)
            g.dma_start(out=dbg["dbg_st"], in_=st_d.rearrange("c p d -> (c p) d")).then_inc(DBG.h, 16)
            g.dma_start(out=dbg["dbg_gath"], in_=gath).then_inc(DBG.h, 16)
            g.dma_start(out=dbg["dbg_xn"], in_=xn_d).then_inc(DBG.h, 16)
        DBG.n += 96

    st3b.close()
    st3.close()
    st23.close()
    barrier()

    st4 = ExitStack()
    yh = sb(st4, "yh", [128, 64 * 512], BF16)
    st4a = ExitStack()
    xn2 = sb(st4a, "xn2", [128, 32 * TH], BF16)
    wb4 = [sb(st4a, f"wb4{i}", [128, 4096], BF16) for i in range(3)]
    ytl = [sb(st4a, f"ytl{i}", [128, T], BF16) for i in range(3)]
    xre = [sb(st4a, f"xre{i}", [128, T], BF16) for i in range(3)]
    sgb = [sb(st4a, f"sgb{i}", [128, T], BF16) for i in range(2)]
    y2b = [sb(st4a, f"y2b{i}", [128, T], BF16) for i in range(2)]
    ysq = [sb(st4a, f"ysq{i}", [128, T], BF16) for i in range(2)]
    yob = [sb(st4a, f"yob{i}", [128, T], BF16) for i in range(2)]
    yGg = sb(st4a, "yGg", [128, 8 * T], BF16)
    rsb = sb(st4a, "rsb", [128, T], BF16)
    vr2 = sb(st4a, "vr2", [1, T], F32)
    s_dex = sb(st4a, "s_dex", [128, 64], BF16)
    s_nw = sb(st4a, "s_nw", [128, 64], F32)

    lx5, pe5, ae5, ve5, sp5, lw5 = LX.n, PEs.n, AEs.n, VEs.n, SP.n, LW.n

    pe_seq4 = []
    for gt in range(64):
        pe_seq4.append(("gmm", gt))
        pe_seq4.append(("vmm", gt))
        if gt % 8 == 7:
            pe_seq4.append(("rbc", gt // 8))
    pe_at4 = {tg: pe5 + k + 1 for k, tg in enumerate(pe_seq4)}

    ae_seq4 = []
    for gt in range(64):
        ae_seq4.append(("silu", gt))
        if gt % 8 == 7:
            gr = gt // 8
            ae_seq4.append(("vev", gr))
            ae_seq4.append(("rbce", gr))
            for q in range(8):
                ae_seq4.append(("nw", 8 * gr + q))
    ae_at4 = {tg: ae5 + k + 1 for k, tg in enumerate(ae_seq4)}

    ve_seq4 = []
    for gt in range(64):
        ve_seq4.append(("y2", gt))
        ve_seq4.append(("yG", gt))
        ve_seq4.append(("sq", gt))
        if gt % 8 == 7:
            for q in range(8):
                ve_seq4.append(("ynm", 8 * (gt // 8) + q))
    ve_at4 = {tg: ve5 + k + 1 for k, tg in enumerate(ve_seq4)}

    @block.sync
    def _(sy):
        sy.wait_ge(SP.h, sp5)
        ins = sy.dma_start(out=xn2[:], in_=xn_d)
        LX.inc(ins, 16)
        ins = sy.dma_start(out=s_nw[:], in_=nwi)
        LX.inc(ins, 16)
        for gt in range(64):
            if gt >= 3:
                sy.wait_ge(VEs.h, ve_at4[("y2", gt - 3)])
            ins = sy.dma_start(out=ytl[gt % 3][:], in_=yf_d[:, 128 * gt:128 * (gt + 1)], transpose=True)
            LX.inc(ins, 16)
            ins = sy.dma_start(out=xre[gt % 3][:], in_=x_cm[128 * gt:128 * (gt + 1), :])
            LX.inc(ins, 16)
        for k in range(64):
            sy.wait_ge(SP.h, sp5 + 16 * (k + 1))
            ins = sy.dma_start(out=yh[:, 512 * k:512 * (k + 1)], in_=yo_d[128 * k:128 * (k + 1), 0:512])
            LX.inc(ins, 16)

    @block.gpsimd
    def _(g):
        g.wait_ge(SP.h, sp5)
        ins = g.dma_start(out=s_dex[:], in_=dexpi)
        LW.inc(ins, 16)
        for gt in range(64):
            if gt >= 3:
                g.wait_ge(PEs.h, pe_at4[("gmm", gt - 3)])
            ins = g.dma_start(out=wb4[gt % 3][:], in_=wp[gt])
            LW.inc(ins, 16)

    @block.tensor
    def _(t):
        for gt in range(64):
            gr = gt // 8
            t.wait_ge(LW.h, lw5 + 16 * (gt + 2))
            t.wait_ge(LX.h, lx5 + 32)
            if gt >= 2:
                t.wait_ge(AEs.h, ae_at4[("silu", gt - 2)])
            a = pA if gt % 2 == 0 else pC
            b = pB if gt % 2 == 0 else pD
            w = wb4[gt % 3]
            for k in range(32):
                s0, s1 = (k == 0), (k == 31)
                t.matmul(a[:], w[:, 128 * k:128 * (k + 1)], xn2[:, TH * k + 3:TH * k + 515], start=s0, stop=s1)
                ins = t.matmul(b[:], w[:, 128 * k:128 * (k + 1)], xn2[:, TH * k + 515:TH * k + 1027], start=s0, stop=s1)
            PEs.inc(ins)
            t.wait_ge(VEs.h, ve_at4[("sq", gt)])
            if gt % 8 == 0 and gr >= 1:
                t.wait_ge(AEs.h, ae_at4[("rbce", gr - 1)])
            t.matmul(pF[0:1, :], onescol_bf[:], ysq[gt % 2][:, 0:512], start=(gt % 8 == 0), stop=(gt % 8 == 7))
            ins = t.matmul(pG[0:1, :], onescol_bf[:], ysq[gt % 2][:, 512:1024], start=(gt % 8 == 0), stop=(gt % 8 == 7))
            PEs.inc(ins)
            if gt % 8 == 7:
                t.wait_ge(AEs.h, ae_at4[("vev", gr)])
                t.matmul(pF[:], ones1[:], vr2[0:1, 0:512], start=True, stop=True)
                ins = t.matmul(pG[:], ones1[:], vr2[0:1, 512:1024], start=True, stop=True)
                PEs.inc(ins)

    @block.scalar
    def _(s):
        for gt in range(64):
            gr = gt // 8
            s.wait_ge(PEs.h, pe_at4[("gmm", gt)])
            if gt >= 2:
                s.wait_ge(VEs.h, ve_at4[("yG", gt - 2)])
            a = pA if gt % 2 == 0 else pC
            b = pB if gt % 2 == 0 else pD
            s.activation(sgb[gt % 2][:, 0:512], a[:], AF.Silu)
            ins = s.activation(sgb[gt % 2][:, 512:1024], b[:], AF.Silu)
            AEs.inc(ins)
            if gt % 8 == 7:
                s.wait_ge(PEs.h, pe_at4[("vmm", gt)])
                s.copy(vr2[0:1, 0:512], pF[0:1, :])
                s.copy(vr2[0:1, 512:1024], pG[0:1, :])
                s.activation(vr2[:], vr2[:], AF.Ln, scale=1.0 / 1024.0, bias=EPS)
                ins = s.activation(vr2[:], vr2[:], AF.Exp, scale=-0.5)
                AEs.inc(ins)
                s.wait_ge(PEs.h, pe_at4[("rbc", gr)])
                s.copy(rsb[:, 0:512], pF[:])
                ins = s.copy(rsb[:, 512:1024], pG[:])
                AEs.inc(ins)
                for q in range(8):
                    gq = 8 * gr + q
                    s.wait_ge(VEs.h, ve_at4[("ynm", gq)])
                    ins = s.copy(lrow[:], lrow[:])
                    AEs.inc(ins)
                    ins = s.dma_start(out=yo_d[128 * gq:128 * (gq + 1), :], in_=yob[gq % 2][:])
                    SP.inc(ins, 16)
                if DEBUG:
                    for q in range(8):
                        gq = 8 * gr + q
                        ins = s.dma_start(out=dbg["dbg_yg"][128 * gq:128 * (gq + 1), :], in_=yGg[:, T * q:T * (q + 1)])
                        DBG.inc(ins, 16)
                    ins = s.dma_start(out=dbg["dbg_rs"][128 * gr:128 * (gr + 1), :], in_=rsb[:])
                    DBG.inc(ins, 16)
    assert AEs.n == ae5 + len(ae_seq4)

    @block.vector
    def _(v):
        for gt in range(64):
            gr = gt // 8
            v.wait_ge(LX.h, lx5 + 32 + 32 * (gt + 1))
            v.wait_ge(LW.h, lw5 + 16)
            ins = v.scalar_tensor_tensor(y2b[gt % 2][:], xre[gt % 3][:], s_dex[:, gt:gt + 1],
                                         ytl[gt % 3][:], ALU.mult, ALU.add)
            VEs.inc(ins)
            v.wait_ge(AEs.h, ae_at4[("silu", gt)])
            if gt >= 16:
                v.wait_ge(AEs.h, ae_at4[("nw", gt - 16)])
            ins = v.tensor_tensor(yGg[:, T * (gt % 8):T * (gt % 8 + 1)], y2b[gt % 2][:], sgb[gt % 2][:], ALU.mult)
            VEs.inc(ins)
            if gt >= 2:
                v.wait_ge(PEs.h, pe_at4[("vmm", gt - 2)])
            ins = v.tensor_tensor(ysq[gt % 2][:], yGg[:, T * (gt % 8):T * (gt % 8 + 1)],
                                  yGg[:, T * (gt % 8):T * (gt % 8 + 1)], ALU.mult)
            VEs.inc(ins)
            if gt % 8 == 7:
                v.wait_ge(AEs.h, ae_at4[("rbce", gr)])
                for q in range(8):
                    gq = 8 * gr + q
                    if gq >= 2:
                        v.wait_ge(SP.h, sp5 + 16 * (gq - 1))
                    ins = v.scalar_tensor_tensor(yob[gq % 2][:], yGg[:, T * q:T * (q + 1)],
                                                 s_nw[:, gq:gq + 1], rsb[:], ALU.mult, ALU.mult)
                    VEs.inc(ins)
    assert VEs.n == ve5 + len(ve_seq4)

    st4a.close()
    barrier()

    st4b = ExitStack()
    xnh = sb(st4b, "xnh", [128, 32 * 512], BF16)
    wob = [sb(st4b, f"wob{i}", [128, 8192], BF16) for i in range(2)]
    tb = [sb(st4b, f"tb{i}", [128, 4096], BF16) for i in range(2)]
    scb = [sb(st4b, f"scb{i}", [128, 4096], BF16) for i in range(2)]
    fb = [sb(st4b, f"fb{i}", [128, 4096], BF16) for i in range(2)]
    xrb = [sb(st4b, f"xrb{i}", [128, 512], F32) for i in range(2)]
    oat = [sb(st4b, f"oat{i}", [128, 512], F32) for i in range(4)]

    lx6, pe6, ae6, ve6, lw6, sp6, lwb6 = LX.n, PEs.n, AEs.n, VEs.n, LW.n, SP.n, LWB.n

    lx_xr = {}
    cnt = lx6
    lxh = []
    lxff = []
    for th in range(2):
        cnt += (32 if th == 0 else 96) * 16
        lxh.append(cnt)
        for ot in range(32):
            cnt += 16
            lx_xr[(th, ot)] = cnt
    lxff = [lxh[0], lxh[1]]

    @block.sync
    def _(sy):
        sy.wait_ge(SP.h, sp6)
        for th in range(2):
            if th == 1:
                sy.wait_ge(PEs.h, pe6 + 32)
                for k in range(64):
                    ins = sy.dma_start(out=yh[:, 512 * k:512 * (k + 1)],
                                       in_=yo_d[128 * k:128 * (k + 1), 512:1024])
                    LX.inc(ins, 16)
            for k in range(32):
                ins = sy.dma_start(out=xnh[:, 512 * k:512 * (k + 1)],
                                   in_=xn_d[0:128, TH * k + 3 + 512 * th:TH * k + 3 + 512 * (th + 1)])
                LX.inc(ins, 16)
            for ot in range(32):
                i = 32 * th + ot
                if i >= 2:
                    sy.wait_ge(VEs.h, ve6 + 2 * (i - 1))
                ins = sy.dma_start(out=xrb[i % 2][:], in_=xt[128 * ot:128 * (ot + 1), 3 + 512 * th:3 + 512 * (th + 1)])
                LX.inc(ins, 16)

    @block.gpsimd
    def _(g):
        g.wait_ge(SP.h, sp6)
        for th in range(2):
            for ot in range(32):
                i = 32 * th + ot
                if i >= 2:
                    g.wait_ge(PEs.h, pe6 + i - 1)
                ins = g.dma_start(out=wob[i % 2][:], in_=wop[ot])
                LWB.inc(ins, 16)
                if i >= 2:
                    g.wait_ge(VEs.h, ve6 + 2 * (i - 2) + 1)
                ins = g.dma_start(out=tb[i % 2][:], in_=tern[ot])
                LW.inc(ins, 16)
                ins = g.dma_start(out=scb[i % 2][:], in_=scf[ot])
                LW.inc(ins, 16)

    @block.vector
    def _(v):
        for th in range(2):
            for ot in range(32):
                i = 32 * th + ot
                v.wait_ge(LW.h, lw6 + 32 * (i + 1))
                if i >= 2:
                    v.wait_ge(PEs.h, pe6 + i - 1)
                ins = v.tensor_tensor(fb[i % 2][:], tb[i % 2][:], scb[i % 2][:], ALU.mult)
                VEs.inc(ins)
                v.wait_ge(PEs.h, pe6 + i + 1)
                v.wait_ge(LX.h, lx_xr[(th, ot)])
                if i >= 4:
                    v.wait_ge(STO.h, 16 * (i - 3))
                p = pA if i % 2 == 0 else pB
                ins = v.tensor_tensor(oat[i % 4][:], p[:], xrb[i % 2][:], ALU.add)
                VEs.inc(ins)

    @block.tensor
    def _(t):
        for th in range(2):
            t.wait_ge(LX.h, lxh[th])
            for ot in range(32):
                i = 32 * th + ot
                t.wait_ge(LWB.h, lwb6 + 16 * (i + 1))
                t.wait_ge(VEs.h, ve6 + 2 * i + 1)
                if i >= 2:
                    t.wait_ge(VEs.h, ve6 + 2 * (i - 1))
                p = pA if i % 2 == 0 else pB
                w = wob[i % 2]
                f = fb[i % 2]
                for k in range(64):
                    t.matmul(p[:], w[:, 128 * k:128 * (k + 1)], yh[:, 512 * k:512 * (k + 1)],
                             start=(k == 0), stop=False)
                for k2 in range(32):
                    ins = t.matmul(p[:], f[:, 128 * k2:128 * (k2 + 1)], xnh[:, 512 * k2:512 * (k2 + 1)],
                                   start=False, stop=(k2 == 31))
                PEs.inc(ins)

    @block.scalar
    def _(s):
        for th in range(2):
            for ot in range(32):
                i = 32 * th + ot
                s.wait_ge(VEs.h, ve6 + 2 * (i + 1))
                ins = s.dma_start(out=outT[128 * ot:128 * (ot + 1), 512 * th:512 * (th + 1)], in_=oat[i % 4][:])
                STO.inc(ins, 16)
        s.wait_ge(STO.h, STO.n)

    if DEBUG:
        @block.gpsimd
        def _(g):
            g.wait_ge(STO.h, 1024)
            g.dma_start(out=dbg["dbg_yo"], in_=yo_d).then_inc(DBG.h, 16)
            g.wait_ge(DBG.h, DBG.n + 16)
        DBG.n += 16

    st4b.close()
    st4.close()
    return nc, st


def _prep(inputs):
    x = np.asarray(inputs["L0_attn_hidden_states"])[0]
    W_in = np.asarray(inputs["L0_attn_in_proj_w"])
    conv_w = np.asarray(inputs["L0_attn_conv_w"])
    conv_b = np.asarray(inputs["L0_attn_conv_b"])
    dt_bias = np.asarray(inputs["L0_attn_dt_bias"])
    A_log = np.asarray(inputs["L0_attn_A_log"])
    D = np.asarray(inputs["L0_attn_D"])
    norm_w = np.asarray(inputs["L0_attn_norm_w"])
    W_out = np.asarray(inputs["L0_attn_out_proj_w"])
    ternary = np.asarray(inputs["L0_ffn_ternary"])
    scales = np.asarray(inputs["L0_ffn_scales"])

    xT = np.ascontiguousarray(x.T.astype(np.float32))
    wp = np.ascontiguousarray(W_in.astype(np.float32).reshape(N_OT, 128, 32, 128).transpose(0, 3, 2, 1)).reshape(N_OT, 128, 4096)
    wopk = np.ascontiguousarray(W_out.astype(np.float32).reshape(32, 128, 64, 128).transpose(0, 3, 2, 1)).reshape(32, 128, 8192)
    ternk = np.ascontiguousarray(ternary.reshape(32, 128, 32, 128).transpose(0, 3, 2, 1)).reshape(32, 128, 4096)
    s2 = scales.astype(np.float32).reshape(4096, 32)
    s3 = np.ascontiguousarray(s2.reshape(32, 128, 32).transpose(0, 2, 1))
    scfk = np.ascontiguousarray(np.broadcast_to(s3[:, None, :, :], (32, 128, 32, 128)).reshape(32, 128, 4096))
    cwp = np.ascontiguousarray(conv_w.astype(np.float32).reshape(N_XBC, 128, 4).transpose(1, 0, 2)).reshape(128, N_XBC * 4)
    cbp = np.ascontiguousarray(conv_b.astype(np.float32).reshape(N_XBC, 128).transpose(1, 0))
    dtb = np.ascontiguousarray(dt_bias.astype(np.float32).reshape(128, 1))
    alg_ = np.ascontiguousarray(A_log.astype(np.float32).reshape(128, 1))
    dexp = np.empty((128, 64), np.float32)
    nwp_ = np.empty((128, 64), np.float32)
    ar = np.arange(128)
    for tt in range(64):
        ch = 128 * tt + ar
        dexp[:, tt] = D[ch // 64]
        nwp_[:, tt] = norm_w[ch]
    in_maps = []
    for g in range(NCORE):
        lo = g * T
        if g == 0:
            xs = np.concatenate([np.zeros((HID, 3), np.float32), xT[:, 0:T]], axis=1)
        else:
            xs = np.ascontiguousarray(xT[:, lo - 3:lo + T])
        m2v = (np.arange(8) < g).astype(np.float32).reshape(8, 1)
        d = dict(xt=xs, wp=wp, wop=wopk, tern=ternk, scf=scfk,
                 cwi=cwp, cbi=cbp, dtbi=dtb, alogi=alg_, dexpi=dexp, nwi=nwp_,
                 m2i=np.ascontiguousarray(m2v), m2ci=np.ascontiguousarray(1.0 - m2v),
                 m2hi=np.ascontiguousarray(np.broadcast_to(m2v.reshape(1, 8), (128, 8))))
        in_maps.append(d)
    return in_maps


def kernel(**inputs):
    in_maps = _prep(inputs)
    nc, st = build_nc()
    try:
        res = run_bass_kernel_spmd(nc, in_maps, core_ids=list(range(NCORE)))
    finally:
        st.close()
    outs = [np.asarray(res.results[g]["outT"]) for g in range(NCORE)]
    outT = np.concatenate(outs, axis=1)
    out = np.ascontiguousarray(outT.T).reshape(1, 8192, HID).astype(np.float32)
    if DEBUG:
        kernel._last = res
    return out
